# version 08
# baseline (speedup 1.0000x reference)
import os
import numpy as np
import ml_dtypes
import concourse.bass as bass
import concourse.mybir as mybir
from concourse.bass_utils import run_bass_kernel_spmd

F32 = mybir.dt.float32
BF16 = mybir.dt.bfloat16
AF = mybir.ActivationFunctionType
ALU = mybir.AluOpType
AX = mybir.AxisListType
BF = ml_dtypes.bfloat16

S = 2048
H = 1024
CC = 8
NH = 16
DH = 64
QCA = 512
NQA = 4
QCB = 256
NQB = 8
NE = 16
WINDOW = 256

ENGS = ("t", "v", "s", "g", "y")
ALL_STAGES = ("ln1", "attn_a", "ln2", "ffn1", "ln3", "attn_b", "ln4", "ffn2")


class Rec:

    def __init__(self, nc):
        self.nc = nc
        self.prog = {e: [] for e in ENGS}
        self.cnt = {}
        self.lastwait = {e: {} for e in ENGS}
        self.sems = {}
        self.owner = {}
        self.nbar = 0
        self.pw = [None] * 8
        self.pr = [[] for _ in range(8)]

    def sem(self, name, owner=None):
        self.cnt[name] = 0
        self.owner[name] = owner

    def tok(self, name):
        return (name, self.cnt[name])

    def op(self, eng, fn, waits=(), inc=None, n=1):
        lw = self.lastwait[eng]
        real = []
        for w in waits:
            if w is None:
                continue
            s_, v = w
            if v > 0 and lw.get(s_, 0) < v:
                real.append((s_, v))
                lw[s_] = v
        out_tok = None
        if inc is not None:
            self.cnt[inc] += n
            out_tok = (inc, self.cnt[inc])
        sems = self.sems

        def play(e, real=real, fn=fn, inc=inc, n=n):
            for s_, v in real:
                e.wait_ge(sems[s_], v)
            inst = fn(e)
            if inc is not None:
                if inst is None:
                    e.nop().then_inc(sems[inc], n)
                else:
                    inst.then_inc(sems[inc], n)
        self.prog[eng].append(play)
        return out_tok

    def dma(self, eng, out, in_, sem, waits=()):
        return self.op(eng, lambda e, o=out, i=in_: e.dma_start(out=o, in_=i),
                       waits=waits, inc=sem, n=16)

    def mmw(self, bank, fn, waits=(), start=True, stop=True, nowait=False, noinc=False):
        ws = list(waits)
        if start and not nowait:
            ws += self.pr[bank]
        do_inc = (stop and not noinc) or noinc == "force"
        tok = self.op("t", fn, ws, inc="mm" if do_inc else None)
        if start:
            self.pr[bank] = []
        if stop:
            self.pw[bank] = tok
        return tok

    def takepr(self, bank):
        r = self.pr[bank]
        self.pr[bank] = []
        return list(r)

    def twait(self, waits):
        self.op("t", lambda e: None, waits=waits)

    def prd(self, eng, bank, fn, waits=(), inc=None):
        inc = inc or {"v": "vec", "s": "act"}[eng]
        tok = self.op(eng, fn, list(waits) + [self.pw[bank]], inc=inc)
        self.pr[bank].append(tok)
        return tok

    def prd2(self, eng, banks, fn, waits=(), inc=None):
        inc = inc or {"v": "vec", "s": "act"}[eng]
        tok = self.op(eng, fn, list(waits) + [self.pw[b] for b in banks], inc=inc)
        for b in banks:
            self.pr[b].append(tok)
        return tok

    def barrier(self):
        self.nbar += 1
        target = 5 * self.nbar
        for eng in ENGS:
            own = [(n, self.cnt[n]) for n, o in self.owner.items()
                   if o == eng and self.cnt[n] > 0]
            self.op(eng, lambda e: None, waits=own, inc="ph")
        for eng in ENGS:
            self.op(eng, lambda e: None, waits=[("ph", target)])
        return ("ph", target)


def _blocked(wt):
    c, f = wt.shape
    return np.ascontiguousarray(
        wt.reshape(c // 128, 128, f // 128, 128).transpose(2, 1, 0, 3))


def host_prep(ins):
    d = {}
    pm = np.zeros(H, np.int64)
    pr = np.zeros(H, np.int64)
    for h in range(NH):
        for j in range(DH):
            if j < 32:
                pm[64 * h + j] = 64 * h + 2 * j
                pr[64 * h + j] = 64 * h + 2 * j + 1
            else:
                pm[64 * h + j] = 64 * h + 2 * (j - 32) + 1
                pr[64 * h + j] = 64 * h + 2 * (j - 32)
    wq, wk = np.asarray(ins["L0_attn_a_w_q"]), np.asarray(ins["L0_attn_a_w_k"])
    d["aqm"] = _blocked(np.ascontiguousarray(wq[pm].T))
    d["aqr"] = _blocked(np.ascontiguousarray(wq[pr].T))
    d["akm"] = _blocked(np.ascontiguousarray(wk[pm].T))
    d["akr"] = _blocked(np.ascontiguousarray(wk[pr].T))
    d["avT"] = _blocked(np.ascontiguousarray(np.asarray(ins["L0_attn_a_w_v"]).T))
    d["aoT"] = _blocked(np.ascontiguousarray(np.asarray(ins["L0_attn_a_w_o"]).T))
    d["bqT"] = _blocked(np.ascontiguousarray(np.asarray(ins["L0_attn_b_Wq"]).T))
    d["bkT"] = _blocked(np.ascontiguousarray(np.asarray(ins["L0_attn_b_Wk"]).T))
    d["bvT"] = _blocked(np.ascontiguousarray(np.asarray(ins["L0_attn_b_Wv"]).T))
    d["boT"] = _blocked(np.ascontiguousarray(np.asarray(ins["L0_attn_b_Wo"]).T))
    sgb = _blocked(np.ascontiguousarray(np.asarray(ins["L0_ffn_shared_gate_w"]).T))
    sub = _blocked(np.ascontiguousarray(np.asarray(ins["L0_ffn_shared_up_w"]).T))
    d["sgsu"] = np.ascontiguousarray(
        np.concatenate([sgb.reshape(16, 128, 1024), sub.reshape(16, 128, 1024)], axis=2))
    sdw = np.ascontiguousarray(np.asarray(ins["L0_ffn_shared_down_w"]).T)
    d["sd"] = np.ascontiguousarray(
        sdw.reshape(16, 128, 8, 128).transpose(2, 1, 0, 3).reshape(8, 128, 2048))
    gw = np.asarray(ins["L0_ffn_gate_w"])
    uw = np.asarray(ins["L0_ffn_up_w"])
    dw = np.asarray(ins["L0_ffn_down_w"])

    def gu_blk(w):
        return np.ascontiguousarray(
            w.reshape(NE, 8, 128, 2, 128).transpose(0, 2, 3, 1, 4).reshape(NE, 128, 2048))
    d["expgu"] = np.ascontiguousarray(
        np.concatenate([gu_blk(gw), gu_blk(uw)], axis=2))
    d["dwcc"] = np.ascontiguousarray(
        dw.reshape(NE, 2, 128, 8, 128).transpose(3, 2, 0, 1, 4).reshape(8, 128, 4096))
    d["rwT"] = np.ascontiguousarray(
        np.asarray(ins["L0_ffn_router_w"]).T.reshape(8, 128, 16).transpose(1, 0, 2).reshape(128, 128))
    d["cbias"] = np.ascontiguousarray(
        np.tile(np.asarray(ins["L0_ffn_correction_bias"]).astype(np.float32), (128, NE)))
    half = DH // 2
    inv = 10000.0 ** (-(np.arange(half, dtype=np.float64) * 2.0 / DH))
    t = np.arange(S, dtype=np.float64)
    ang = t[None, :] * inv[:, None]
    cos = np.cos(ang).astype(np.float32)
    sin = np.sin(ang).astype(np.float32)
    csA = np.empty((128, S), BF)
    csB = np.empty((128, S), BF)
    for b in range(4):
        csA[32 * b:32 * b + 32] = cos
        csB[32 * b:32 * b + 32] = (-sin) if (b % 2 == 0) else sin
    d["csA"], d["csB"] = csA, csB
    kk = np.arange(128)[:, None]
    qq = np.arange(QCA)[None, :]
    ma = np.zeros((128, 4 * QCA), BF)
    for r in range(4):
        ma[:, r * QCA:(r + 1) * QCA] = (qq >= kk + 128 * r).astype(BF)
    d["maska"] = ma
    qq2 = np.arange(QCB)[None, :]
    mb = np.zeros((128, 4 * QCB), BF)
    for i, r in enumerate((-2, -1, 0, 1)):
        off = qq2 - kk - 128 * r
        mb[:, i * QCB:(i + 1) * QCB] = ((off >= 0) & (off <= WINDOW)).astype(BF)
    d["maskb"] = mb
    d["identf"] = np.eye(128, dtype=np.float32)
    esel = np.zeros((NE, NE * 128), BF)
    for e in range(NE):
        esel[e, 128 * e:128 * e + 128] = 1.0
    d["esel"] = esel
    return d


def build_nc(stages):
    nc = bass.Bass("TRN2", target_bir_lowering=False)
    ap = {}

    def din(name, shape, dtype=F32):
        ap[name] = nc.dram_tensor(name, list(shape), dtype, kind="ExternalInput").ap()

    din("xin", (CC, 128, S))
    for n in ("aqm", "aqr", "akm", "akr", "avT", "aoT", "bqT", "bkT", "bvT", "boT"):
        din(n, (8, 128, 8, 128))
    din("sgsu", (16, 128, 2048))
    din("sd", (8, 128, 2048))
    din("expgu", (NE, 128, 4096))
    din("dwcc", (8, 128, 4096))
    din("rwT", (128, 128))
    din("cbias", (128, 256))
    din("csA", (128, S), BF16)
    din("csB", (128, S), BF16)
    din("maska", (128, 4 * QCA), BF16)
    din("maskb", (128, 4 * QCB), BF16)
    din("identf", (128, 128))
    din("esel", (NE, NE * 128), BF16)
    yout = nc.dram_tensor("y", [CC, 128, S], F32, kind="ExternalOutput").ap()

    sc = {}
    scw = {}
    need_attn = any(st.startswith("attn") for st in stages)
    if need_attn:
        for n in ("aqm", "aqr", "akm", "akr", "avT", "aoT", "bqT", "bkT", "bvT", "boT"):
            scw[n] = nc.dram_tensor(f"{n}_bf", [8, 128, 1024], BF16, kind="Internal").ap()
    need_ffn = any(st.startswith("ffn") for st in stages)
    if need_ffn:
        sc["sgsu"] = nc.dram_tensor("sgsu_bf", [16, 128, 2048], BF16, kind="Internal").ap()
        sc["sd"] = nc.dram_tensor("sd_bf", [8, 128, 2048], BF16, kind="Internal").ap()
        sc["expgu"] = nc.dram_tensor("expgu_bf", [NE, 128, 4096], BF16, kind="Internal").ap()
        sc["dwcc"] = nc.dram_tensor("dwcc_bf", [8, 128, 4096], BF16, kind="Internal").ap()

    sb = {}

    def sbt(name, shape, dtype, off=None):
        if off is None:
            sb[name] = nc.alloc_sbuf_tensor(name, list(shape), dtype)
        else:
            sb[name] = nc.alloc_sbuf_tensor_at(name, list(shape), dtype, offset=off)
        return sb[name]

    sbt("x", (128, CC * S), F32)
    sbt("xn", (128, CC * S), BF16)
    sbt("ones_bf", (128, 128), BF16)
    sbt("eps_sb", (128, 1), F32)
    sbt("onesf", (128, 128), F32)
    sbt("identf_sb", (128, 128), F32)
    arena = nc.sbuf_base
    top = nc.sbuf_top
    A = [arena]
    ph_id = [0]

    def ar(name, shape, dtype, align=64):
        nbytes = int(np.prod(shape[1:])) * (2 if dtype == BF16 else 4)
        off = (A[0] + align - 1) // align * align
        assert off + nbytes <= top, f"SBUF arena overflow at {name}: {off+nbytes-top} over"
        A[0] = off + nbytes
        return sbt(f"{name}_{ph_id[0]}", shape, dtype, off=off)

    def ar_at(name, shape, dtype, off):
        return sbt(f"{name}_{ph_id[0]}", shape, dtype, off=off)

    T = [top]

    ar_offs = {}

    def ar_top(name, shape, dtype, align=64):
        nbytes = int(np.prod(shape[1:])) * (2 if dtype == BF16 else 4)
        off = (T[0] - nbytes) // align * align
        assert off >= A[0], f"SBUF top-arena overflow at {name}"
        T[0] = off
        ar_offs[name] = off
        return sbt(f"{name}_{ph_id[0]}", shape, dtype, off=off)

    def ar_reset():
        A[0] = arena
        T[0] = top
        ph_id[0] += 1

    PPf = [nc.alloc_psum_tensor("PPa", [128, 1024], F32),
           nc.alloc_psum_tensor("PPb", [128, 1024], F32)]
    P = [PPf[0][:, 0:512], PPf[0][:, 512:1024],
         PPf[1][:, 0:512], PPf[1][:, 512:1024]]
    P += [nc.alloc_psum_tensor(f"P{i}", [128, 512], F32)[:] for i in range(4, 8)]

    k = Rec(nc)
    k.sem("mm", "t")
    k.sem("act", "s")
    k.sem("vec", "v")
    k.sem("ph", None)
    for s_ in ("dxin", "dcst", "dst", "fs0", "fs1", "fd0", "fd1", "fe0", "fe1",
               "fw0", "fw1"):
        k.sem(s_, "y")
    for s_ in ("gcst", "fcst", "awcast"):
        k.sem(s_, "g")
    for i in range(12):
        k.sem(f"aw{i}", "y")
    for i in (20, 21):
        k.sem(f"aw{i}", "y")

    ident_tok = [None]

    def emit_load():
        for ccx in range(CC):
            k.dma("y", sb["x"][:, ccx * S:(ccx + 1) * S], ap["xin"][ccx], "dxin")
        ident_tok[0] = k.dma("y", sb["identf_sb"][:], ap["identf"], "dcst")
        k.op("v", lambda e: e.memset(sb["ones_bf"][:], 1.0), inc="vec")
        k.op("v", lambda e: e.memset(sb["onesf"][:], 1.0), inc="vec")
        k.op("v", lambda e: e.memset(sb["eps_sb"][:], 1e-5), inc="vec")
        if need_attn:
            for n in scw:
                k.dma("g", scw[n], ap[n].rearrange("a p c j -> a p (c j)"), "awcast")
        if need_ffn:
            k.dma("g", sc["sgsu"], ap["sgsu"], "fcst")
            k.dma("g", sc["sd"], ap["sd"], "fcst")
            for e2 in range(4):
                k.dma("g", sc["expgu"][4 * e2:4 * e2 + 4], ap["expgu"][4 * e2:4 * e2 + 4], "fcst")
            k.dma("g", sc["dwcc"], ap["dwcc"], "fcst")

    def emit_ln():
        lnbf = ar_top("lnbf", (128, 2 * QCA), BF16)
        lnsq = ar_top("lnsq", (128, 2 * QCA), BF16)
        m1 = ar_top("m1", (128, QCA), F32)
        m2 = ar_top("m2", (128, QCA), F32)
        m3 = ar_top("m3", (128, QCA), F32)
        m4 = ar_top("m4", (128, QCA), F32)
        xin_tok = k.tok("dxin")
        prev_last = None
        sq_war = {0: None, 1: None}
        mm_war = {0: None, 1: None}
        for qc in range(NQA):
            q0 = qc * QCA
            for ccx in range(CC):
                slot = ccx % 2
                war = mm_war.get(slot)
                tb = k.op("s", lambda e, ccx=ccx, slot=slot, q0=q0:
                          e.copy(lnbf[:, slot * QCA:(slot + 1) * QCA],
                                 sb["x"][:, ccx * S + q0:ccx * S + q0 + QCA]),
                          waits=[xin_tok, war, sq_war[slot]], inc="act")
                tq = k.op("v", lambda e, slot=slot:
                          e.tensor_mul(lnsq[:, slot * QCA:(slot + 1) * QCA],
                                       lnbf[:, slot * QCA:(slot + 1) * QCA],
                                       lnbf[:, slot * QCA:(slot + 1) * QCA]),
                          waits=[tb, war], inc="vec")
                sq_war[slot] = tq
                k.mmw(0, lambda e, slot=slot, ccx=ccx:
                      e.matmul(P[0][:], sb["ones_bf"][:],
                               lnbf[:, slot * QCA:(slot + 1) * QCA],
                               start=(ccx == 0), stop=(ccx == CC - 1)),
                      waits=[tb], start=(ccx == 0), stop=(ccx == CC - 1))
                mm_war[slot] = k.mmw(1, lambda e, slot=slot, ccx=ccx:
                      e.matmul(P[1][:], sb["ones_bf"][:],
                               lnsq[:, slot * QCA:(slot + 1) * QCA],
                               start=(ccx == 0), stop=(ccx == CC - 1)),
                      waits=[tq], start=(ccx == 0), stop=(ccx == CC - 1),
                      noinc="force")
            t1 = k.prd("s", 0, lambda e: e.mul(m1[:], P[0][:], 1.0 / H), waits=[prev_last])
            t2 = k.prd("s", 1, lambda e: e.mul(m2[:], P[1][:], 1.0 / H), waits=[prev_last])
            t3 = k.op("v", lambda e: e.tensor_mul(m3[:], m1[:], m1[:]),
                      waits=[t1, prev_last], inc="vec")
            t4 = k.op("v", lambda e: e.tensor_sub(m2[:], m2[:], m3[:]), waits=[t2, t3], inc="vec")
            t5 = k.op("s", lambda e: e.activation(m3[:], m2[:], AF.Ln, bias=sb["eps_sb"][:, 0:1]),
                      waits=[t4], inc="act")
            t6 = k.op("s", lambda e: e.activation(m2[:], m3[:], AF.Exp, scale=-0.5),
                      waits=[t5], inc="act")
            t7 = k.op("v", lambda e: e.tensor_mul(m3[:], m1[:], m2[:]), waits=[t6], inc="vec")
            lastc = prev_last
            for ccx in range(CC):
                t8 = k.op("v", lambda e, ccx=ccx, q0=q0:
                          e.tensor_mul(m4[:], sb["x"][:, ccx * S + q0:ccx * S + q0 + QCA], m2[:]),
                          waits=[t6, lastc], inc="vec")
                lastc = k.op("v", lambda e, ccx=ccx, q0=q0:
                             e.tensor_sub(sb["xn"][:, ccx * S + q0:ccx * S + q0 + QCA],
                                          m4[:], m3[:]),
                             waits=[t7, t8], inc="vec")
            last = lastc
            prev_last = last
        return last

    def attn_alloc_preload(mode):
        a = mode == "a"
        pre = {"a": a}
        pre["csA"] = ar("csA", (128, S), BF16) if a else None
        pre["csB"] = ar("csB", (128, S), BF16) if a else None
        pre["mask"] = ar("maska", (128, 4 * QCA), BF16) if a else ar("maskb", (128, 4 * QCB), BF16)
        pre["Qh"] = ar("Qh", (128, 2 * S), BF16)
        pre["Kh"] = ar("Kh", (128, 2 * S), BF16)
        pre["VT"] = ar("VT", (128, 2 * 16 * 130), BF16)
        pre["zv"] = ar("zv", (128, 65), BF16)
        pre["aout"] = ar("aout", (128, CC * S), BF16)
        nsl = 6 if a else 4
        nqk = nsl - 2
        pre["nqk"] = nqk
        pre["wsl"] = [ar(f"w{i}", (128, 1024), BF16) for i in range(2 * nqk + 1)]
        pre["rp1"] = ar("rp1", (128, QCA), F32)
        pre["rp2"] = ar("rp2", (128, QCA), F32)
        pre["qsq"] = ar("qsq", (128, 2 * QCA), BF16)
        pre["ksq"] = ar("ksq", (128, 2 * QCA), BF16)
        pre["nacc"] = ar("nacc", (1, 32), F32)
        pre["nred"] = ar("nred", (1, 16), F32)
        pre["bias_sb"] = ar("bias_sb", (128, 2), F32)
        cst = []
        if a:
            cst.append(k.dma("y", pre["csA"][:], ap["csA"], "dcst"))
            cst.append(k.dma("y", pre["csB"][:], ap["csB"], "dcst"))
            cst.append(k.dma("y", pre["mask"][:], ap["maska"], "dcst"))
        else:
            cst.append(k.dma("y", pre["mask"][:], ap["maskb"], "dcst"))
        pre["cst_tok"] = cst[-1]
        mats = (["aqm", "aqr", "akm", "akr", "avT", "aoT"] if a
                else ["bqT", "bkT", "bvT", "boT"])
        pre["mats"] = mats
        nm = 4 if a else 2
        pre["nm"] = nm
        cast_tok = ("awcast", k.cnt["awcast"])
        pre["cast_tok"] = cast_tok
        wtok = {}
        wfree = {}
        pre["wtok"] = wtok
        pre["wfree"] = wfree

        def loadw(mi, c):
            slot = (c % 2) * nqk + mi if mi < nm else 2 * nqk
            t = k.dma("y", pre["wsl"][slot][:], scw[mats[mi]][c], f"aw{slot}",
                      waits=[wfree.get(slot), cast_tok])
            wtok[(mi, c)] = (t, slot)
        pre["loadw"] = loadw
        for mi in range(nm + 1):
            loadw(mi, 0)
        for mi in range(nm):
            loadw(mi, 1)
        return pre

    def emit_attn(pre):
        a = pre["a"]
        QC = QCA if a else QCB
        NQ = NQA if a else NQB
        LAG = 2
        T[0] = top
        ebuf = ar_top("ebuf", (128, 4 * QC), BF16)
        embuf = ar_top("embuf", (128, 4 * QC), BF16)
        nsb = ar_top("nsb", (128, QC), F32)
        dinv = ar_top("dinv", (65, 2 * QC), BF16)
        wo_sl = [ar_at("wo0", (128, 1024), BF16, ar_offs["dinv"]),
                 ar_at("wo1", (128, 1024), BF16, ar_offs["dinv"] + 2048)]
        pre["wo_sl"] = wo_sl
        csA, csB, mask = pre["csA"], pre["csB"], pre["mask"]
        QhF, KhF, VTF = pre["Qh"], pre["Kh"], pre["VT"]
        zv = pre["zv"]
        aout = pre["aout"]
        wsl, wo_sl = pre["wsl"], pre["wo_sl"]
        rp1, rp2 = pre["rp1"], pre["rp2"]
        qsq_a, ksq_a = pre["qsq"], pre["ksq"]
        nacc, nred = pre["nacc"], pre["nred"]
        bias_sb = pre["bias_sb"]
        cst_tok = pre["cst_tok"]
        mats, nm, nqk = pre["mats"], pre["nm"], pre["nqk"]
        nw = len(mats) - 1
        iv = nm
        wtok, wfree, loadw = pre["wtok"], pre["wfree"], pre["loadw"]
        last_xn = k.tok("vec")
        last_ln_act = k.tok("act")
        vz1 = k.op("v", lambda e: e.memset(
            VTF[:].rearrange("p (t c) -> p t c", c=130)[:, :, 64:65], 1.0), inc="vec")
        vz2 = k.op("v", lambda e: e.memset(
            VTF[:].rearrange("p (t c) -> p t c", c=130)[:, :, 129:130], 1.0), inc="vec")
        vz3 = k.op("v", lambda e: e.memset(zv[:, 0:64], 0.0), inc="vec")
        vz4 = k.op("v", lambda e: e.memset(zv[:, 64:65], 1.0), inc="vec")
        zv_tok = vz4

        chunk_info = {}
        prev_sq = {}
        prev_norm = {}
        rope_prev = [None]
        chunk_done_mm = {-1: None, -2: None}

        def bg_units(c):
            par = c % 2
            Qh = QhF[:, par * S:(par + 1) * S]
            Kh = KhF[:, par * S:(par + 1) * S]
            VT = VTF[:, par * 2080:(par + 1) * 2080]
            units = []
            info = {}
            chunk_info[c] = info
            old_tok = chunk_done_mm.get(c - 2)

            def u_proj(qc, pair):
                def f():
                    q0 = qc * QCA
                    mi0 = pair * (2 if a else 1)
                    wt0, sl0 = wtok[(mi0, c)]
                    k.twait(k.takepr(7) + [wt0, last_xn, last_ln_act, old_tok])
                    wb0 = wsl[sl0]
                    for ccx in range(CC):
                        tk = k.mmw(7, lambda e, ccx=ccx, q0=q0, wb0=wb0:
                                   e.matmul(P[7][:], wb0[:, ccx * 128:ccx * 128 + 128],
                                            sb["xn"][:, ccx * S + q0:ccx * S + q0 + QCA],
                                            start=(ccx == 0), stop=(ccx == CC - 1)),
                                   start=(ccx == 0), stop=(ccx == CC - 1), nowait=True)
                    wfree[sl0] = tk
                    dst = (qsq_a if pair == 0 else ksq_a)[:, (qc % 2) * QCA:(qc % 2 + 1) * QCA]
                    tsq = k.prd("s", 7, lambda e, dst=dst: e.square(dst, P[7][:]),
                                waits=[prev_norm.get((pair, qc % 2))])
                    prev_sq[(pair, qc % 2, c)] = tsq
                    dstQ = (QhF if pair == 0 else KhF)[:, (c % 2) * S:(c % 2) * S + S]
                    if a:
                        t1 = k.prd("v", 7, lambda e, q0=q0: e.tensor_mul(
                            rp1[:], P[7][:], csA[:, q0:q0 + QCA]),
                            waits=[cst_tok, rope_prev[0], tsq])
                        wt1, sl1 = wtok[(mi0 + 1, c)]
                        k.twait(k.takepr(7) + [wt1])
                        wb1 = wsl[sl1]
                        for ccx in range(CC):
                            tk = k.mmw(7, lambda e, ccx=ccx, q0=q0, wb1=wb1:
                                       e.matmul(P[7][:], wb1[:, ccx * 128:ccx * 128 + 128],
                                                sb["xn"][:, ccx * S + q0:ccx * S + q0 + QCA],
                                                start=(ccx == 0), stop=(ccx == CC - 1)),
                                       start=(ccx == 0), stop=(ccx == CC - 1), nowait=True)
                        wfree[sl1] = tk
                        t2 = k.prd("v", 7, lambda e, q0=q0: e.tensor_mul(
                            rp2[:], P[7][:], csB[:, q0:q0 + QCA]))
                        tq = k.op("v", lambda e, q0=q0, dstQ=dstQ: e.tensor_add(
                            dstQ[:, q0:q0 + QCA], rp1[:], rp2[:]), waits=[t1, t2], inc="vec")
                        rope_prev[0] = tq
                    else:
                        tq = k.prd("s", 7, lambda e, q0=q0, dstQ=dstQ: e.copy(
                            dstQ[:, q0:q0 + QCA], P[7][:]), waits=[tsq])
                    info[("qk", qc, pair)] = tq
                return f

            def u_norm(qc):
                def f():
                    npar = (c % 2) * 16
                    tr = None
                    for pair in range(2):
                        src_a = (qsq_a if pair == 0 else ksq_a)[:, (qc % 2) * QCA:(qc % 2 + 1) * QCA]
                        k.twait(k.takepr(7) + [prev_sq[(pair, qc % 2, c)]])
                        tn = None
                        for hh in range(2):
                            tn = k.mmw(7, lambda e, src_a=src_a, hh=hh:
                                       e.matmul(P[7][32 * hh:32 * hh + 1, :],
                                                sb["ones_bf"][64 * hh:64 * hh + 64, 0:1],
                                                src_a[64 * hh:64 * hh + 64, :],
                                                start=True, stop=True,
                                                tile_position=(64 * hh, 32 * hh)),
                                       nowait=True, noinc=(hh == 0))
                        k.pw[7] = tn
                        for hh in range(2):
                            col = npar + 8 * hh + 4 * pair + qc
                            tr = k.prd("v", 7, lambda e, col=col, hh=hh:
                                       e.tensor_reduce(nacc[0:1, col:col + 1],
                                                       P[7][32 * hh:32 * hh + 1, :],
                                                       AX.X, ALU.max))
                            prev_norm[(pair, qc % 2)] = tr
                    info[("nrm", qc)] = tr
                return f

            def u_bias():
                def f():
                    npar = (c % 2) * 16
                    bpar = (c % 2) * 8
                    rprev = info[("nrm", NQA - 1)]
                    for hh in range(2):
                        rr1 = k.op("v", lambda e, hh=hh: e.tensor_reduce(
                            nred[0:1, bpar + 3 * hh:bpar + 3 * hh + 1],
                            nacc[0:1, npar + 8 * hh:npar + 8 * hh + 4], AX.X, ALU.max),
                            waits=[rprev], inc="vec")
                        rr2 = k.op("v", lambda e, hh=hh: e.tensor_reduce(
                            nred[0:1, bpar + 3 * hh + 1:bpar + 3 * hh + 2],
                            nacc[0:1, npar + 8 * hh + 4:npar + 8 * hh + 8], AX.X, ALU.max),
                            waits=[rr1], inc="vec")
                        rprev = k.op("v", lambda e, hh=hh: e.tensor_add(
                            nred[0:1, bpar + 3 * hh + 2:bpar + 3 * hh + 3],
                            nred[0:1, bpar + 3 * hh:bpar + 3 * hh + 1],
                            nred[0:1, bpar + 3 * hh + 1:bpar + 3 * hh + 2]),
                            waits=[rr2], inc="vec")
                    rmx = k.op("v", lambda e: e.tensor_tensor(
                        nred[0:1, bpar + 6:bpar + 7], nred[0:1, bpar + 2:bpar + 3],
                        nred[0:1, bpar + 5:bpar + 6], ALU.max), waits=[rprev], inc="vec")
                    rng = k.op("v", lambda e: e.tensor_scalar_mul(
                        nred[0:1, bpar + 7:bpar + 8], nred[0:1, bpar + 6:bpar + 7],
                        -1.0 / 16.0), waits=[rmx], inc="vec")
                    k.twait(k.takepr(7) + [rng])
                    tb = k.mmw(7, lambda e: e.matmul(
                        P[7][:, 0:1], sb["onesf"][0:1, 0:128],
                        nred[0:1, bpar + 7:bpar + 8], start=True, stop=True),
                        nowait=True)
                    tbias = k.prd("s", 7, lambda e: e.copy(
                        bias_sb[:, c % 2:c % 2 + 1], P[7][:, 0:1]))
                    info["bias"] = tbias
                return f

            def u_vt(tt):
                def f():
                    wt_v, slot_v = wtok[(iv, c)]
                    bnk = 7
                    k.twait(k.takepr(bnk) + [wt_v, old_tok])
                    for ccx in range(CC):
                        k.mmw(bnk, lambda e, tt=tt, ccx=ccx, bnk=bnk:
                              e.matmul(P[bnk][:, 0:128],
                                       sb["xn"][:, ccx * S + tt * 128:ccx * S + tt * 128 + 128],
                                       wsl[slot_v][:, ccx * 128:ccx * 128 + 128],
                                       start=(ccx == 0), stop=(ccx == CC - 1)),
                              start=(ccx == 0), stop=(ccx == CC - 1), nowait=True)
                    dst = VT[:, tt * 130:tt * 130 + 130].rearrange(
                        "p (b c) -> p b c", c=65)[:, :, 0:64]
                    src = P[bnk][:, 0:128].rearrange("p (b c) -> p b c", c=64)
                    ev = k.prd("v", bnk, lambda e, dst=dst, src=src: e.tensor_copy(dst, src),
                               waits=[vz1 if tt == 0 else None, vz2 if tt == 0 else None])
                    info[("vt", tt)] = ev
                    if tt == 15:
                        wfree[slot_v] = ev
                        info["vt_done"] = ev
                        if c + 1 < CC:
                            loadw(iv, c + 1)
                return f

            for qc in range(NQA):
                units.append(u_proj(qc, 0))
                units.append(u_proj(qc, 1))
                units.append(u_norm(qc))
            units.append(u_bias())
            for tt in range(16):
                units.append(u_vt(tt))
            return units

        for f in bg_units(0):
            f()
        bgq = []

        dv_toks = [None, None]
        pending_div = [None]
        for c in range(CC):
            par = c % 2
            Qh = QhF[:, par * S:(par + 1) * S]
            Kh = KhF[:, par * S:(par + 1) * S]
            VT = VTF[:, par * 2080:(par + 1) * 2080]
            info = chunk_info[c]
            qk_done = info[("qk", NQA - 1, 1)]
            tbias = info["bias"]
            vt_done = info["vt_done"]
            bias_ap = bias_sb[:, c % 2:c % 2 + 1]
            if c + 1 < CC:
                if c + 2 < CC:
                    for mi in range(nm):
                        loadw(mi, c + 2)
                bgq = list(bg_units(c + 1))
            else:
                bgq = []

            def run_bg(n=1):
                for _ in range(n):
                    if bgq:
                        bgq.pop(0)()

            econs = {}
            emcons = {}
            for qc in range(NQ):
                q0 = qc * QC
                if a:
                    kts = list(range(16))
                    pv_lo, pv_hi = 0, 4 * qc + 4
                    str_lo = 4 * qc
                else:
                    kts = [t_ for t_ in range(2 * qc - 2, 2 * qc + 2) if t_ >= 0]
                    pv_lo, pv_hi = kts[0], kts[-1] + 1
                    str_lo = -10**9
                nkt = len(kts)
                state = {}
                pv_cnt = [0]
                d_cnt = [0]

                def emit_s_exp(j, qc=qc, q0=q0, kts=kts, str_lo=str_lo,
                               pv_lo=pv_lo, pv_hi=pv_hi, state=state,
                               Qh=Qh, Kh=Kh, qk_done=qk_done, tbias=tbias,
                               bias_ap=bias_ap):
                    kt = kts[j]
                    pb = j % 2
                    w = k.takepr(2 * pb) + k.takepr(2 * pb + 1)
                    if j == 0:
                        w += [qk_done, tbias]
                    k.twait(w)
                    for hh in range(2):
                        k.mmw(2 * pb + hh, lambda e, hh=hh, kt=kt, q0=q0, pb=pb, Qh=Qh, Kh=Kh:
                              e.matmul(P[2 * pb + hh][:, 0:QC],
                                       Kh[64 * hh:64 * hh + 64, kt * 128:kt * 128 + 128],
                                       Qh[64 * hh:64 * hh + 64, q0:q0 + QC],
                                       start=True, stop=True),
                              nowait=True, noinc=(hh == 0))
                    sl4 = (j % 2) * 2 * QC
                    eb2 = ebuf[:, sl4:sl4 + 2 * QC]
                    if QC == QCA:
                        src = PPf[pb][:, 0:1024]
                        dst = eb2
                    else:
                        src = PPf[pb][:].rearrange("p (b c) -> p b c", c=512)[:, :, 0:QC]
                        dst = eb2.rearrange("p (b c) -> p b c", c=QC)
                    texp = k.prd2("s", (2 * pb, 2 * pb + 1),
                                  lambda e, src=src, dst=dst, bias_ap=bias_ap:
                                  e.activation(dst, src, AF.Exp,
                                               bias=bias_ap[:, 0:1], scale=0.125),
                                  waits=[tbias, econs.get(j % 2)])
                    use_mask = ((a and str_lo <= kt < pv_hi) or (not a))
                    pv_use, tmask = eb2, texp
                    if use_mask and pv_lo <= kt < pv_hi:
                        mb2 = embuf[:, sl4:sl4 + 2 * QC]
                        r = (kt - 4 * qc) if a else (kt - 2 * qc + 2)
                        msl = mask[:, r * QC:r * QC + QC]
                        tmask = k.op("v", lambda e, mb2=mb2, eb2=eb2, msl=msl:
                                     e.tensor_tensor(
                                         mb2.rearrange("p (b c) -> p b c", c=QC),
                                         eb2.rearrange("p (b c) -> p b c", c=QC),
                                         msl.rearrange("p (b c) -> p b c", b=1).broadcast_to(
                                             (128, 2, QC)),
                                         ALU.mult),
                                     waits=[texp, cst_tok, emcons.get(j % 2)],
                                     inc="vec")
                        pv_use = mb2
                    state[j] = (texp, tmask, pv_use, eb2)

                def emit_dpv(j, qc=qc, kts=kts, pv_lo=pv_lo, pv_hi=pv_hi,
                             state=state, pv_cnt=pv_cnt, d_cnt=d_cnt,
                             VT=VT, vt_done=vt_done):
                    kt = kts[j]
                    nkt = len(kts)
                    npv = pv_hi - pv_lo
                    first_d = d_cnt[0] == 0
                    last_d = d_cnt[0] == nkt - 1
                    in_pv = pv_lo <= kt < pv_hi
                    first_pv = in_pv and pv_cnt[0] == 0
                    last_pv = in_pv and pv_cnt[0] == npv - 1
                    texp, tmask, pv_use, eb2 = state[j]
                    w = []
                    if first_d:
                        w += k.takepr(4) + k.takepr(6) + [vt_done, zv_tok]
                    if first_pv:
                        w += k.takepr(5)
                    w.append(texp)
                    if in_pv:
                        w.append(tmask)
                    k.twait(w)
                    ebA = eb2[:, 0:QC]
                    ebB = eb2[:, QC:2 * QC]
                    pvA = pv_use[:, 0:QC]
                    pvB = pv_use[:, QC:2 * QC]
                    dsrcA = ebA if a else pvA
                    dsrcB = ebB if a else pvB
                    if (not a) or (pv_lo <= kt < pv_hi and pv_use is eb2):
                        fa = k.mmw(4, lambda e, kt=kt, src=(pvA if not a else ebA),
                                   st=first_d, sp=last_d:
                                   e.matmul(P[4][0:65, 0:QC],
                                            VT[:, kt * 130:kt * 130 + 65],
                                            src, start=st, stop=sp),
                                   start=first_d, stop=last_d, nowait=True)
                    elif in_pv:
                        if last_d:
                            k.mmw(4, lambda e, kt=kt, pvA=pvA:
                                  e.matmul(P[4][0:64, 0:QC],
                                           VT[:, kt * 130:kt * 130 + 64],
                                           pvA, start=False, stop=False),
                                  start=False, stop=False, nowait=True, noinc=True)
                            fa = k.mmw(4, lambda e, dsrcA=dsrcA:
                                       e.matmul(P[4][0:65, 0:QC], zv[:, 0:65], dsrcA,
                                                start=False, stop=True),
                                       start=False, stop=True, nowait=True)
                        else:
                            k.mmw(4, lambda e, dsrcA=dsrcA, st=first_d:
                                  e.matmul(P[4][0:65, 0:QC], zv[:, 0:65], dsrcA,
                                           start=st, stop=False),
                                  start=first_d, stop=False, nowait=True, noinc=True)
                            fa = k.mmw(4, lambda e, kt=kt, pvA=pvA:
                                       e.matmul(P[4][0:64, 0:QC],
                                                VT[:, kt * 130:kt * 130 + 64],
                                                pvA, start=False, stop=False),
                                       start=False, stop=False, nowait=True, noinc="force")
                    else:
                        fa = k.mmw(4, lambda e, dsrcA=dsrcA, st=first_d, sp=last_d:
                                   e.matmul(P[4][0:65, 0:QC], zv[:, 0:65], dsrcA,
                                            start=st, stop=sp),
                                   start=first_d, stop=last_d, nowait=True)
                    fb = k.mmw(6, lambda e, dsrcB=dsrcB, st=first_d, sp=last_d:
                               e.matmul(P[6][0:1, 0:QC], sb["ones_bf"][:, 0:1], dsrcB,
                                        start=st, stop=sp),
                               start=first_d, stop=last_d, nowait=True)
                    d_cnt[0] += 1
                    if in_pv:
                        fpb = k.mmw(5, lambda e, kt=kt, pvB=pvB, st=first_pv, sp=last_pv:
                                    e.matmul(P[5][64:128, 0:QC],
                                             VT[:, kt * 130 + 65:kt * 130 + 129],
                                             pvB, start=st, stop=sp,
                                             tile_position=(0, 64)),
                                    start=first_pv, stop=last_pv, nowait=True)
                        pv_cnt[0] += 1
                        last_tok = fpb
                    else:
                        last_tok = fb
                    if pv_use is eb2:
                        econs[j % 2] = last_tok
                    else:
                        econs[j % 2] = last_tok
                        emcons[j % 2] = last_tok
                    del state[j]

                def make_divide(qc=qc, q0=q0, c=c):
                    def _recipa(e, qc=qc, c=c):
                        with nc.allow_low_precision(reason="bf16 inv-denominator, margin-checked"):
                            inst = e.reciprocal(dinv[64:65, 0:QC], P[4][64:65, 0:QC])
                            inst.annotate(f"recipA c{c} qc{qc}")
                            return inst

                    def _recipb(e):
                        with nc.allow_low_precision(reason="bf16 inv-denominator, margin-checked"):
                            return e.reciprocal(dinv[0:1, QC:2 * QC], P[6][0:1, 0:QC])
                    dra = k.prd("v", 4, _recipa)
                    drb = k.prd("v", 6, _recipb)
                    ne1 = k.prd("v", 4, lambda e: e.tensor_copy(
                        nsb[0:64, 0:QC], P[4][0:64, 0:QC]), waits=[dv_toks[0]])
                    ne2 = k.prd("v", 5, lambda e: e.tensor_copy(
                        nsb[64:128, 0:QC], P[5][64:128, 0:QC]), waits=[dv_toks[1]])

                    def emit_divide():
                        k.twait(k.takepr(4) + k.takepr(5) + [dra, drb, ne1, ne2])
                        k.mmw(4, lambda e: e.matmul(
                            P[4][:, 0:QC], sb["ones_bf"][64:65, 0:128],
                            dinv[64:65, 0:QC],
                            start=True, stop=True), nowait=True, noinc=True)
                        bcb = k.mmw(5, lambda e: e.matmul(
                            P[5][:, 0:QC], sb["ones_bf"][0:1, 0:128],
                            dinv[0:1, QC:2 * QC],
                            start=True, stop=True), nowait=True)
                        k.pw[4] = bcb
                        dv_toks[0] = k.prd("v", 4, lambda e: e.tensor_mul(
                            aout[0:64, c * S + q0:c * S + q0 + QC], nsb[0:64, 0:QC],
                            P[4][0:64, 0:QC]))
                        dv_toks[1] = k.prd("v", 5, lambda e: e.tensor_mul(
                            aout[64:128, c * S + q0:c * S + q0 + QC], nsb[64:128, 0:QC],
                            P[5][64:128, 0:QC]))
                    return emit_divide

                for j in range(nkt):
                    if j >= LAG:
                        emit_dpv(j - LAG)
                    run_bg(1)
                    emit_s_exp(j)
                    if j == 1 and pending_div[0] is not None:
                        pending_div[0]()
                        pending_div[0] = None
                for j in range(max(0, nkt - LAG), nkt):
                    emit_dpv(j)
                    run_bg(1)
                if pending_div[0] is not None:
                    pending_div[0]()
                    pending_div[0] = None
                pending_div[0] = make_divide()
            run_bg(len(bgq))
            chunk_done_mm[c] = ("mm", k.cnt["mm"])
        pending_div[0]()
        pending_div[0] = None
        last_aout = dv_toks[1]

        for fc in range(CC):
            slot = 20 + fc % 2
            t_wo = k.dma("y", wo_sl[fc % 2][:], scw[mats[nw]][fc], f"aw{slot}",
                         waits=[wfree.get(slot), pre["cast_tok"], last_aout])
            wfree[slot] = t_wo
            for qc in range(NQA):
                q0 = qc * QCA
                pp = 6 + (fc * NQA + qc) % 2
                k.twait(k.takepr(pp) + [t_wo, last_aout])
                for dc in range(CC):
                    k.mmw(pp, lambda e, dc=dc, q0=q0, fc=fc, pp=pp:
                          e.matmul(P[pp][:], wo_sl[fc % 2][:, dc * 128:dc * 128 + 128],
                                   aout[:, dc * S + q0:dc * S + q0 + QCA],
                                   start=(dc == 0), stop=(dc == CC - 1)),
                          start=(dc == 0), stop=(dc == CC - 1), nowait=True)
                tadd = k.prd("v", pp, lambda e, fc=fc, q0=q0, pp=pp: e.tensor_add(
                    sb["x"][:, fc * S + q0:fc * S + q0 + QCA],
                    sb["x"][:, fc * S + q0:fc * S + q0 + QCA], P[pp][:]))
            wfree[slot] = tadd
        return tadd

    def emit_ffn():
        FLV = int(os.environ.get("FFN_LV", "5"))
        esel_sb = ar("esel", (NE, NE * 128), BF16)
        cbias_sb = ar("cbias", (128, 256), F32)
        rw_sb = ar("rw", (128, 128), BF16)
        comb_fm = ar("comb_fm", (NE, S), BF16)
        hh = ar("hh", (128, 48 * QCA), BF16)
        hh_off = int(hh.handle.offset) if hasattr(hh, "handle") else None
        hh_base = A[0] - 48 * QCA * 2
        sco_fm = ar_at("sco_fm", (NE, S), F32, hh_base)
        scT = ar_at("scT", (128, 256), F32, hh_base + 8192)
        sbT = ar_at("sbT", (128, 256), F32, hh_base + 8192 + 1024)
        msk = ar_at("msk", (128, 256), F32, hh_base + 8192 + 2048)
        cnt = ar_at("cnt", (128, 256), F32, hh_base + 8192 + 3072)
        ctm = ar_at("ctm", (128, 256), F32, hh_base + 8192 + 4096)
        rb = hh_base + 8192 + 5120
        gs = ar_at("gs", (128, 64), F32, rb)
        gct = ar_at("gct", (128, 64), F32, rb + 256)
        gtm = ar_at("gtm", (128, 64), F32, rb + 512)
        gm = ar_at("gm", (128, 64), F32, rb + 768)
        s1 = ar_at("s1", (128, 64), F32, rb + 1024)
        s2 = ar_at("s2", (128, 64), F32, rb + 1280)
        s3 = ar_at("s3", (128, 64), F32, rb + 1536)
        wsum = ar_at("wsum", (128, 16), F32, rb + 1792)
        comb_tm = ar_at("comb_tm", (128, 256), F32, rb + 1856)
        silu_t = [ar(f"silu_t{i}", (128, QCA), BF16) for i in range(2)]
        t2b = [ar(f"t2b{i}", (128, QCA), BF16) for i in range(2)]
        sgsu_sl = [ar(f"sgsu{i}", (128, 2048), BF16) for i in range(2)]
        sd_sl = [ar(f"sd{i}", (128, 2048), BF16) for i in range(2)]
        dw_sl = [ar(f"dw{i}", (128, 4096), BF16) for i in range(2)]
        T[0] = top
        eg_sl = [ar(f"eg{i}", (128, 4096), BF16) for i in range(1)]
        eg_sl.append(ar_top("eg1", (128, 4096), BF16))
        ln_guard = ("vec", k.cnt["vec"])

        k.dma("y", esel_sb[:], ap["esel"], "dcst")
        tcs = k.dma("y", cbias_sb[:], ap["cbias"], "dcst")
        trw = k.dma("g", rw_sb[:], ap["rwT"], "gcst")
        cast_done = k.tok("fcst")
        last_xn = k.tok("vec")
        last_ln_act = k.tok("act")

        for qc in range(NQA):
            q0 = qc * QCA
            for ccx in range(CC):
                k.mmw(6, lambda e, ccx=ccx, q0=q0:
                      e.matmul(P[6][0:16, :], rw_sb[:, ccx * 16:ccx * 16 + 16],
                               sb["xn"][:, ccx * S + q0:ccx * S + q0 + QCA],
                               start=(ccx == 0), stop=(ccx == CC - 1)),
                      waits=[trw, last_xn, last_ln_act],
                      start=(ccx == 0), stop=(ccx == CC - 1))
            tsig = k.prd("s", 6, lambda e, q0=q0: e.activation(
                sco_fm[0:16, q0:q0 + QCA], P[6][0:16, :], AF.Sigmoid))
        for tt in range(16):
            k.mmw(7, lambda e, tt=tt: e.transpose(
                P[7][:, tt * 16:tt * 16 + 16], sco_fm[0:16, tt * 128:tt * 128 + 128],
                sb["identf_sb"][0:16, 0:16]),
                waits=[tsig, ident_tok[0]], start=(tt == 0), stop=(tt == 15))
        v0 = k.prd("v", 7, lambda e: e.tensor_copy(scT[:], P[7][:, 0:256]))
        vt = k.op("v", lambda e: e.tensor_add(sbT[:], scT[:], cbias_sb[:]),
                  waits=[v0, tcs], inc="vec")

        def re(apx):
            return apx.rearrange("p (t e) -> p t e", e=16)
        sb4 = sbT[:].rearrange("p (t g j) -> p t g j", g=4, j=4)
        g4 = gs[:].rearrange("p (t g) -> p t g", g=4)
        s1r = s1[:].rearrange("p (t g) -> p t g", g=4)
        s2r = s2[:].rearrange("p (t g) -> p t g", g=4)
        s3r = s3[:].rearrange("p (t g) -> p t g", g=4)
        gmr = gm[:].rearrange("p (t g) -> p t g", g=4)
        gcr = gct[:].rearrange("p (t g) -> p t g", g=4)
        gtr = gtm[:].rearrange("p (t g) -> p t g", g=4)

        def vop(fn):
            nonlocal vt
            vt = k.op("v", fn, waits=[vt], inc="vec")
            return vt
        vop(lambda e: e.tensor_tensor(s1r, sb4[:, :, :, 0], sb4[:, :, :, 1], ALU.max))
        vop(lambda e: e.tensor_tensor(s2r, sb4[:, :, :, 0], sb4[:, :, :, 1], ALU.min))
        vop(lambda e: e.tensor_tensor(s3r, sb4[:, :, :, 2], sb4[:, :, :, 3], ALU.max))
        vop(lambda e: e.tensor_tensor(gmr, sb4[:, :, :, 2], sb4[:, :, :, 3], ALU.min))
        vop(lambda e: e.tensor_tensor(gcr, s1r, s3r, ALU.max))
        vop(lambda e: e.tensor_tensor(s1r, s1r, s3r, ALU.min))
        vop(lambda e: e.tensor_tensor(s2r, s2r, gmr, ALU.max))
        vop(lambda e: e.tensor_tensor(s1r, s1r, s2r, ALU.max))
        vop(lambda e: e.tensor_tensor(g4, gcr, s1r, ALU.add))
        vop(lambda e: e.memset(gct[:], 0.0))
        for dd in range(1, 4):
            Aap = g4[:, :, 0:4 - dd]
            Bap = g4[:, :, dd:4]
            cA = gcr[:, :, 0:4 - dd]
            cB = gcr[:, :, dd:4]
            tmp = gtr[:, :, 0:4 - dd]
            vop(lambda e, A=Aap, B=Bap, t=tmp: e.tensor_tensor(t, A, B, ALU.is_ge))
            vop(lambda e, t=tmp, c2=cB: e.tensor_tensor(c2, c2, t, ALU.add))
            vop(lambda e, A=Aap, B=Bap, t=tmp: e.tensor_tensor(t, B, A, ALU.is_gt))
            vop(lambda e, t=tmp, c2=cA: e.tensor_tensor(c2, c2, t, ALU.add))
        vop(lambda e: e.tensor_single_scalar(gm[:], gct[:], 2.0, ALU.is_lt))
        mr4 = msk[:].rearrange("p (t g j) -> p t g j", g=4, j=4)
        for j in range(4):
            vop(lambda e, j=j: e.scalar_tensor_tensor(
                mr4[:, :, :, j], sb4[:, :, :, j], 4.0, gmr, ALU.add, ALU.mult))
        vop(lambda e: e.memset(cnt[:], 0.0))
        mre = re(msk[:])
        cre = re(cnt[:])
        tre = re(ctm[:])
        for dd in range(1, 16):
            Aap = mre[:, :, 0:16 - dd]
            Bap = mre[:, :, dd:16]
            cA = cre[:, :, 0:16 - dd]
            cB = cre[:, :, dd:16]
            tmp = tre[:, :, 0:16 - dd]
            vop(lambda e, A=Aap, B=Bap, t=tmp: e.tensor_tensor(t, A, B, ALU.is_ge))
            vop(lambda e, t=tmp, c2=cB: e.tensor_tensor(c2, c2, t, ALU.add))
            vop(lambda e, A=Aap, B=Bap, t=tmp: e.tensor_tensor(t, B, A, ALU.is_gt))
            vop(lambda e, t=tmp, c2=cA: e.tensor_tensor(c2, c2, t, ALU.add))
        vop(lambda e: e.tensor_single_scalar(msk[:], cnt[:], 4.0, ALU.is_lt))
        vop(lambda e: e.tensor_mul(ctm[:], scT[:], msk[:]))
        vop(lambda e: e.tensor_reduce(wsum[:], re(ctm[:]), AX.X, ALU.add))
        vop(lambda e: e.tensor_scalar_add(wsum[:], wsum[:], 1e-20))
        vop(lambda e: e.reciprocal(wsum[:], wsum[:]))
        winb = wsum[:].rearrange("p (t e) -> p t e", e=1).broadcast_to((128, 16, 16))
        vop(lambda e: e.tensor_tensor(re(comb_tm[:]), re(ctm[:]), winb, ALU.mult))
        comb_vt = vt
        comb_state = {}

        def emit_comb_back():
            cevac = None
            for b4 in range(4):
                for j4 in range(4):
                    tt = 4 * b4 + j4
                    k.mmw(7, lambda e, tt=tt, j4=j4: e.transpose(
                        P[7][0:16, j4 * 128:j4 * 128 + 128],
                        comb_tm[:, tt * 16:tt * 16 + 16], sb["identf_sb"][:]),
                        waits=[comb_vt], start=(j4 == 0), stop=(j4 == 3))
                cevac = k.prd("s", 7, lambda e, b4=b4: e.copy(
                    comb_fm[0:16, b4 * 512:b4 * 512 + 512], P[7][0:16, 0:512]))
            comb_state["done"] = cevac

        emit_comb_back()

        sl_free = {0: None, 1: None}
        ed_free = {0: None, 1: None}
        dwf = {0: None, 1: None}
        sdf = {0: None, 1: None}
        st_free = {0: None, 1: None}
        t2_free = {0: None, 1: None}
        hh_tok = {}
        down_done = None
        res_tok = None
        for qc in range(NQA):
            q0 = qc * QCA
            if FLV < 2:
                continue
            for hc in range(16):
                sl = hc % 2
                tl = k.dma("y", sgsu_sl[sl][:], sc["sgsu"][hc], f"fs{sl}",
                           waits=[cast_done, sl_free[sl]])
                pp = hc % 2
                for ccx in range(CC):
                    k.mmw(pp, lambda e, ccx=ccx, q0=q0, sl=sl, pp=pp:
                          e.matmul(P[pp][:], sgsu_sl[sl][:, ccx * 128:ccx * 128 + 128],
                                   sb["xn"][:, ccx * S + q0:ccx * S + q0 + QCA],
                                   start=(ccx == 0), stop=(ccx == CC - 1)),
                          waits=[tl, last_xn, last_ln_act],
                          start=(ccx == 0), stop=(ccx == CC - 1))
                for ccx in range(CC):
                    tu = k.mmw(2 + pp, lambda e, ccx=ccx, q0=q0, sl=sl, pp=pp:
                               e.matmul(P[2 + pp][:],
                                        sgsu_sl[sl][:, 1024 + ccx * 128:1024 + ccx * 128 + 128],
                                        sb["xn"][:, ccx * S + q0:ccx * S + q0 + QCA],
                                        start=(ccx == 0), stop=(ccx == CC - 1)),
                               waits=[tl],
                               start=(ccx == 0), stop=(ccx == CC - 1))
                ts = k.prd("s", pp, lambda e, pp=pp: e.activation(
                    silu_t[pp][:], P[pp][:], AF.Silu), waits=[st_free[pp]])
                th = k.prd("v", 2 + pp, lambda e, hc=hc, pp=pp: e.tensor_mul(
                    hh[:, hc * QCA:hc * QCA + QCA], silu_t[pp][:], P[2 + pp][:]),
                    waits=[ts, down_done])
                st_free[pp] = th
                hh_tok[hc] = th
                sl_free[sl] = tu
            comb_done = comb_state["done"]
            for ee in range(NE):
                if FLV < 3:
                    break
                sl = ee % 2
                tl = k.dma("y", eg_sl[sl][:], sc["expgu"][ee], f"fe{sl}",
                           waits=[cast_done, ed_free[sl], ln_guard if sl == 1 else None])
                tcb = k.mmw(4, lambda e, ee=ee, q0=q0: e.matmul(
                    P[4][:], esel_sb[:, ee * 128:ee * 128 + 128],
                    comb_fm[0:16, q0:q0 + QCA], start=True, stop=True),
                    waits=[comb_done])
                for h2 in range(2):
                    pp = h2
                    for ccx in range(CC):
                        k.mmw(pp, lambda e, ccx=ccx, q0=q0, sl=sl, h2=h2, pp=pp:
                              e.matmul(P[pp][:],
                                       eg_sl[sl][:, (h2 * 8 + ccx) * 128:(h2 * 8 + ccx) * 128 + 128],
                                       sb["xn"][:, ccx * S + q0:ccx * S + q0 + QCA],
                                       start=(ccx == 0), stop=(ccx == CC - 1)),
                              waits=[tl], start=(ccx == 0), stop=(ccx == CC - 1))
                    for ccx in range(CC):
                        tu = k.mmw(2 + pp, lambda e, ccx=ccx, q0=q0, sl=sl, h2=h2, pp=pp:
                                   e.matmul(P[2 + pp][:],
                                            eg_sl[sl][:, 2048 + (h2 * 8 + ccx) * 128:2048 + (h2 * 8 + ccx) * 128 + 128],
                                            sb["xn"][:, ccx * S + q0:ccx * S + q0 + QCA],
                                            start=(ccx == 0), stop=(ccx == CC - 1)),
                                   waits=[tl], start=(ccx == 0), stop=(ccx == CC - 1))
                    ts = k.prd("s", pp, lambda e, pp=pp: e.activation(
                        silu_t[pp][:], P[pp][:], AF.Silu), waits=[st_free[pp]])
                    tm = k.prd("v", 2 + pp, lambda e, pp=pp: e.tensor_mul(
                        t2b[pp][:], silu_t[pp][:], P[2 + pp][:]),
                        waits=[ts, t2_free[pp]])
                    st_free[pp] = tm
                    th = k.prd("v", 4, lambda e, ee=ee, h2=h2, pp=pp: e.tensor_mul(
                        hh[:, (16 + 2 * ee + h2) * QCA:(16 + 2 * ee + h2 + 1) * QCA],
                        t2b[pp][:], P[4][:]), waits=[tm, down_done])
                    t2_free[pp] = th
                    hh_tok[16 + 2 * ee + h2] = th
                ed_free[sl] = tu
            for ccx in range(CC):
                if FLV < 4:
                    break
                sl = ccx % 2
                tsd = k.dma("y", sd_sl[sl][:], sc["sd"][ccx], f"fd{sl}",
                            waits=[cast_done, sdf[sl]])
                tdw = k.dma("y", dw_sl[sl][:], sc["dwcc"][ccx], f"fw{sl}",
                            waits=[cast_done, dwf[sl]])
                pp = 5 + ccx % 2
                shstop = FLV < 5
                for hc in range(16):
                    tk = k.mmw(pp, lambda e, hc=hc, sl=sl, pp=pp,
                               sp=(shstop and hc == 15):
                               e.matmul(P[pp][:], sd_sl[sl][:, hc * 128:hc * 128 + 128],
                                        hh[:, hc * QCA:hc * QCA + QCA],
                                        start=(hc == 0), stop=sp),
                               waits=[tsd, hh_tok[hc]],
                               start=(hc == 0), stop=(shstop and hc == 15))
                for ee in range(NE):
                    if FLV < 5:
                        break
                    for h2 in range(2):
                        idx = 2 * ee + h2
                        tk = k.mmw(pp, lambda e, idx=idx, sl=sl, pp=pp, sp=(idx == 31):
                                   e.matmul(P[pp][:],
                                            dw_sl[sl][:, idx * 128:idx * 128 + 128],
                                            hh[:, (16 + idx) * QCA:(16 + idx + 1) * QCA],
                                            start=False, stop=sp),
                                   waits=[tdw, hh_tok[16 + idx]],
                                   start=False, stop=(idx == 31))
                def _res(e, ccx=ccx, q0=q0, pp=pp, qc=qc):
                    inst = e.tensor_add(
                        sb["x"][:, ccx * S + q0:ccx * S + q0 + QCA],
                        sb["x"][:, ccx * S + q0:ccx * S + q0 + QCA], P[pp][:])
                    inst.annotate(f"res qc{qc} ccx{ccx} pp{pp}")
                    return inst
                res_tok = k.prd("v", pp, _res)
                sdf[sl] = tk
                dwf[sl] = tk
                down_done = res_tok
        return res_tok

    def emit_store():
        for ccx in range(CC):
            k.dma("y", yout[ccx], sb["x"][:, ccx * S:(ccx + 1) * S], "dst")
        k.op("y", lambda e: None, waits=[("dst", 16 * CC)])

    emit_load()
    k.barrier()
    i = 0
    while i < len(stages):
        st = stages[i]
        ar_reset()
        if st.startswith("ln"):
            nxt = stages[i + 1] if i + 1 < len(stages) else None
            pre = None
            if nxt in ("attn_a", "attn_b"):
                pre = attn_alloc_preload("a" if nxt == "attn_a" else "b")
            emit_ln()
            if nxt in ("attn_a", "attn_b"):
                emit_attn(pre)
                i += 1
            elif nxt is not None and nxt.startswith("ffn"):
                emit_ffn()
                i += 1
        k.barrier()
        i += 1
    emit_store()

    for name in list(k.cnt):
        k.sems[name] = nc.alloc_semaphore(name)

    with nc.Block() as block:
        @block.tensor
        def _(e):
            for f in k.prog["t"]:
                f(e)

        @block.vector
        def _(e):
            for f in k.prog["v"]:
                f(e)

        @block.scalar
        def _(e):
            for f in k.prog["s"]:
                f(e)

        @block.gpsimd
        def _(e):
            for f in k.prog["g"]:
                f(e)

        @block.sync
        def _(e):
            for f in k.prog["y"]:
                f(e)
    return nc


def kernel(**inputs):
    return _run(inputs, list(ALL_STAGES))


def _run(inputs, stages, cores=None):
    hp = host_prep(inputs)
    x = np.asarray(inputs["L0_attn_a_x"])
    B = x.shape[0]
    if cores is None:
        cores = list(range(B))
    nc = build_nc(tuple(stages))
    in_maps = []
    for b in cores:
        m = dict(hp)
        m["xin"] = np.ascontiguousarray(x[b].T.reshape(CC, 128, S))
        in_maps.append(m)
    res = run_bass_kernel_spmd(nc, in_maps, core_ids=list(range(len(cores))))
    outs = []
    for r in res.results:
        y = np.asarray(r["y"]).reshape(H, S)
        outs.append(np.ascontiguousarray(y.T))
    return np.stack(outs, axis=0)


# version 09
# speedup vs baseline: 1.1152x; 1.1152x over previous
import os
import numpy as np
import ml_dtypes
import concourse.bass as bass
import concourse.mybir as mybir
from concourse.bass_utils import run_bass_kernel_spmd

F32 = mybir.dt.float32
BF16 = mybir.dt.bfloat16
AF = mybir.ActivationFunctionType
ALU = mybir.AluOpType
AX = mybir.AxisListType
BF = ml_dtypes.bfloat16

S = 2048
H = 1024
CC = 8
NH = 16
DH = 64
QCA = 512
NQA = 4
QCB = 256
NQB = 8
NE = 16
WINDOW = 256

ENGS = ("t", "v", "s", "g", "y")
ALL_STAGES = ("ln1", "attn_a", "ln2", "ffn1", "ln3", "attn_b", "ln4", "ffn2")


class Rec:

    def __init__(self, nc):
        self.nc = nc
        self.prog = {e: [] for e in ENGS}
        self.cnt = {}
        self.lastwait = {e: {} for e in ENGS}
        self.sems = {}
        self.owner = {}
        self.nbar = 0
        self.pw = [None] * 8
        self.pr = [[] for _ in range(8)]

    def sem(self, name, owner=None):
        self.cnt[name] = 0
        self.owner[name] = owner

    def tok(self, name):
        return (name, self.cnt[name])

    def op(self, eng, fn, waits=(), inc=None, n=1):
        lw = self.lastwait[eng]
        real = []
        for w in waits:
            if w is None:
                continue
            s_, v = w
            if v > 0 and lw.get(s_, 0) < v:
                real.append((s_, v))
                lw[s_] = v
        out_tok = None
        if inc is not None:
            self.cnt[inc] += n
            out_tok = (inc, self.cnt[inc])
        sems = self.sems

        def play(e, real=real, fn=fn, inc=inc, n=n):
            for s_, v in real:
                e.wait_ge(sems[s_], v)
            inst = fn(e)
            if inc is not None:
                if inst is None:
                    e.nop().then_inc(sems[inc], n)
                else:
                    inst.then_inc(sems[inc], n)
        self.prog[eng].append(play)
        return out_tok

    def dma(self, eng, out, in_, sem, waits=()):
        return self.op(eng, lambda e, o=out, i=in_: e.dma_start(out=o, in_=i),
                       waits=waits, inc=sem, n=16)

    def mmw(self, bank, fn, waits=(), start=True, stop=True, nowait=False, noinc=False):
        ws = list(waits)
        if start and not nowait:
            ws += self.pr[bank]
        do_inc = (stop and not noinc) or noinc == "force"
        tok = self.op("t", fn, ws, inc="mm" if do_inc else None)
        if start:
            self.pr[bank] = []
        if stop:
            self.pw[bank] = tok
        return tok

    def takepr(self, bank):
        r = self.pr[bank]
        self.pr[bank] = []
        return list(r)

    def twait(self, waits):
        self.op("t", lambda e: None, waits=waits)

    def prd(self, eng, bank, fn, waits=(), inc=None):
        inc = inc or {"v": "vec", "s": "act"}[eng]
        tok = self.op(eng, fn, list(waits) + [self.pw[bank]], inc=inc)
        self.pr[bank].append(tok)
        return tok

    def prd2(self, eng, banks, fn, waits=(), inc=None):
        inc = inc or {"v": "vec", "s": "act"}[eng]
        tok = self.op(eng, fn, list(waits) + [self.pw[b] for b in banks], inc=inc)
        for b in banks:
            self.pr[b].append(tok)
        return tok

    def barrier(self):
        self.nbar += 1
        target = 5 * self.nbar
        for eng in ENGS:
            own = [(n, self.cnt[n]) for n, o in self.owner.items()
                   if o == eng and self.cnt[n] > 0]
            self.op(eng, lambda e: None, waits=own, inc="ph")
        for eng in ENGS:
            self.op(eng, lambda e: None, waits=[("ph", target)])
        return ("ph", target)


def _blocked(wt):
    c, f = wt.shape
    return np.ascontiguousarray(
        wt.reshape(c // 128, 128, f // 128, 128).transpose(2, 1, 0, 3))


def host_prep(ins):
    d = {}
    pm = np.zeros(H, np.int64)
    pr = np.zeros(H, np.int64)
    for h in range(NH):
        for j in range(DH):
            if j < 32:
                pm[64 * h + j] = 64 * h + 2 * j
                pr[64 * h + j] = 64 * h + 2 * j + 1
            else:
                pm[64 * h + j] = 64 * h + 2 * (j - 32) + 1
                pr[64 * h + j] = 64 * h + 2 * (j - 32)
    wq, wk = np.asarray(ins["L0_attn_a_w_q"]), np.asarray(ins["L0_attn_a_w_k"])
    d["aqm"] = _blocked(np.ascontiguousarray(wq[pm].T))
    d["aqr"] = _blocked(np.ascontiguousarray(wq[pr].T))
    d["akm"] = _blocked(np.ascontiguousarray(wk[pm].T))
    d["akr"] = _blocked(np.ascontiguousarray(wk[pr].T))
    d["avT"] = _blocked(np.ascontiguousarray(np.asarray(ins["L0_attn_a_w_v"]).T))
    d["aoT"] = _blocked(np.ascontiguousarray(np.asarray(ins["L0_attn_a_w_o"]).T))
    d["bqT"] = _blocked(np.ascontiguousarray(np.asarray(ins["L0_attn_b_Wq"]).T))
    d["bkT"] = _blocked(np.ascontiguousarray(np.asarray(ins["L0_attn_b_Wk"]).T))
    d["bvT"] = _blocked(np.ascontiguousarray(np.asarray(ins["L0_attn_b_Wv"]).T))
    d["boT"] = _blocked(np.ascontiguousarray(np.asarray(ins["L0_attn_b_Wo"]).T))
    sgb = _blocked(np.ascontiguousarray(np.asarray(ins["L0_ffn_shared_gate_w"]).T))
    sub = _blocked(np.ascontiguousarray(np.asarray(ins["L0_ffn_shared_up_w"]).T))
    d["sgsu"] = np.ascontiguousarray(
        np.concatenate([sgb.reshape(16, 128, 1024), sub.reshape(16, 128, 1024)], axis=2))
    sdw = np.ascontiguousarray(np.asarray(ins["L0_ffn_shared_down_w"]).T)
    d["sd"] = np.ascontiguousarray(
        sdw.reshape(16, 128, 8, 128).transpose(2, 1, 0, 3).reshape(8, 128, 2048))
    gw = np.asarray(ins["L0_ffn_gate_w"])
    uw = np.asarray(ins["L0_ffn_up_w"])
    dw = np.asarray(ins["L0_ffn_down_w"])

    def gu_blk(w):
        return np.ascontiguousarray(
            w.reshape(NE, 8, 128, 2, 128).transpose(0, 2, 3, 1, 4).reshape(NE, 128, 2048))
    d["expgu"] = np.ascontiguousarray(
        np.concatenate([gu_blk(gw), gu_blk(uw)], axis=2))
    d["dwcc"] = np.ascontiguousarray(
        dw.reshape(NE, 2, 128, 8, 128).transpose(3, 2, 0, 1, 4).reshape(8, 128, 4096))
    d["rwT"] = np.ascontiguousarray(
        np.asarray(ins["L0_ffn_router_w"]).T.reshape(8, 128, 16).transpose(1, 0, 2).reshape(128, 128))
    d["cbias"] = np.ascontiguousarray(
        np.tile(np.asarray(ins["L0_ffn_correction_bias"]).astype(np.float32), (128, NE)))
    half = DH // 2
    inv = 10000.0 ** (-(np.arange(half, dtype=np.float64) * 2.0 / DH))
    t = np.arange(S, dtype=np.float64)
    ang = t[None, :] * inv[:, None]
    cos = np.cos(ang).astype(np.float32)
    sin = np.sin(ang).astype(np.float32)
    csA = np.empty((128, S), BF)
    csB = np.empty((128, S), BF)
    for b in range(4):
        csA[32 * b:32 * b + 32] = cos
        csB[32 * b:32 * b + 32] = (-sin) if (b % 2 == 0) else sin
    d["csA"], d["csB"] = csA, csB
    kk = np.arange(128)[:, None]
    qq = np.arange(QCA)[None, :]
    ma = np.zeros((128, 4 * QCA), BF)
    for r in range(4):
        ma[:, r * QCA:(r + 1) * QCA] = (qq >= kk + 128 * r).astype(BF)
    d["maska"] = ma
    qq2 = np.arange(QCB)[None, :]
    mb = np.zeros((128, 4 * QCB), BF)
    for i, r in enumerate((-2, -1, 0, 1)):
        off = qq2 - kk - 128 * r
        mb[:, i * QCB:(i + 1) * QCB] = ((off >= 0) & (off <= WINDOW)).astype(BF)
    d["maskb"] = mb
    d["identf"] = np.eye(128, dtype=np.float32)
    selab = np.zeros((128, 256), BF)
    selab[64, 0:128] = 1.0
    selab[65, 128:256] = 1.0
    d["selab"] = selab
    esel = np.zeros((NE, NE * 128), BF)
    for e in range(NE):
        esel[e, 128 * e:128 * e + 128] = 1.0
    d["esel"] = esel
    return d


def build_nc(stages):
    nc = bass.Bass("TRN2", target_bir_lowering=False)
    ap = {}

    def din(name, shape, dtype=F32):
        ap[name] = nc.dram_tensor(name, list(shape), dtype, kind="ExternalInput").ap()

    din("xin", (CC, 128, S))
    for n in ("aqm", "aqr", "akm", "akr", "avT", "aoT", "bqT", "bkT", "bvT", "boT"):
        din(n, (8, 128, 8, 128))
    din("sgsu", (16, 128, 2048))
    din("sd", (8, 128, 2048))
    din("expgu", (NE, 128, 4096))
    din("dwcc", (8, 128, 4096))
    din("rwT", (128, 128))
    din("cbias", (128, 256))
    din("csA", (128, S), BF16)
    din("csB", (128, S), BF16)
    din("maska", (128, 4 * QCA), BF16)
    din("maskb", (128, 4 * QCB), BF16)
    din("identf", (128, 128))
    din("esel", (NE, NE * 128), BF16)
    din("selab", (128, 256), BF16)
    yout = nc.dram_tensor("y", [CC, 128, S], F32, kind="ExternalOutput").ap()

    sc = {}
    scw = {}
    need_attn = any(st.startswith("attn") for st in stages)
    if need_attn:
        for n in ("aqm", "aqr", "akm", "akr", "avT", "aoT", "bqT", "bkT", "bvT", "boT"):
            scw[n] = nc.dram_tensor(f"{n}_bf", [8, 128, 1024], BF16, kind="Internal").ap()
    need_ffn = any(st.startswith("ffn") for st in stages)
    if need_ffn:
        sc["sgsu"] = nc.dram_tensor("sgsu_bf", [16, 128, 2048], BF16, kind="Internal").ap()
        sc["sd"] = nc.dram_tensor("sd_bf", [8, 128, 2048], BF16, kind="Internal").ap()
        sc["expgu"] = nc.dram_tensor("expgu_bf", [NE, 128, 4096], BF16, kind="Internal").ap()
        sc["dwcc"] = nc.dram_tensor("dwcc_bf", [8, 128, 4096], BF16, kind="Internal").ap()

    sb = {}

    def sbt(name, shape, dtype, off=None):
        if off is None:
            sb[name] = nc.alloc_sbuf_tensor(name, list(shape), dtype)
        else:
            sb[name] = nc.alloc_sbuf_tensor_at(name, list(shape), dtype, offset=off)
        return sb[name]

    sbt("x", (128, CC * S), F32)
    sbt("xn", (128, CC * S), BF16)
    sbt("ones_bf", (128, 128), BF16)
    sbt("eps_sb", (128, 1), F32)
    sbt("onesf", (128, 128), F32)
    sbt("identf_sb", (128, 128), F32)
    arena = nc.sbuf_base
    top = nc.sbuf_top
    A = [arena]
    ph_id = [0]

    def ar(name, shape, dtype, align=64):
        nbytes = int(np.prod(shape[1:])) * (2 if dtype == BF16 else 4)
        off = (A[0] + align - 1) // align * align
        assert off + nbytes <= top, f"SBUF arena overflow at {name}: {off+nbytes-top} over"
        A[0] = off + nbytes
        return sbt(f"{name}_{ph_id[0]}", shape, dtype, off=off)

    def ar_at(name, shape, dtype, off):
        return sbt(f"{name}_{ph_id[0]}", shape, dtype, off=off)

    T = [top]

    ar_offs = {}

    def ar_top(name, shape, dtype, align=64):
        nbytes = int(np.prod(shape[1:])) * (2 if dtype == BF16 else 4)
        off = (T[0] - nbytes) // align * align
        assert off >= A[0], f"SBUF top-arena overflow at {name}"
        T[0] = off
        ar_offs[name] = off
        return sbt(f"{name}_{ph_id[0]}", shape, dtype, off=off)

    def ar_reset():
        A[0] = arena
        T[0] = top
        ph_id[0] += 1

    PPf = [nc.alloc_psum_tensor("PPa", [128, 1024], F32),
           nc.alloc_psum_tensor("PPb", [128, 1024], F32)]
    P = [PPf[0][:, 0:512], PPf[0][:, 512:1024],
         PPf[1][:, 0:512], PPf[1][:, 512:1024]]
    P += [nc.alloc_psum_tensor(f"P{i}", [128, 512], F32)[:] for i in range(4, 8)]

    k = Rec(nc)
    k.sem("mm", "t")
    k.sem("act", "s")
    k.sem("vec", "v")
    k.sem("ph", None)
    for s_ in ("dxin", "dcst", "dst", "fs0", "fs1", "fd0", "fd1", "fe0", "fe1",
               "fw0", "fw1"):
        k.sem(s_, "y")
    for s_ in ("gcst", "fcst", "awcast"):
        k.sem(s_, "g")
    for i in range(12):
        k.sem(f"aw{i}", "y")
    for i in (20, 21):
        k.sem(f"aw{i}", "y")

    ident_tok = [None]

    def emit_load():
        for ccx in range(CC):
            k.dma("y", sb["x"][:, ccx * S:(ccx + 1) * S], ap["xin"][ccx], "dxin")
        ident_tok[0] = k.dma("y", sb["identf_sb"][:], ap["identf"], "dcst")
        k.op("v", lambda e: e.memset(sb["ones_bf"][:], 1.0), inc="vec")
        k.op("v", lambda e: e.memset(sb["onesf"][:], 1.0), inc="vec")
        k.op("v", lambda e: e.memset(sb["eps_sb"][:], 1e-5), inc="vec")
        if need_attn:
            for n in scw:
                k.dma("g", scw[n], ap[n].rearrange("a p c j -> a p (c j)"), "awcast")
        if need_ffn:
            k.dma("g", sc["sgsu"], ap["sgsu"], "fcst")
            k.dma("g", sc["sd"], ap["sd"], "fcst")
            for e2 in range(4):
                k.dma("g", sc["expgu"][4 * e2:4 * e2 + 4], ap["expgu"][4 * e2:4 * e2 + 4], "fcst")
            k.dma("g", sc["dwcc"], ap["dwcc"], "fcst")

    def emit_ln():
        lnbf = ar_top("lnbf", (128, 2 * QCA), BF16)
        lnsq = ar_top("lnsq", (128, 2 * QCA), BF16)
        m1 = ar_top("m1", (128, QCA), F32)
        m2 = ar_top("m2", (128, QCA), F32)
        m3 = ar_top("m3", (128, QCA), F32)
        m4 = ar_top("m4", (128, QCA), F32)
        xin_tok = k.tok("dxin")
        prev_last = None
        sq_war = {0: None, 1: None}
        mm_war = {0: None, 1: None}
        for qc in range(NQA):
            q0 = qc * QCA
            for ccx in range(CC):
                slot = ccx % 2
                war = mm_war.get(slot)
                tb = k.op("s", lambda e, ccx=ccx, slot=slot, q0=q0:
                          e.copy(lnbf[:, slot * QCA:(slot + 1) * QCA],
                                 sb["x"][:, ccx * S + q0:ccx * S + q0 + QCA]),
                          waits=[xin_tok, war, sq_war[slot]], inc="act")
                tq = k.op("v", lambda e, slot=slot:
                          e.tensor_mul(lnsq[:, slot * QCA:(slot + 1) * QCA],
                                       lnbf[:, slot * QCA:(slot + 1) * QCA],
                                       lnbf[:, slot * QCA:(slot + 1) * QCA]),
                          waits=[tb, war], inc="vec")
                sq_war[slot] = tq
                k.mmw(0, lambda e, slot=slot, ccx=ccx:
                      e.matmul(P[0][:], sb["ones_bf"][:],
                               lnbf[:, slot * QCA:(slot + 1) * QCA],
                               start=(ccx == 0), stop=(ccx == CC - 1)),
                      waits=[tb], start=(ccx == 0), stop=(ccx == CC - 1))
                mm_war[slot] = k.mmw(1, lambda e, slot=slot, ccx=ccx:
                      e.matmul(P[1][:], sb["ones_bf"][:],
                               lnsq[:, slot * QCA:(slot + 1) * QCA],
                               start=(ccx == 0), stop=(ccx == CC - 1)),
                      waits=[tq], start=(ccx == 0), stop=(ccx == CC - 1),
                      noinc="force")
            t1 = k.prd("s", 0, lambda e: e.mul(m1[:], P[0][:], 1.0 / H), waits=[prev_last])
            t2 = k.prd("s", 1, lambda e: e.mul(m2[:], P[1][:], 1.0 / H), waits=[prev_last])
            t3 = k.op("v", lambda e: e.tensor_mul(m3[:], m1[:], m1[:]),
                      waits=[t1, prev_last], inc="vec")
            t4 = k.op("v", lambda e: e.tensor_sub(m2[:], m2[:], m3[:]), waits=[t2, t3], inc="vec")
            t5 = k.op("s", lambda e: e.activation(m3[:], m2[:], AF.Ln, bias=sb["eps_sb"][:, 0:1]),
                      waits=[t4], inc="act")
            t6 = k.op("s", lambda e: e.activation(m2[:], m3[:], AF.Exp, scale=-0.5),
                      waits=[t5], inc="act")
            t7 = k.op("v", lambda e: e.tensor_mul(m3[:], m1[:], m2[:]), waits=[t6], inc="vec")
            lastc = prev_last
            for ccx in range(CC):
                t8 = k.op("v", lambda e, ccx=ccx, q0=q0:
                          e.tensor_mul(m4[:], sb["x"][:, ccx * S + q0:ccx * S + q0 + QCA], m2[:]),
                          waits=[t6, lastc], inc="vec")
                lastc = k.op("v", lambda e, ccx=ccx, q0=q0:
                             e.tensor_sub(sb["xn"][:, ccx * S + q0:ccx * S + q0 + QCA],
                                          m4[:], m3[:]),
                             waits=[t7, t8], inc="vec")
            last = lastc
            prev_last = last
        return last

    def attn_alloc_preload(mode):
        a = mode == "a"
        pre = {"a": a}
        pre["csA"] = ar("csA", (128, S), BF16) if a else None
        pre["csB"] = ar("csB", (128, S), BF16) if a else None
        pre["mask"] = ar("maska", (128, 4 * QCA), BF16) if a else ar("maskb", (128, 4 * QCB), BF16)
        pre["Qh"] = ar("Qh", (128, 2 * S), BF16)
        pre["Kh"] = ar("Kh", (128, 2 * S), BF16)
        pre["VT"] = ar("VT", (128, 2 * 16 * 130), BF16)
        pre["zv"] = ar("zv", (128, 132), BF16)
        pre["selAB"] = ar("selAB", (128, 256), BF16)
        pre["aout"] = ar("aout", (128, CC * S), BF16)
        nsl = 6 if a else 4
        nqk = nsl - 2
        pre["nqk"] = nqk
        pre["wsl"] = [ar(f"w{i}", (128, 1024), BF16) for i in range(2 * nqk + 1)]
        pre["rp1"] = ar("rp1", (128, QCA), F32)
        pre["rp2"] = ar("rp2", (128, QCA), F32)
        pre["qsq"] = ar("qsq", (128, 2 * QCA), BF16)
        pre["ksq"] = ar("ksq", (128, 2 * QCA), BF16)
        pre["nacc"] = ar("nacc", (1, 32), F32)
        pre["nred"] = ar("nred", (1, 16), F32)
        pre["bias_sb"] = ar("bias_sb", (128, 2), F32)
        if a:
            k.dma("y", pre["csA"][:], ap["csA"], "dcst")
            k.dma("y", pre["csB"][:], ap["csB"], "dcst")
            k.dma("y", pre["mask"][:], ap["maska"], "dcst")
        else:
            k.dma("y", pre["mask"][:], ap["maskb"], "dcst")
        tok = k.dma("y", pre["selAB"][:], ap["selab"], "dcst")
        pre["cst_tok"] = tok
        pre["selAB_tok"] = tok
        mats = (["aqm", "aqr", "akm", "akr", "avT", "aoT"] if a
                else ["bqT", "bkT", "bvT", "boT"])
        pre["mats"] = mats
        nm = 4 if a else 2
        pre["nm"] = nm
        cast_tok = ("awcast", k.cnt["awcast"])
        pre["cast_tok"] = cast_tok
        wtok = {}
        wfree = {}
        pre["wtok"] = wtok
        pre["wfree"] = wfree

        def loadw(mi, c):
            slot = (c % 2) * nqk + mi if mi < nm else 2 * nqk
            t = k.dma("y", pre["wsl"][slot][:], scw[mats[mi]][c], f"aw{slot}",
                      waits=[wfree.get(slot), cast_tok])
            wtok[(mi, c)] = (t, slot)
        pre["loadw"] = loadw
        for mi in range(nm + 1):
            loadw(mi, 0)
        for mi in range(nm):
            loadw(mi, 1)
        return pre

    def emit_attn(pre):
        a = pre["a"]
        QC = QCA if a else QCB
        NQ = NQA if a else NQB
        LAG = 2
        T[0] = top
        ebuf = ar_top("ebuf", (128, 4 * QC), BF16)
        embuf = ar_top("embuf", (128, 4 * QC), BF16)
        nsb = ar_top("nsb", (128, QC), F32)
        dinv = ar_top("dinv", (66, QC), BF16)
        wo_sl = [ar_at("wo0", (128, 1024), BF16, ar_offs["dinv"]),
                 ar_at("wo1", (128, 1024), BF16, ar_offs["dinv"] + 2048)]
        pre["wo_sl"] = wo_sl
        selAB = pre["selAB"]
        selAB_tok = pre["selAB_tok"]
        csA, csB, mask = pre["csA"], pre["csB"], pre["mask"]
        QhF, KhF, VTF = pre["Qh"], pre["Kh"], pre["VT"]
        zv = pre["zv"]
        aout = pre["aout"]
        wsl, wo_sl = pre["wsl"], pre["wo_sl"]
        rp1, rp2 = pre["rp1"], pre["rp2"]
        qsq_a, ksq_a = pre["qsq"], pre["ksq"]
        nacc, nred = pre["nacc"], pre["nred"]
        bias_sb = pre["bias_sb"]
        cst_tok = pre["cst_tok"]
        mats, nm, nqk = pre["mats"], pre["nm"], pre["nqk"]
        nw = len(mats) - 1
        iv = nm
        wtok, wfree, loadw = pre["wtok"], pre["wfree"], pre["loadw"]
        last_xn = k.tok("vec")
        last_ln_act = k.tok("act")
        vz1 = k.op("v", lambda e: e.memset(
            VTF[:].rearrange("p (t c) -> p t c", c=130)[:, :, 64:65], 1.0), inc="vec")
        vz2 = k.op("v", lambda e: e.memset(
            VTF[:].rearrange("p (t c) -> p t c", c=130)[:, :, 129:130], 1.0), inc="vec")
        vz3 = k.op("v", lambda e: e.memset(zv[:, 0:64], 0.0), inc="vec")
        vz4 = k.op("v", lambda e: e.memset(zv[:, 64:65], 1.0), inc="vec")
        vz5 = k.op("v", lambda e: e.memset(zv[:, 65:131], 0.0), inc="vec")
        vz6 = k.op("v", lambda e: e.memset(zv[:, 131:132], 1.0), inc="vec")
        zv_tok = vz6

        chunk_info = {}
        prev_sq = {}
        prev_norm = {}
        rope_prev = [None]
        chunk_done_mm = {-1: None, -2: None}

        def bg_units(c):
            par = c % 2
            Qh = QhF[:, par * S:(par + 1) * S]
            Kh = KhF[:, par * S:(par + 1) * S]
            VT = VTF[:, par * 2080:(par + 1) * 2080]
            units = []
            info = {}
            chunk_info[c] = info
            old_tok = chunk_done_mm.get(c - 2)

            def u_proj(qc, pair):
                def f():
                    q0 = qc * QCA
                    mi0 = pair * (2 if a else 1)
                    wt0, sl0 = wtok[(mi0, c)]
                    w = k.takepr(6) + [wt0, last_xn, last_ln_act, old_tok]
                    if a:
                        w += k.takepr(7) + [wtok[(mi0 + 1, c)][0]]
                    k.twait(w)
                    wb0 = wsl[sl0]
                    for ccx in range(CC):
                        tk = k.mmw(6, lambda e, ccx=ccx, q0=q0, wb0=wb0:
                                   e.matmul(P[6][:], wb0[:, ccx * 128:ccx * 128 + 128],
                                            sb["xn"][:, ccx * S + q0:ccx * S + q0 + QCA],
                                            start=(ccx == 0), stop=(ccx == CC - 1)),
                                   start=(ccx == 0), stop=(ccx == CC - 1), nowait=True)
                    wfree[sl0] = tk
                    dstQ = (QhF if pair == 0 else KhF)[:, (c % 2) * S:(c % 2) * S + S]
                    if a:
                        wt1, sl1 = wtok[(mi0 + 1, c)]
                        wb1 = wsl[sl1]
                        for ccx in range(CC):
                            tk = k.mmw(7, lambda e, ccx=ccx, q0=q0, wb1=wb1:
                                       e.matmul(P[7][:], wb1[:, ccx * 128:ccx * 128 + 128],
                                                sb["xn"][:, ccx * S + q0:ccx * S + q0 + QCA],
                                                start=(ccx == 0), stop=(ccx == CC - 1)),
                                       start=(ccx == 0), stop=(ccx == CC - 1), nowait=True)
                        wfree[sl1] = tk
                        t1 = k.prd("v", 6, lambda e, q0=q0: e.tensor_mul(
                            rp1[:], P[6][:], csA[:, q0:q0 + QCA]),
                            waits=[cst_tok, rope_prev[0]])
                        t2 = k.prd("v", 7, lambda e, q0=q0: e.tensor_mul(
                            rp2[:], P[7][:], csB[:, q0:q0 + QCA]))
                        tq = k.op("v", lambda e, q0=q0, dstQ=dstQ: e.tensor_add(
                            dstQ[:, q0:q0 + QCA], rp1[:], rp2[:]), waits=[t1, t2], inc="vec")
                        rope_prev[0] = tq
                    else:
                        tq = k.prd("s", 6, lambda e, q0=q0, dstQ=dstQ: e.copy(
                            dstQ[:, q0:q0 + QCA], P[6][:]))
                    dst = (qsq_a if pair == 0 else ksq_a)[:, (qc % 2) * QCA:(qc % 2 + 1) * QCA]
                    tsq = k.op("s", lambda e, dst=dst, dstQ=dstQ, q0=q0:
                               e.square(dst, dstQ[:, q0:q0 + QCA]),
                               waits=[tq, prev_norm.get((pair, qc % 2))], inc="act")
                    prev_sq[(pair, qc % 2, c)] = tsq
                    info[("qk", qc, pair)] = tq
                return f

            def u_norm(qc):
                def f():
                    npar = (c % 2) * 16
                    tr = None
                    for pair in range(2):
                        src_a = (qsq_a if pair == 0 else ksq_a)[:, (qc % 2) * QCA:(qc % 2 + 1) * QCA]
                        bnk = 6 + pair
                        k.twait(k.takepr(bnk) + [prev_sq[(pair, qc % 2, c)]])
                        tn = None
                        for hh in range(2):
                            tn = k.mmw(bnk, lambda e, src_a=src_a, hh=hh, bnk=bnk:
                                       e.matmul(P[bnk][32 * hh:32 * hh + 1, :],
                                                sb["ones_bf"][64 * hh:64 * hh + 64, 0:1],
                                                src_a[64 * hh:64 * hh + 64, :],
                                                start=True, stop=True,
                                                tile_position=(64 * hh, 32 * hh)),
                                       nowait=True, noinc=(hh == 0))
                        k.pw[bnk] = tn
                        for hh in range(2):
                            col = npar + 8 * hh + 4 * pair + qc
                            tr = k.prd("v", bnk, lambda e, col=col, hh=hh, bnk=bnk:
                                       e.tensor_reduce(nacc[0:1, col:col + 1],
                                                       P[bnk][32 * hh:32 * hh + 1, :],
                                                       AX.X, ALU.max))
                            prev_norm[(pair, qc % 2)] = tr
                    info[("nrm", qc)] = tr
                return f

            def u_bias():
                def f():
                    npar = (c % 2) * 16
                    bpar = (c % 2) * 8
                    rprev = info[("nrm", NQA - 1)]
                    for hh in range(2):
                        rr1 = k.op("v", lambda e, hh=hh: e.tensor_reduce(
                            nred[0:1, bpar + 3 * hh:bpar + 3 * hh + 1],
                            nacc[0:1, npar + 8 * hh:npar + 8 * hh + 4], AX.X, ALU.max),
                            waits=[rprev], inc="vec")
                        rr2 = k.op("v", lambda e, hh=hh: e.tensor_reduce(
                            nred[0:1, bpar + 3 * hh + 1:bpar + 3 * hh + 2],
                            nacc[0:1, npar + 8 * hh + 4:npar + 8 * hh + 8], AX.X, ALU.max),
                            waits=[rr1], inc="vec")
                        rprev = k.op("v", lambda e, hh=hh: e.tensor_add(
                            nred[0:1, bpar + 3 * hh + 2:bpar + 3 * hh + 3],
                            nred[0:1, bpar + 3 * hh:bpar + 3 * hh + 1],
                            nred[0:1, bpar + 3 * hh + 1:bpar + 3 * hh + 2]),
                            waits=[rr2], inc="vec")
                    rmx = k.op("v", lambda e: e.tensor_tensor(
                        nred[0:1, bpar + 6:bpar + 7], nred[0:1, bpar + 2:bpar + 3],
                        nred[0:1, bpar + 5:bpar + 6], ALU.max), waits=[rprev], inc="vec")
                    rng = k.op("v", lambda e: e.tensor_scalar_mul(
                        nred[0:1, bpar + 7:bpar + 8], nred[0:1, bpar + 6:bpar + 7],
                        -1.0 / 16.0), waits=[rmx], inc="vec")
                    k.twait(k.takepr(7) + [rng])
                    tb = k.mmw(7, lambda e: e.matmul(
                        P[7][:, 0:1], sb["onesf"][0:1, 0:128],
                        nred[0:1, bpar + 7:bpar + 8], start=True, stop=True),
                        nowait=True)
                    tbias = k.prd("s", 7, lambda e: e.copy(
                        bias_sb[:, c % 2:c % 2 + 1], P[7][:, 0:1]))
                    info["bias"] = tbias
                return f

            def u_vt(tt):
                def f():
                    wt_v, slot_v = wtok[(iv, c)]
                    bnk = 6 + tt % 2
                    k.twait(k.takepr(bnk) + [wt_v, old_tok])
                    for ccx in range(CC):
                        k.mmw(bnk, lambda e, tt=tt, ccx=ccx, bnk=bnk:
                              e.matmul(P[bnk][:, 0:128],
                                       sb["xn"][:, ccx * S + tt * 128:ccx * S + tt * 128 + 128],
                                       wsl[slot_v][:, ccx * 128:ccx * 128 + 128],
                                       start=(ccx == 0), stop=(ccx == CC - 1)),
                              start=(ccx == 0), stop=(ccx == CC - 1), nowait=True)
                    dst = VT[:, tt * 130:tt * 130 + 130].rearrange(
                        "p (b c) -> p b c", c=65)[:, :, 0:64]
                    src = P[bnk][:, 0:128].rearrange("p (b c) -> p b c", c=64)
                    ev = k.prd("v", bnk, lambda e, dst=dst, src=src: e.tensor_copy(dst, src),
                               waits=[vz1 if tt == 0 else None, vz2 if tt == 0 else None])
                    info[("vt", tt)] = ev
                    if tt == 15:
                        wfree[slot_v] = ev
                        info["vt_done"] = ev
                        if c + 1 < CC:
                            loadw(iv, c + 1)
                return f

            for qc in range(NQA):
                units.append(u_proj(qc, 0))
                units.append(u_proj(qc, 1))
                units.append(u_norm(qc))
            units.append(u_bias())
            for tt in range(16):
                units.append(u_vt(tt))
            return units

        for f in bg_units(0):
            f()
        bgq = []

        dv_toks = [None, None]
        pending_div = [None]
        for c in range(CC):
            par = c % 2
            Qh = QhF[:, par * S:(par + 1) * S]
            Kh = KhF[:, par * S:(par + 1) * S]
            VT = VTF[:, par * 2080:(par + 1) * 2080]
            info = chunk_info[c]
            qk_done = info[("qk", NQA - 1, 1)]
            tbias = info["bias"]
            vt_done = info["vt_done"]
            bias_ap = bias_sb[:, c % 2:c % 2 + 1]
            if c + 1 < CC:
                if c + 2 < CC:
                    for mi in range(nm):
                        loadw(mi, c + 2)
                bgq = list(bg_units(c + 1))
            else:
                bgq = []

            def run_bg(n=1):
                for _ in range(n):
                    if bgq:
                        bgq.pop(0)()

            econs = {}
            emcons = {}
            for qc in range(NQ):
                q0 = qc * QC
                if a:
                    kts = list(range(16))
                    pv_lo, pv_hi = 0, 4 * qc + 4
                    str_lo = 4 * qc
                else:
                    kts = [t_ for t_ in range(2 * qc - 2, 2 * qc + 2) if t_ >= 0]
                    pv_lo, pv_hi = kts[0], kts[-1] + 1
                    str_lo = -10**9
                nkt = len(kts)
                state = {}
                pv_cnt = [0]
                d_cnt = [0]

                def emit_s_exp(j, qc=qc, q0=q0, kts=kts, str_lo=str_lo,
                               pv_lo=pv_lo, pv_hi=pv_hi, state=state,
                               Qh=Qh, Kh=Kh, qk_done=qk_done, tbias=tbias,
                               bias_ap=bias_ap):
                    kt = kts[j]
                    pb = j % 2
                    w = k.takepr(2 * pb) + k.takepr(2 * pb + 1)
                    if j == 0:
                        w += [qk_done, tbias]
                    k.twait(w)
                    for hh in range(2):
                        k.mmw(2 * pb + hh, lambda e, hh=hh, kt=kt, q0=q0, pb=pb, Qh=Qh, Kh=Kh:
                              e.matmul(P[2 * pb + hh][:, 0:QC],
                                       Kh[64 * hh:64 * hh + 64, kt * 128:kt * 128 + 128],
                                       Qh[64 * hh:64 * hh + 64, q0:q0 + QC],
                                       start=True, stop=True),
                              nowait=True, noinc=(hh == 0))
                    sl4 = (j % 2) * 2 * QC
                    eb2 = ebuf[:, sl4:sl4 + 2 * QC]
                    if QC == QCA:
                        src = PPf[pb][:, 0:1024]
                        dst = eb2
                    else:
                        src = PPf[pb][:].rearrange("p (b c) -> p b c", c=512)[:, :, 0:QC]
                        dst = eb2.rearrange("p (b c) -> p b c", c=QC)
                    texp = k.prd2("s", (2 * pb, 2 * pb + 1),
                                  lambda e, src=src, dst=dst, bias_ap=bias_ap:
                                  e.activation(dst, src, AF.Exp,
                                               bias=bias_ap[:, 0:1], scale=0.125),
                                  waits=[tbias, econs.get(j % 2)])
                    use_mask = ((a and str_lo <= kt < pv_hi) or (not a))
                    pv_use, tmask = eb2, texp
                    if use_mask and pv_lo <= kt < pv_hi:
                        mb2 = embuf[:, sl4:sl4 + 2 * QC]
                        r = (kt - 4 * qc) if a else (kt - 2 * qc + 2)
                        msl = mask[:, r * QC:r * QC + QC]
                        tmask = k.op("v", lambda e, mb2=mb2, eb2=eb2, msl=msl:
                                     e.tensor_tensor(
                                         mb2.rearrange("p (b c) -> p b c", c=QC),
                                         eb2.rearrange("p (b c) -> p b c", c=QC),
                                         msl.rearrange("p (b c) -> p b c", b=1).broadcast_to(
                                             (128, 2, QC)),
                                         ALU.mult),
                                     waits=[texp, cst_tok, emcons.get(j % 2)],
                                     inc="vec")
                        pv_use = mb2
                    state[j] = (texp, tmask, pv_use, eb2)

                def emit_dpv(j, qc=qc, kts=kts, pv_lo=pv_lo, pv_hi=pv_hi,
                             state=state, pv_cnt=pv_cnt, d_cnt=d_cnt,
                             VT=VT, vt_done=vt_done):
                    kt = kts[j]
                    nkt = len(kts)
                    npv = pv_hi - pv_lo
                    first_d = d_cnt[0] == 0
                    last_d = d_cnt[0] == nkt - 1
                    in_pv = pv_lo <= kt < pv_hi
                    first_pv = in_pv and pv_cnt[0] == 0
                    last_pv = in_pv and pv_cnt[0] == npv - 1
                    texp, tmask, pv_use, eb2 = state[j]
                    w = []
                    if first_d:
                        w += k.takepr(4) + [vt_done, zv_tok]
                    if first_pv:
                        w += k.takepr(5)
                    w.append(texp)
                    if in_pv:
                        w.append(tmask)
                    k.twait(w)
                    ebA = eb2[:, 0:QC]
                    ebB = eb2[:, QC:2 * QC]
                    pvA = pv_use[:, 0:QC]
                    pvB = pv_use[:, QC:2 * QC]
                    dsrcA = ebA if a else pvA
                    dsrcB = ebB if a else pvB
                    if not last_d:
                        k.mmw(4, lambda e, dsrcB=dsrcB, st=first_d:
                              e.matmul(P[4][0:66, 0:QC], zv[:, 66:132], dsrcB,
                                       start=st, stop=False),
                              start=first_d, stop=False, nowait=True, noinc=True)
                    if (not a) or (in_pv and pv_use is eb2):
                        fa = k.mmw(4, lambda e, kt=kt, src=(pvA if not a else ebA):
                                   e.matmul(P[4][0:65, 0:QC],
                                            VT[:, kt * 130:kt * 130 + 65],
                                            src, start=False, stop=False),
                                   start=False, stop=False, nowait=True, noinc="force")
                    elif in_pv:
                        k.mmw(4, lambda e, dsrcA=dsrcA:
                              e.matmul(P[4][0:65, 0:QC], zv[:, 0:65], dsrcA,
                                       start=False, stop=False),
                              start=False, stop=False, nowait=True, noinc=True)
                        fa = k.mmw(4, lambda e, kt=kt, pvA=pvA:
                                   e.matmul(P[4][0:64, 0:QC],
                                            VT[:, kt * 130:kt * 130 + 64],
                                            pvA, start=False, stop=False),
                                   start=False, stop=False, nowait=True, noinc="force")
                    else:
                        fa = k.mmw(4, lambda e, dsrcA=dsrcA:
                                   e.matmul(P[4][0:65, 0:QC], zv[:, 0:65], dsrcA,
                                            start=False, stop=False),
                                   start=False, stop=False, nowait=True, noinc="force")
                    if last_d:
                        fa = k.mmw(4, lambda e, dsrcB=dsrcB:
                                   e.matmul(P[4][0:66, 0:QC], zv[:, 66:132], dsrcB,
                                            start=False, stop=True),
                                   start=False, stop=True, nowait=True)
                    d_cnt[0] += 1
                    if in_pv:
                        fpb = k.mmw(5, lambda e, kt=kt, pvB=pvB, st=first_pv, sp=last_pv:
                                    e.matmul(P[5][64:128, 0:QC],
                                             VT[:, kt * 130 + 65:kt * 130 + 129],
                                             pvB, start=st, stop=sp,
                                             tile_position=(0, 64)),
                                    start=first_pv, stop=last_pv, nowait=True)
                        pv_cnt[0] += 1
                        last_tok = fpb
                    else:
                        last_tok = fa
                    if last_d:
                        k.pw[4] = last_tok
                    econs[j % 2] = last_tok
                    if pv_use is not eb2:
                        emcons[j % 2] = last_tok
                    del state[j]

                def make_divide(qc=qc, q0=q0, c=c):
                    def _recipa(e):
                        with nc.allow_low_precision(reason="bf16 inv-denominator, margin-checked"):
                            return e.reciprocal(dinv[64:66, 0:QC], P[4][64:66, 0:QC])
                    dra = k.prd("v", 4, _recipa)
                    drb = dra
                    ne1 = k.prd("v", 4, lambda e: e.tensor_copy(
                        nsb[0:64, 0:QC], P[4][0:64, 0:QC]), waits=[dv_toks[0]])
                    ne2 = k.prd("v", 5, lambda e: e.tensor_copy(
                        nsb[64:128, 0:QC], P[5][64:128, 0:QC]), waits=[dv_toks[1]])

                    def emit_divide():
                        k.twait(k.takepr(4) + k.takepr(5) + [dra, ne1, ne2, selAB_tok])
                        k.mmw(4, lambda e: e.matmul(
                            P[4][:, 0:QC], selAB[64:66, 0:128],
                            dinv[64:66, 0:QC],
                            start=True, stop=True), nowait=True, noinc=True)
                        bcb = k.mmw(5, lambda e: e.matmul(
                            P[5][:, 0:QC], selAB[64:66, 128:256],
                            dinv[64:66, 0:QC],
                            start=True, stop=True), nowait=True)
                        k.pw[4] = bcb
                        dv_toks[0] = k.prd("v", 4, lambda e: e.tensor_mul(
                            aout[0:64, c * S + q0:c * S + q0 + QC], nsb[0:64, 0:QC],
                            P[4][0:64, 0:QC]))
                        dv_toks[1] = k.prd("v", 5, lambda e: e.tensor_mul(
                            aout[64:128, c * S + q0:c * S + q0 + QC], nsb[64:128, 0:QC],
                            P[5][64:128, 0:QC]))
                    return emit_divide

                for j in range(nkt):
                    if j >= LAG:
                        emit_dpv(j - LAG)
                    run_bg(1)
                    emit_s_exp(j)
                    if j == 1 and pending_div[0] is not None:
                        pending_div[0]()
                        pending_div[0] = None
                for j in range(max(0, nkt - LAG), nkt):
                    emit_dpv(j)
                    run_bg(1)
                if pending_div[0] is not None:
                    pending_div[0]()
                    pending_div[0] = None
                pending_div[0] = make_divide()
            run_bg(len(bgq))
            chunk_done_mm[c] = ("mm", k.cnt["mm"])
        pending_div[0]()
        pending_div[0] = None
        last_aout = dv_toks[1]

        for fc in range(CC):
            slot = 20 + fc % 2
            t_wo = k.dma("y", wo_sl[fc % 2][:], scw[mats[nw]][fc], f"aw{slot}",
                         waits=[wfree.get(slot), pre["cast_tok"], last_aout])
            wfree[slot] = t_wo
            for qc in range(NQA):
                q0 = qc * QCA
                pp = 6 + (fc * NQA + qc) % 2
                k.twait(k.takepr(pp) + [t_wo, last_aout])
                for dc in range(CC):
                    k.mmw(pp, lambda e, dc=dc, q0=q0, fc=fc, pp=pp:
                          e.matmul(P[pp][:], wo_sl[fc % 2][:, dc * 128:dc * 128 + 128],
                                   aout[:, dc * S + q0:dc * S + q0 + QCA],
                                   start=(dc == 0), stop=(dc == CC - 1)),
                          start=(dc == 0), stop=(dc == CC - 1), nowait=True)
                tadd = k.prd("v", pp, lambda e, fc=fc, q0=q0, pp=pp: e.tensor_add(
                    sb["x"][:, fc * S + q0:fc * S + q0 + QCA],
                    sb["x"][:, fc * S + q0:fc * S + q0 + QCA], P[pp][:]))
            wfree[slot] = tadd
        return tadd

    def emit_ffn():
        FLV = int(os.environ.get("FFN_LV", "5"))
        esel_sb = ar("esel", (NE, NE * 128), BF16)
        cbias_sb = ar("cbias", (128, 256), F32)
        rw_sb = ar("rw", (128, 128), BF16)
        comb_fm = ar("comb_fm", (NE, S), BF16)
        hh = ar("hh", (128, 48 * QCA), BF16)
        hh_off = int(hh.handle.offset) if hasattr(hh, "handle") else None
        hh_base = A[0] - 48 * QCA * 2
        sco_fm = ar_at("sco_fm", (NE, S), F32, hh_base)
        scT = ar_at("scT", (128, 256), F32, hh_base + 8192)
        sbT = ar_at("sbT", (128, 256), F32, hh_base + 8192 + 1024)
        msk = ar_at("msk", (128, 256), F32, hh_base + 8192 + 2048)
        cnt = ar_at("cnt", (128, 256), F32, hh_base + 8192 + 3072)
        ctm = ar_at("ctm", (128, 256), F32, hh_base + 8192 + 4096)
        rb = hh_base + 8192 + 5120
        gs = ar_at("gs", (128, 64), F32, rb)
        gct = ar_at("gct", (128, 64), F32, rb + 256)
        gtm = ar_at("gtm", (128, 64), F32, rb + 512)
        gm = ar_at("gm", (128, 64), F32, rb + 768)
        s1 = ar_at("s1", (128, 64), F32, rb + 1024)
        s2 = ar_at("s2", (128, 64), F32, rb + 1280)
        s3 = ar_at("s3", (128, 64), F32, rb + 1536)
        wsum = ar_at("wsum", (128, 16), F32, rb + 1792)
        comb_tm = ar_at("comb_tm", (128, 256), F32, rb + 1856)
        silu_t = [ar(f"silu_t{i}", (128, QCA), BF16) for i in range(2)]
        t2b = [ar(f"t2b{i}", (128, QCA), BF16) for i in range(2)]
        sgsu_sl = [ar(f"sgsu{i}", (128, 2048), BF16) for i in range(2)]
        sd_sl = [ar(f"sd{i}", (128, 2048), BF16) for i in range(2)]
        dw_sl = [ar(f"dw{i}", (128, 4096), BF16) for i in range(2)]
        T[0] = top
        eg_sl = [ar(f"eg{i}", (128, 4096), BF16) for i in range(1)]
        eg_sl.append(ar_top("eg1", (128, 4096), BF16))
        ln_guard = ("vec", k.cnt["vec"])

        k.dma("y", esel_sb[:], ap["esel"], "dcst")
        tcs = k.dma("y", cbias_sb[:], ap["cbias"], "dcst")
        trw = k.dma("g", rw_sb[:], ap["rwT"], "gcst")
        cast_done = k.tok("fcst")
        last_xn = k.tok("vec")
        last_ln_act = k.tok("act")

        for qc in range(NQA):
            q0 = qc * QCA
            for ccx in range(CC):
                k.mmw(6, lambda e, ccx=ccx, q0=q0:
                      e.matmul(P[6][0:16, :], rw_sb[:, ccx * 16:ccx * 16 + 16],
                               sb["xn"][:, ccx * S + q0:ccx * S + q0 + QCA],
                               start=(ccx == 0), stop=(ccx == CC - 1)),
                      waits=[trw, last_xn, last_ln_act],
                      start=(ccx == 0), stop=(ccx == CC - 1))
            tsig = k.prd("s", 6, lambda e, q0=q0: e.activation(
                sco_fm[0:16, q0:q0 + QCA], P[6][0:16, :], AF.Sigmoid))
        for tt in range(16):
            k.mmw(7, lambda e, tt=tt: e.transpose(
                P[7][:, tt * 16:tt * 16 + 16], sco_fm[0:16, tt * 128:tt * 128 + 128],
                sb["identf_sb"][0:16, 0:16]),
                waits=[tsig, ident_tok[0]], start=(tt == 0), stop=(tt == 15))
        v0 = k.prd("v", 7, lambda e: e.tensor_copy(scT[:], P[7][:, 0:256]))
        vt = k.op("v", lambda e: e.tensor_add(sbT[:], scT[:], cbias_sb[:]),
                  waits=[v0, tcs], inc="vec")

        def re(apx):
            return apx.rearrange("p (t e) -> p t e", e=16)
        sb4 = sbT[:].rearrange("p (t g j) -> p t g j", g=4, j=4)
        g4 = gs[:].rearrange("p (t g) -> p t g", g=4)
        s1r = s1[:].rearrange("p (t g) -> p t g", g=4)
        s2r = s2[:].rearrange("p (t g) -> p t g", g=4)
        s3r = s3[:].rearrange("p (t g) -> p t g", g=4)
        gmr = gm[:].rearrange("p (t g) -> p t g", g=4)
        gcr = gct[:].rearrange("p (t g) -> p t g", g=4)
        gtr = gtm[:].rearrange("p (t g) -> p t g", g=4)

        def vop(fn):
            nonlocal vt
            vt = k.op("v", fn, waits=[vt], inc="vec")
            return vt
        vop(lambda e: e.tensor_tensor(s1r, sb4[:, :, :, 0], sb4[:, :, :, 1], ALU.max))
        vop(lambda e: e.tensor_tensor(s2r, sb4[:, :, :, 0], sb4[:, :, :, 1], ALU.min))
        vop(lambda e: e.tensor_tensor(s3r, sb4[:, :, :, 2], sb4[:, :, :, 3], ALU.max))
        vop(lambda e: e.tensor_tensor(gmr, sb4[:, :, :, 2], sb4[:, :, :, 3], ALU.min))
        vop(lambda e: e.tensor_tensor(gcr, s1r, s3r, ALU.max))
        vop(lambda e: e.tensor_tensor(s1r, s1r, s3r, ALU.min))
        vop(lambda e: e.tensor_tensor(s2r, s2r, gmr, ALU.max))
        vop(lambda e: e.tensor_tensor(s1r, s1r, s2r, ALU.max))
        vop(lambda e: e.tensor_tensor(g4, gcr, s1r, ALU.add))
        vop(lambda e: e.memset(gct[:], 0.0))
        for dd in range(1, 4):
            Aap = g4[:, :, 0:4 - dd]
            Bap = g4[:, :, dd:4]
            cA = gcr[:, :, 0:4 - dd]
            cB = gcr[:, :, dd:4]
            tmp = gtr[:, :, 0:4 - dd]
            vop(lambda e, A=Aap, B=Bap, t=tmp: e.tensor_tensor(t, A, B, ALU.is_ge))
            vop(lambda e, t=tmp, c2=cB: e.tensor_tensor(c2, c2, t, ALU.add))
            vop(lambda e, A=Aap, B=Bap, t=tmp: e.tensor_tensor(t, B, A, ALU.is_gt))
            vop(lambda e, t=tmp, c2=cA: e.tensor_tensor(c2, c2, t, ALU.add))
        vop(lambda e: e.tensor_single_scalar(gm[:], gct[:], 2.0, ALU.is_lt))
        mr4 = msk[:].rearrange("p (t g j) -> p t g j", g=4, j=4)
        for j in range(4):
            vop(lambda e, j=j: e.scalar_tensor_tensor(
                mr4[:, :, :, j], sb4[:, :, :, j], 4.0, gmr, ALU.add, ALU.mult))
        vop(lambda e: e.memset(cnt[:], 0.0))
        mre = re(msk[:])
        cre = re(cnt[:])
        tre = re(ctm[:])
        for dd in range(1, 16):
            Aap = mre[:, :, 0:16 - dd]
            Bap = mre[:, :, dd:16]
            cA = cre[:, :, 0:16 - dd]
            cB = cre[:, :, dd:16]
            tmp = tre[:, :, 0:16 - dd]
            vop(lambda e, A=Aap, B=Bap, t=tmp: e.tensor_tensor(t, A, B, ALU.is_ge))
            vop(lambda e, t=tmp, c2=cB: e.tensor_tensor(c2, c2, t, ALU.add))
            vop(lambda e, A=Aap, B=Bap, t=tmp: e.tensor_tensor(t, B, A, ALU.is_gt))
            vop(lambda e, t=tmp, c2=cA: e.tensor_tensor(c2, c2, t, ALU.add))
        vop(lambda e: e.tensor_single_scalar(msk[:], cnt[:], 4.0, ALU.is_lt))
        vop(lambda e: e.tensor_mul(ctm[:], scT[:], msk[:]))
        vop(lambda e: e.tensor_reduce(wsum[:], re(ctm[:]), AX.X, ALU.add))
        vop(lambda e: e.tensor_scalar_add(wsum[:], wsum[:], 1e-20))
        vop(lambda e: e.reciprocal(wsum[:], wsum[:]))
        winb = wsum[:].rearrange("p (t e) -> p t e", e=1).broadcast_to((128, 16, 16))
        vop(lambda e: e.tensor_tensor(re(comb_tm[:]), re(ctm[:]), winb, ALU.mult))
        comb_vt = vt
        comb_state = {}

        def emit_comb_back():
            cevac = None
            for b4 in range(4):
                for j4 in range(4):
                    tt = 4 * b4 + j4
                    k.mmw(7, lambda e, tt=tt, j4=j4: e.transpose(
                        P[7][0:16, j4 * 128:j4 * 128 + 128],
                        comb_tm[:, tt * 16:tt * 16 + 16], sb["identf_sb"][:]),
                        waits=[comb_vt], start=(j4 == 0), stop=(j4 == 3))
                cevac = k.prd("s", 7, lambda e, b4=b4: e.copy(
                    comb_fm[0:16, b4 * 512:b4 * 512 + 512], P[7][0:16, 0:512]))
            comb_state["done"] = cevac

        emit_comb_back()

        sl_free = {0: None, 1: None}
        ed_free = {0: None, 1: None}
        dwf = {0: None, 1: None}
        sdf = {0: None, 1: None}
        st_free = {0: None, 1: None}
        t2_free = {0: None, 1: None}
        hh_tok = {}
        down_done = None
        res_tok = None
        for qc in range(NQA):
            q0 = qc * QCA
            if FLV < 2:
                continue
            for hc in range(16):
                sl = hc % 2
                tl = k.dma("y", sgsu_sl[sl][:], sc["sgsu"][hc], f"fs{sl}",
                           waits=[cast_done, sl_free[sl]])
                pp = hc % 2
                for ccx in range(CC):
                    k.mmw(pp, lambda e, ccx=ccx, q0=q0, sl=sl, pp=pp:
                          e.matmul(P[pp][:], sgsu_sl[sl][:, ccx * 128:ccx * 128 + 128],
                                   sb["xn"][:, ccx * S + q0:ccx * S + q0 + QCA],
                                   start=(ccx == 0), stop=(ccx == CC - 1)),
                          waits=[tl, last_xn, last_ln_act],
                          start=(ccx == 0), stop=(ccx == CC - 1))
                for ccx in range(CC):
                    tu = k.mmw(2 + pp, lambda e, ccx=ccx, q0=q0, sl=sl, pp=pp:
                               e.matmul(P[2 + pp][:],
                                        sgsu_sl[sl][:, 1024 + ccx * 128:1024 + ccx * 128 + 128],
                                        sb["xn"][:, ccx * S + q0:ccx * S + q0 + QCA],
                                        start=(ccx == 0), stop=(ccx == CC - 1)),
                               waits=[tl],
                               start=(ccx == 0), stop=(ccx == CC - 1))
                ts = k.prd("s", pp, lambda e, pp=pp: e.activation(
                    silu_t[pp][:], P[pp][:], AF.Silu), waits=[st_free[pp]])
                th = k.prd("v", 2 + pp, lambda e, hc=hc, pp=pp: e.tensor_mul(
                    hh[:, hc * QCA:hc * QCA + QCA], silu_t[pp][:], P[2 + pp][:]),
                    waits=[ts, down_done])
                st_free[pp] = th
                hh_tok[hc] = th
                sl_free[sl] = tu
            comb_done = comb_state["done"]
            for ee in range(NE):
                if FLV < 3:
                    break
                sl = ee % 2
                tl = k.dma("y", eg_sl[sl][:], sc["expgu"][ee], f"fe{sl}",
                           waits=[cast_done, ed_free[sl], ln_guard if sl == 1 else None])
                tcb = k.mmw(4, lambda e, ee=ee, q0=q0: e.matmul(
                    P[4][:], esel_sb[:, ee * 128:ee * 128 + 128],
                    comb_fm[0:16, q0:q0 + QCA], start=True, stop=True),
                    waits=[comb_done])
                for h2 in range(2):
                    pp = h2
                    for ccx in range(CC):
                        k.mmw(pp, lambda e, ccx=ccx, q0=q0, sl=sl, h2=h2, pp=pp:
                              e.matmul(P[pp][:],
                                       eg_sl[sl][:, (h2 * 8 + ccx) * 128:(h2 * 8 + ccx) * 128 + 128],
                                       sb["xn"][:, ccx * S + q0:ccx * S + q0 + QCA],
                                       start=(ccx == 0), stop=(ccx == CC - 1)),
                              waits=[tl], start=(ccx == 0), stop=(ccx == CC - 1))
                    for ccx in range(CC):
                        tu = k.mmw(2 + pp, lambda e, ccx=ccx, q0=q0, sl=sl, h2=h2, pp=pp:
                                   e.matmul(P[2 + pp][:],
                                            eg_sl[sl][:, 2048 + (h2 * 8 + ccx) * 128:2048 + (h2 * 8 + ccx) * 128 + 128],
                                            sb["xn"][:, ccx * S + q0:ccx * S + q0 + QCA],
                                            start=(ccx == 0), stop=(ccx == CC - 1)),
                                   waits=[tl], start=(ccx == 0), stop=(ccx == CC - 1))
                    ts = k.prd("s", pp, lambda e, pp=pp: e.activation(
                        silu_t[pp][:], P[pp][:], AF.Silu), waits=[st_free[pp]])
                    tm = k.prd("v", 2 + pp, lambda e, pp=pp: e.tensor_mul(
                        t2b[pp][:], silu_t[pp][:], P[2 + pp][:]),
                        waits=[ts, t2_free[pp]])
                    st_free[pp] = tm
                    th = k.prd("v", 4, lambda e, ee=ee, h2=h2, pp=pp: e.tensor_mul(
                        hh[:, (16 + 2 * ee + h2) * QCA:(16 + 2 * ee + h2 + 1) * QCA],
                        t2b[pp][:], P[4][:]), waits=[tm, down_done])
                    t2_free[pp] = th
                    hh_tok[16 + 2 * ee + h2] = th
                ed_free[sl] = tu
            for ccx in range(CC):
                if FLV < 4:
                    break
                sl = ccx % 2
                tsd = k.dma("y", sd_sl[sl][:], sc["sd"][ccx], f"fd{sl}",
                            waits=[cast_done, sdf[sl]])
                tdw = k.dma("y", dw_sl[sl][:], sc["dwcc"][ccx], f"fw{sl}",
                            waits=[cast_done, dwf[sl]])
                pp = 5 + ccx % 2
                shstop = FLV < 5
                for hc in range(16):
                    tk = k.mmw(pp, lambda e, hc=hc, sl=sl, pp=pp,
                               sp=(shstop and hc == 15):
                               e.matmul(P[pp][:], sd_sl[sl][:, hc * 128:hc * 128 + 128],
                                        hh[:, hc * QCA:hc * QCA + QCA],
                                        start=(hc == 0), stop=sp),
                               waits=[tsd, hh_tok[hc]],
                               start=(hc == 0), stop=(shstop and hc == 15))
                for ee in range(NE):
                    if FLV < 5:
                        break
                    for h2 in range(2):
                        idx = 2 * ee + h2
                        tk = k.mmw(pp, lambda e, idx=idx, sl=sl, pp=pp, sp=(idx == 31):
                                   e.matmul(P[pp][:],
                                            dw_sl[sl][:, idx * 128:idx * 128 + 128],
                                            hh[:, (16 + idx) * QCA:(16 + idx + 1) * QCA],
                                            start=False, stop=sp),
                                   waits=[tdw, hh_tok[16 + idx]],
                                   start=False, stop=(idx == 31))
                def _res(e, ccx=ccx, q0=q0, pp=pp, qc=qc):
                    inst = e.tensor_add(
                        sb["x"][:, ccx * S + q0:ccx * S + q0 + QCA],
                        sb["x"][:, ccx * S + q0:ccx * S + q0 + QCA], P[pp][:])
                    inst.annotate(f"res qc{qc} ccx{ccx} pp{pp}")
                    return inst
                res_tok = k.prd("v", pp, _res)
                sdf[sl] = tk
                dwf[sl] = tk
                down_done = res_tok
        return res_tok

    def emit_store():
        for ccx in range(CC):
            k.dma("y", yout[ccx], sb["x"][:, ccx * S:(ccx + 1) * S], "dst")
        k.op("y", lambda e: None, waits=[("dst", 16 * CC)])

    emit_load()
    k.barrier()
    i = 0
    while i < len(stages):
        st = stages[i]
        ar_reset()
        if st.startswith("ln"):
            nxt = stages[i + 1] if i + 1 < len(stages) else None
            pre = None
            if nxt in ("attn_a", "attn_b"):
                pre = attn_alloc_preload("a" if nxt == "attn_a" else "b")
            emit_ln()
            if nxt in ("attn_a", "attn_b"):
                emit_attn(pre)
                i += 1
            elif nxt is not None and nxt.startswith("ffn"):
                emit_ffn()
                i += 1
        k.barrier()
        i += 1
    emit_store()

    for name in list(k.cnt):
        k.sems[name] = nc.alloc_semaphore(name)

    with nc.Block() as block:
        @block.tensor
        def _(e):
            for f in k.prog["t"]:
                f(e)

        @block.vector
        def _(e):
            for f in k.prog["v"]:
                f(e)

        @block.scalar
        def _(e):
            for f in k.prog["s"]:
                f(e)

        @block.gpsimd
        def _(e):
            for f in k.prog["g"]:
                f(e)

        @block.sync
        def _(e):
            for f in k.prog["y"]:
                f(e)
    return nc


def kernel(**inputs):
    return _run(inputs, list(ALL_STAGES))


def _run(inputs, stages, cores=None):
    hp = host_prep(inputs)
    x = np.asarray(inputs["L0_attn_a_x"])
    B = x.shape[0]
    if cores is None:
        cores = list(range(B))
    nc = build_nc(tuple(stages))
    in_maps = []
    for b in cores:
        m = dict(hp)
        m["xin"] = np.ascontiguousarray(x[b].T.reshape(CC, 128, S))
        in_maps.append(m)
    res = run_bass_kernel_spmd(nc, in_maps, core_ids=list(range(len(cores))))
    outs = []
    for r in res.results:
        y = np.asarray(r["y"]).reshape(H, S)
        outs.append(np.ascontiguousarray(y.T))
    return np.stack(outs, axis=0)


# version 10
# speedup vs baseline: 1.1314x; 1.0146x over previous
import os
import numpy as np
import ml_dtypes
import concourse.bass as bass
import concourse.mybir as mybir
from concourse.bass_utils import run_bass_kernel_spmd

F32 = mybir.dt.float32
BF16 = mybir.dt.bfloat16
AF = mybir.ActivationFunctionType
ALU = mybir.AluOpType
AX = mybir.AxisListType
BF = ml_dtypes.bfloat16

S = 2048
H = 1024
CC = 8
NH = 16
DH = 64
QCA = 512
NQA = 4
QCB = 512
NQB = 4
NE = 16
WINDOW = 256

ENGS = ("t", "v", "s", "g", "y")
ALL_STAGES = ("ln1", "attn_a", "ln2", "ffn1", "ln3", "attn_b", "ln4", "ffn2")


class Rec:

    def __init__(self, nc):
        self.nc = nc
        self.prog = {e: [] for e in ENGS}
        self.cnt = {}
        self.lastwait = {e: {} for e in ENGS}
        self.sems = {}
        self.owner = {}
        self.nbar = 0
        self.pw = [None] * 8
        self.pr = [[] for _ in range(8)]

    def sem(self, name, owner=None):
        self.cnt[name] = 0
        self.owner[name] = owner

    def tok(self, name):
        return (name, self.cnt[name])

    def op(self, eng, fn, waits=(), inc=None, n=1):
        lw = self.lastwait[eng]
        real = []
        for w in waits:
            if w is None:
                continue
            s_, v = w
            if v > 0 and lw.get(s_, 0) < v:
                real.append((s_, v))
                lw[s_] = v
        out_tok = None
        if inc is not None:
            self.cnt[inc] += n
            out_tok = (inc, self.cnt[inc])
        sems = self.sems

        def play(e, real=real, fn=fn, inc=inc, n=n):
            for s_, v in real:
                e.wait_ge(sems[s_], v)
            inst = fn(e)
            if inc is not None:
                if inst is None:
                    e.nop().then_inc(sems[inc], n)
                else:
                    inst.then_inc(sems[inc], n)
        self.prog[eng].append(play)
        return out_tok

    def dma(self, eng, out, in_, sem, waits=()):
        return self.op(eng, lambda e, o=out, i=in_: e.dma_start(out=o, in_=i),
                       waits=waits, inc=sem, n=16)

    def mmw(self, bank, fn, waits=(), start=True, stop=True, nowait=False, noinc=False):
        ws = list(waits)
        if start and not nowait:
            ws += self.pr[bank]
        do_inc = (stop and not noinc) or noinc == "force"
        tok = self.op("t", fn, ws, inc="mm" if do_inc else None)
        if start:
            self.pr[bank] = []
        if stop:
            self.pw[bank] = tok
        return tok

    def takepr(self, bank):
        r = self.pr[bank]
        self.pr[bank] = []
        return list(r)

    def twait(self, waits):
        self.op("t", lambda e: None, waits=waits)

    def prd(self, eng, bank, fn, waits=(), inc=None):
        inc = inc or {"v": "vec", "s": "act"}[eng]
        tok = self.op(eng, fn, list(waits) + [self.pw[bank]], inc=inc)
        self.pr[bank].append(tok)
        return tok

    def prd2(self, eng, banks, fn, waits=(), inc=None):
        inc = inc or {"v": "vec", "s": "act"}[eng]
        tok = self.op(eng, fn, list(waits) + [self.pw[b] for b in banks], inc=inc)
        for b in banks:
            self.pr[b].append(tok)
        return tok

    def barrier(self):
        self.nbar += 1
        target = 5 * self.nbar
        for eng in ENGS:
            own = [(n, self.cnt[n]) for n, o in self.owner.items()
                   if o == eng and self.cnt[n] > 0]
            self.op(eng, lambda e: None, waits=own, inc="ph")
        for eng in ENGS:
            self.op(eng, lambda e: None, waits=[("ph", target)])
        return ("ph", target)


def _blocked(wt):
    c, f = wt.shape
    return np.ascontiguousarray(
        wt.reshape(c // 128, 128, f // 128, 128).transpose(2, 1, 0, 3))


def host_prep(ins):
    d = {}
    pm = np.zeros(H, np.int64)
    pr = np.zeros(H, np.int64)
    for h in range(NH):
        for j in range(DH):
            if j < 32:
                pm[64 * h + j] = 64 * h + 2 * j
                pr[64 * h + j] = 64 * h + 2 * j + 1
            else:
                pm[64 * h + j] = 64 * h + 2 * (j - 32) + 1
                pr[64 * h + j] = 64 * h + 2 * (j - 32)
    wq, wk = np.asarray(ins["L0_attn_a_w_q"]), np.asarray(ins["L0_attn_a_w_k"])
    d["aqm"] = _blocked(np.ascontiguousarray(wq[pm].T))
    d["aqr"] = _blocked(np.ascontiguousarray(wq[pr].T))
    d["akm"] = _blocked(np.ascontiguousarray(wk[pm].T))
    d["akr"] = _blocked(np.ascontiguousarray(wk[pr].T))
    d["avT"] = _blocked(np.ascontiguousarray(np.asarray(ins["L0_attn_a_w_v"]).T))
    d["aoT"] = _blocked(np.ascontiguousarray(np.asarray(ins["L0_attn_a_w_o"]).T))
    d["bqT"] = _blocked(np.ascontiguousarray(np.asarray(ins["L0_attn_b_Wq"]).T))
    d["bkT"] = _blocked(np.ascontiguousarray(np.asarray(ins["L0_attn_b_Wk"]).T))
    d["bvT"] = _blocked(np.ascontiguousarray(np.asarray(ins["L0_attn_b_Wv"]).T))
    d["boT"] = _blocked(np.ascontiguousarray(np.asarray(ins["L0_attn_b_Wo"]).T))
    sgb = _blocked(np.ascontiguousarray(np.asarray(ins["L0_ffn_shared_gate_w"]).T))
    sub = _blocked(np.ascontiguousarray(np.asarray(ins["L0_ffn_shared_up_w"]).T))
    d["sgsu"] = np.ascontiguousarray(
        np.concatenate([sgb.reshape(16, 128, 1024), sub.reshape(16, 128, 1024)], axis=2))
    sdw = np.ascontiguousarray(np.asarray(ins["L0_ffn_shared_down_w"]).T)
    d["sd"] = np.ascontiguousarray(
        sdw.reshape(16, 128, 8, 128).transpose(2, 1, 0, 3).reshape(8, 128, 2048))
    gw = np.asarray(ins["L0_ffn_gate_w"])
    uw = np.asarray(ins["L0_ffn_up_w"])
    dw = np.asarray(ins["L0_ffn_down_w"])

    def gu_blk(w):
        return np.ascontiguousarray(
            w.reshape(NE, 8, 128, 2, 128).transpose(0, 2, 3, 1, 4).reshape(NE, 128, 2048))
    d["expgu"] = np.ascontiguousarray(
        np.concatenate([gu_blk(gw), gu_blk(uw)], axis=2))
    d["dwcc"] = np.ascontiguousarray(
        dw.reshape(NE, 2, 128, 8, 128).transpose(3, 2, 0, 1, 4).reshape(8, 128, 4096))
    d["rwT"] = np.ascontiguousarray(
        np.asarray(ins["L0_ffn_router_w"]).T.reshape(8, 128, 16).transpose(1, 0, 2).reshape(128, 128))
    d["cbias"] = np.ascontiguousarray(
        np.tile(np.asarray(ins["L0_ffn_correction_bias"]).astype(np.float32), (128, NE)))
    half = DH // 2
    inv = 10000.0 ** (-(np.arange(half, dtype=np.float64) * 2.0 / DH))
    t = np.arange(S, dtype=np.float64)
    ang = t[None, :] * inv[:, None]
    cos = np.cos(ang).astype(np.float32)
    sin = np.sin(ang).astype(np.float32)
    csA = np.empty((128, S), BF)
    csB = np.empty((128, S), BF)
    for b in range(4):
        csA[32 * b:32 * b + 32] = cos
        csB[32 * b:32 * b + 32] = (-sin) if (b % 2 == 0) else sin
    d["csA"], d["csB"] = csA, csB
    kk = np.arange(128)[:, None]
    qq = np.arange(QCA)[None, :]
    ma = np.zeros((128, 4 * QCA), BF)
    for r in range(4):
        ma[:, r * QCA:(r + 1) * QCA] = (qq >= kk + 128 * r).astype(BF)
    d["maska"] = ma
    qq2 = np.arange(QCB)[None, :]
    mb = np.zeros((128, 6 * QCB), BF)
    for i, r in enumerate((-2, -1, 0, 1, 2, 3)):
        off = qq2 - kk - 128 * r
        mb[:, i * QCB:(i + 1) * QCB] = ((off >= 0) & (off <= WINDOW)).astype(BF)
    d["maskb"] = mb
    d["identf"] = np.eye(128, dtype=np.float32)
    selab = np.zeros((128, 256), BF)
    selab[64, 0:128] = 1.0
    selab[65, 128:256] = 1.0
    d["selab"] = selab
    esel = np.zeros((NE, NE * 128), BF)
    for e in range(NE):
        esel[e, 128 * e:128 * e + 128] = 1.0
    d["esel"] = esel
    return d


def build_nc(stages):
    nc = bass.Bass("TRN2", target_bir_lowering=False)
    ap = {}

    def din(name, shape, dtype=F32):
        ap[name] = nc.dram_tensor(name, list(shape), dtype, kind="ExternalInput").ap()

    din("xin", (CC, 128, S))
    for n in ("aqm", "aqr", "akm", "akr", "avT", "aoT", "bqT", "bkT", "bvT", "boT"):
        din(n, (8, 128, 8, 128))
    din("sgsu", (16, 128, 2048))
    din("sd", (8, 128, 2048))
    din("expgu", (NE, 128, 4096))
    din("dwcc", (8, 128, 4096))
    din("rwT", (128, 128))
    din("cbias", (128, 256))
    din("csA", (128, S), BF16)
    din("csB", (128, S), BF16)
    din("maska", (128, 4 * QCA), BF16)
    din("maskb", (128, 6 * QCB), BF16)
    din("identf", (128, 128))
    din("esel", (NE, NE * 128), BF16)
    din("selab", (128, 256), BF16)
    yout = nc.dram_tensor("y", [CC, 128, S], F32, kind="ExternalOutput").ap()

    sc = {}
    scw = {}
    need_attn = any(st.startswith("attn") for st in stages)
    if need_attn:
        for n in ("aqm", "aqr", "akm", "akr", "avT", "aoT", "bqT", "bkT", "bvT", "boT"):
            scw[n] = nc.dram_tensor(f"{n}_bf", [8, 128, 1024], BF16, kind="Internal").ap()
    need_ffn = any(st.startswith("ffn") for st in stages)
    if need_ffn:
        sc["sgsu"] = nc.dram_tensor("sgsu_bf", [16, 128, 2048], BF16, kind="Internal").ap()
        sc["sd"] = nc.dram_tensor("sd_bf", [8, 128, 2048], BF16, kind="Internal").ap()
        sc["expgu"] = nc.dram_tensor("expgu_bf", [NE, 128, 4096], BF16, kind="Internal").ap()
        sc["dwcc"] = nc.dram_tensor("dwcc_bf", [8, 128, 4096], BF16, kind="Internal").ap()

    sb = {}

    def sbt(name, shape, dtype, off=None):
        if off is None:
            sb[name] = nc.alloc_sbuf_tensor(name, list(shape), dtype)
        else:
            sb[name] = nc.alloc_sbuf_tensor_at(name, list(shape), dtype, offset=off)
        return sb[name]

    sbt("x", (128, CC * S), F32)
    sbt("xn", (128, CC * S), BF16)
    sbt("ones_bf", (128, 128), BF16)
    sbt("eps_sb", (128, 1), F32)
    sbt("onesf", (128, 128), F32)
    sbt("identf_sb", (128, 128), F32)
    arena = nc.sbuf_base
    top = nc.sbuf_top
    A = [arena]
    ph_id = [0]

    def ar(name, shape, dtype, align=64):
        nbytes = int(np.prod(shape[1:])) * (2 if dtype == BF16 else 4)
        off = (A[0] + align - 1) // align * align
        assert off + nbytes <= top, f"SBUF arena overflow at {name}: {off+nbytes-top} over"
        A[0] = off + nbytes
        return sbt(f"{name}_{ph_id[0]}", shape, dtype, off=off)

    def ar_at(name, shape, dtype, off):
        return sbt(f"{name}_{ph_id[0]}", shape, dtype, off=off)

    T = [top]

    ar_offs = {}

    def ar_top(name, shape, dtype, align=64):
        nbytes = int(np.prod(shape[1:])) * (2 if dtype == BF16 else 4)
        off = (T[0] - nbytes) // align * align
        assert off >= A[0], f"SBUF top-arena overflow at {name}"
        T[0] = off
        ar_offs[name] = off
        return sbt(f"{name}_{ph_id[0]}", shape, dtype, off=off)

    def ar_reset():
        A[0] = arena
        T[0] = top
        ph_id[0] += 1

    PPf = [nc.alloc_psum_tensor("PPa", [128, 1024], F32),
           nc.alloc_psum_tensor("PPb", [128, 1024], F32)]
    P = [PPf[0][:, 0:512], PPf[0][:, 512:1024],
         PPf[1][:, 0:512], PPf[1][:, 512:1024]]
    P += [nc.alloc_psum_tensor(f"P{i}", [128, 512], F32)[:] for i in range(4, 8)]

    k = Rec(nc)
    k.sem("mm", "t")
    k.sem("act", "s")
    k.sem("vec", "v")
    k.sem("ph", None)
    for s_ in ("dxin", "dcst", "dst", "fs0", "fs1", "fd0", "fd1", "fe0", "fe1",
               "fw0", "fw1"):
        k.sem(s_, "y")
    for s_ in ("gcst", "fcst", "awcast"):
        k.sem(s_, "g")
    for i in range(12):
        k.sem(f"aw{i}", "y")
    for i in (20, 21):
        k.sem(f"aw{i}", "y")

    ident_tok = [None]

    def emit_load():
        for ccx in range(CC):
            k.dma("y", sb["x"][:, ccx * S:(ccx + 1) * S], ap["xin"][ccx], "dxin")
        ident_tok[0] = k.dma("y", sb["identf_sb"][:], ap["identf"], "dcst")
        k.op("v", lambda e: e.memset(sb["ones_bf"][:], 1.0), inc="vec")
        k.op("v", lambda e: e.memset(sb["onesf"][:], 1.0), inc="vec")
        k.op("v", lambda e: e.memset(sb["eps_sb"][:], 1e-5), inc="vec")
        if need_attn:
            for n in scw:
                k.dma("g", scw[n], ap[n].rearrange("a p c j -> a p (c j)"), "awcast")
        if need_ffn:
            k.dma("g", sc["sgsu"], ap["sgsu"], "fcst")
            k.dma("g", sc["sd"], ap["sd"], "fcst")
            for e2 in range(4):
                k.dma("g", sc["expgu"][4 * e2:4 * e2 + 4], ap["expgu"][4 * e2:4 * e2 + 4], "fcst")
            k.dma("g", sc["dwcc"], ap["dwcc"], "fcst")

    def emit_ln():
        lnbf = ar_top("lnbf", (128, 2 * QCA), BF16)
        lnsq = ar_top("lnsq", (128, 2 * QCA), BF16)
        m1 = ar_top("m1", (128, QCA), F32)
        m2 = ar_top("m2", (128, QCA), F32)
        m3 = ar_top("m3", (128, QCA), F32)
        m4 = ar_top("m4", (128, QCA), F32)
        xin_tok = k.tok("dxin")
        prev_last = None
        sq_war = {0: None, 1: None}
        mm_war = {0: None, 1: None}
        for qc in range(NQA):
            q0 = qc * QCA
            for ccx in range(CC):
                slot = ccx % 2
                war = mm_war.get(slot)
                tb = k.op("s", lambda e, ccx=ccx, slot=slot, q0=q0:
                          e.copy(lnbf[:, slot * QCA:(slot + 1) * QCA],
                                 sb["x"][:, ccx * S + q0:ccx * S + q0 + QCA]),
                          waits=[xin_tok, war, sq_war[slot]], inc="act")
                tq = k.op("v", lambda e, slot=slot:
                          e.tensor_mul(lnsq[:, slot * QCA:(slot + 1) * QCA],
                                       lnbf[:, slot * QCA:(slot + 1) * QCA],
                                       lnbf[:, slot * QCA:(slot + 1) * QCA]),
                          waits=[tb, war], inc="vec")
                sq_war[slot] = tq
                k.mmw(0, lambda e, slot=slot, ccx=ccx:
                      e.matmul(P[0][:], sb["ones_bf"][:],
                               lnbf[:, slot * QCA:(slot + 1) * QCA],
                               start=(ccx == 0), stop=(ccx == CC - 1)),
                      waits=[tb], start=(ccx == 0), stop=(ccx == CC - 1))
                mm_war[slot] = k.mmw(1, lambda e, slot=slot, ccx=ccx:
                      e.matmul(P[1][:], sb["ones_bf"][:],
                               lnsq[:, slot * QCA:(slot + 1) * QCA],
                               start=(ccx == 0), stop=(ccx == CC - 1)),
                      waits=[tq], start=(ccx == 0), stop=(ccx == CC - 1),
                      noinc="force")
            t1 = k.prd("s", 0, lambda e: e.mul(m1[:], P[0][:], 1.0 / H), waits=[prev_last])
            t2 = k.prd("s", 1, lambda e: e.mul(m2[:], P[1][:], 1.0 / H), waits=[prev_last])
            t3 = k.op("v", lambda e: e.tensor_mul(m3[:], m1[:], m1[:]),
                      waits=[t1, prev_last], inc="vec")
            t4 = k.op("v", lambda e: e.tensor_sub(m2[:], m2[:], m3[:]), waits=[t2, t3], inc="vec")
            t5 = k.op("s", lambda e: e.activation(m3[:], m2[:], AF.Ln, bias=sb["eps_sb"][:, 0:1]),
                      waits=[t4], inc="act")
            t6 = k.op("s", lambda e: e.activation(m2[:], m3[:], AF.Exp, scale=-0.5),
                      waits=[t5], inc="act")
            t7 = k.op("v", lambda e: e.tensor_mul(m3[:], m1[:], m2[:]), waits=[t6], inc="vec")
            lastc = prev_last
            for ccx in range(CC):
                t8 = k.op("v", lambda e, ccx=ccx, q0=q0:
                          e.tensor_mul(m4[:], sb["x"][:, ccx * S + q0:ccx * S + q0 + QCA], m2[:]),
                          waits=[t6, lastc], inc="vec")
                lastc = k.op("v", lambda e, ccx=ccx, q0=q0:
                             e.tensor_sub(sb["xn"][:, ccx * S + q0:ccx * S + q0 + QCA],
                                          m4[:], m3[:]),
                             waits=[t7, t8], inc="vec")
            last = lastc
            prev_last = last
        return last

    def attn_alloc_preload(mode):
        a = mode == "a"
        pre = {"a": a}
        pre["csA"] = ar("csA", (128, S), BF16) if a else None
        pre["csB"] = ar("csB", (128, S), BF16) if a else None
        pre["mask"] = ar("maska", (128, 4 * QCA), BF16) if a else ar("maskb", (128, 6 * QCB), BF16)
        pre["Qh"] = ar("Qh", (128, 2 * S), BF16)
        pre["Kh"] = ar("Kh", (128, 2 * S), BF16)
        pre["VT"] = ar("VT", (128, 2 * 16 * 130), BF16)
        pre["zv"] = ar("zv", (128, 132), BF16)
        pre["selAB"] = ar("selAB", (128, 256), BF16)
        pre["aout"] = ar("aout", (128, CC * S), BF16)
        nsl = 6 if a else 4
        nqk = nsl - 2
        pre["nqk"] = nqk
        pre["wsl"] = [ar(f"w{i}", (128, 1024), BF16) for i in range(2 * nqk + 1)]
        pre["rp1"] = ar("rp1", (128, QCA), F32)
        pre["rp2"] = ar("rp2", (128, QCA), F32)
        pre["qsq"] = ar("qsq", (128, 2 * QCA), BF16)
        pre["ksq"] = ar("ksq", (128, 2 * QCA), BF16)
        pre["nacc"] = ar("nacc", (1, 32), F32)
        pre["nred"] = ar("nred", (1, 16), F32)
        pre["bias_sb"] = ar("bias_sb", (128, 2), F32)
        if a:
            k.dma("y", pre["csA"][:], ap["csA"], "dcst")
            k.dma("y", pre["csB"][:], ap["csB"], "dcst")
            k.dma("y", pre["mask"][:], ap["maska"], "dcst")
        else:
            k.dma("y", pre["mask"][:], ap["maskb"], "dcst")
        tok = k.dma("y", pre["selAB"][:], ap["selab"], "dcst")
        pre["cst_tok"] = tok
        pre["selAB_tok"] = tok
        mats = (["aqm", "aqr", "akm", "akr", "avT", "aoT"] if a
                else ["bqT", "bkT", "bvT", "boT"])
        pre["mats"] = mats
        nm = 4 if a else 2
        pre["nm"] = nm
        cast_tok = ("awcast", k.cnt["awcast"])
        pre["cast_tok"] = cast_tok
        wtok = {}
        wfree = {}
        pre["wtok"] = wtok
        pre["wfree"] = wfree

        def loadw(mi, c):
            slot = (c % 2) * nqk + mi if mi < nm else 2 * nqk
            t = k.dma("y", pre["wsl"][slot][:], scw[mats[mi]][c], f"aw{slot}",
                      waits=[wfree.get(slot), cast_tok])
            wtok[(mi, c)] = (t, slot)
        pre["loadw"] = loadw
        for mi in range(nm + 1):
            loadw(mi, 0)
        for mi in range(nm):
            loadw(mi, 1)
        return pre

    def emit_attn(pre):
        a = pre["a"]
        QC = QCA if a else QCB
        NQ = NQA if a else NQB
        LAG = 2
        T[0] = top
        ebuf = ar_top("ebuf", (128, 4 * QC), BF16)
        embuf = ar_top("embuf", (128, 4 * QC), BF16)
        nsb = ar_top("nsb", (128, QC), F32)
        dinv = ar_top("dinv", (66, QC), BF16)
        wo_sl = [ar_at("wo0", (128, 1024), BF16, ar_offs["dinv"]),
                 ar_at("wo1", (128, 1024), BF16, ar_offs["dinv"] + 2048)]
        pre["wo_sl"] = wo_sl
        selAB = pre["selAB"]
        selAB_tok = pre["selAB_tok"]
        csA, csB, mask = pre["csA"], pre["csB"], pre["mask"]
        QhF, KhF, VTF = pre["Qh"], pre["Kh"], pre["VT"]
        zv = pre["zv"]
        aout = pre["aout"]
        wsl, wo_sl = pre["wsl"], pre["wo_sl"]
        rp1, rp2 = pre["rp1"], pre["rp2"]
        qsq_a, ksq_a = pre["qsq"], pre["ksq"]
        nacc, nred = pre["nacc"], pre["nred"]
        bias_sb = pre["bias_sb"]
        cst_tok = pre["cst_tok"]
        mats, nm, nqk = pre["mats"], pre["nm"], pre["nqk"]
        nw = len(mats) - 1
        iv = nm
        wtok, wfree, loadw = pre["wtok"], pre["wfree"], pre["loadw"]
        last_xn = k.tok("vec")
        last_ln_act = k.tok("act")
        vz1 = k.op("v", lambda e: e.memset(
            VTF[:].rearrange("p (t c) -> p t c", c=130)[:, :, 64:65], 1.0), inc="vec")
        vz2 = k.op("v", lambda e: e.memset(
            VTF[:].rearrange("p (t c) -> p t c", c=130)[:, :, 129:130], 1.0), inc="vec")
        vz3 = k.op("v", lambda e: e.memset(zv[:, 0:64], 0.0), inc="vec")
        vz4 = k.op("v", lambda e: e.memset(zv[:, 64:65], 1.0), inc="vec")
        vz5 = k.op("v", lambda e: e.memset(zv[:, 65:131], 0.0), inc="vec")
        vz6 = k.op("v", lambda e: e.memset(zv[:, 131:132], 1.0), inc="vec")
        zv_tok = vz6

        chunk_info = {}
        prev_sq = {}
        prev_norm = {}
        rope_prev = [None]
        chunk_done_mm = {-1: None, -2: None}

        def bg_units(c):
            par = c % 2
            Qh = QhF[:, par * S:(par + 1) * S]
            Kh = KhF[:, par * S:(par + 1) * S]
            VT = VTF[:, par * 2080:(par + 1) * 2080]
            units = []
            info = {}
            chunk_info[c] = info
            old_tok = chunk_done_mm.get(c - 2)

            def u_proj(qc, pair):
                def f():
                    q0 = qc * QCA
                    mi0 = pair * (2 if a else 1)
                    wt0, sl0 = wtok[(mi0, c)]
                    w = k.takepr(6) + [wt0, last_xn, last_ln_act, old_tok]
                    if a:
                        w += k.takepr(7) + [wtok[(mi0 + 1, c)][0]]
                    k.twait(w)
                    wb0 = wsl[sl0]
                    for ccx in range(CC):
                        tk = k.mmw(6, lambda e, ccx=ccx, q0=q0, wb0=wb0:
                                   e.matmul(P[6][:], wb0[:, ccx * 128:ccx * 128 + 128],
                                            sb["xn"][:, ccx * S + q0:ccx * S + q0 + QCA],
                                            start=(ccx == 0), stop=(ccx == CC - 1)),
                                   start=(ccx == 0), stop=(ccx == CC - 1), nowait=True)
                    wfree[sl0] = tk
                    dstQ = (QhF if pair == 0 else KhF)[:, (c % 2) * S:(c % 2) * S + S]
                    if a:
                        wt1, sl1 = wtok[(mi0 + 1, c)]
                        wb1 = wsl[sl1]
                        for ccx in range(CC):
                            tk = k.mmw(7, lambda e, ccx=ccx, q0=q0, wb1=wb1:
                                       e.matmul(P[7][:], wb1[:, ccx * 128:ccx * 128 + 128],
                                                sb["xn"][:, ccx * S + q0:ccx * S + q0 + QCA],
                                                start=(ccx == 0), stop=(ccx == CC - 1)),
                                       start=(ccx == 0), stop=(ccx == CC - 1), nowait=True)
                        wfree[sl1] = tk
                        t1 = k.prd("v", 6, lambda e, q0=q0: e.tensor_mul(
                            rp1[:], P[6][:], csA[:, q0:q0 + QCA]),
                            waits=[cst_tok, rope_prev[0]])
                        t2 = k.prd("v", 7, lambda e, q0=q0: e.tensor_mul(
                            rp2[:], P[7][:], csB[:, q0:q0 + QCA]))
                        tq = k.op("v", lambda e, q0=q0, dstQ=dstQ: e.tensor_add(
                            dstQ[:, q0:q0 + QCA], rp1[:], rp2[:]), waits=[t1, t2], inc="vec")
                        rope_prev[0] = tq
                    else:
                        tq = k.prd("s", 6, lambda e, q0=q0, dstQ=dstQ: e.copy(
                            dstQ[:, q0:q0 + QCA], P[6][:]))
                    dst = (qsq_a if pair == 0 else ksq_a)[:, (qc % 2) * QCA:(qc % 2 + 1) * QCA]
                    tsq = k.op("s", lambda e, dst=dst, dstQ=dstQ, q0=q0:
                               e.square(dst, dstQ[:, q0:q0 + QCA]),
                               waits=[tq, prev_norm.get((pair, qc % 2))], inc="act")
                    prev_sq[(pair, qc % 2, c)] = tsq
                    info[("qk", qc, pair)] = tq
                return f

            def u_norm(qc):
                def f():
                    npar = (c % 2) * 16
                    tr = None
                    for pair in range(2):
                        src_a = (qsq_a if pair == 0 else ksq_a)[:, (qc % 2) * QCA:(qc % 2 + 1) * QCA]
                        bnk = 6 + pair
                        k.twait(k.takepr(bnk) + [prev_sq[(pair, qc % 2, c)]])
                        tn = None
                        for hh in range(2):
                            tn = k.mmw(bnk, lambda e, src_a=src_a, hh=hh, bnk=bnk:
                                       e.matmul(P[bnk][32 * hh:32 * hh + 1, :],
                                                sb["ones_bf"][64 * hh:64 * hh + 64, 0:1],
                                                src_a[64 * hh:64 * hh + 64, :],
                                                start=True, stop=True,
                                                tile_position=(64 * hh, 32 * hh)),
                                       nowait=True, noinc=(hh == 0))
                        k.pw[bnk] = tn
                        for hh in range(2):
                            col = npar + 8 * hh + 4 * pair + qc
                            tr = k.prd("v", bnk, lambda e, col=col, hh=hh, bnk=bnk:
                                       e.tensor_reduce(nacc[0:1, col:col + 1],
                                                       P[bnk][32 * hh:32 * hh + 1, :],
                                                       AX.X, ALU.max))
                            prev_norm[(pair, qc % 2)] = tr
                    info[("nrm", qc)] = tr
                return f

            def u_bias():
                def f():
                    npar = (c % 2) * 16
                    bpar = (c % 2) * 8
                    rprev = info[("nrm", NQA - 1)]
                    for hh in range(2):
                        rr1 = k.op("v", lambda e, hh=hh: e.tensor_reduce(
                            nred[0:1, bpar + 3 * hh:bpar + 3 * hh + 1],
                            nacc[0:1, npar + 8 * hh:npar + 8 * hh + 4], AX.X, ALU.max),
                            waits=[rprev], inc="vec")
                        rr2 = k.op("v", lambda e, hh=hh: e.tensor_reduce(
                            nred[0:1, bpar + 3 * hh + 1:bpar + 3 * hh + 2],
                            nacc[0:1, npar + 8 * hh + 4:npar + 8 * hh + 8], AX.X, ALU.max),
                            waits=[rr1], inc="vec")
                        rprev = k.op("v", lambda e, hh=hh: e.tensor_add(
                            nred[0:1, bpar + 3 * hh + 2:bpar + 3 * hh + 3],
                            nred[0:1, bpar + 3 * hh:bpar + 3 * hh + 1],
                            nred[0:1, bpar + 3 * hh + 1:bpar + 3 * hh + 2]),
                            waits=[rr2], inc="vec")
                    rmx = k.op("v", lambda e: e.tensor_tensor(
                        nred[0:1, bpar + 6:bpar + 7], nred[0:1, bpar + 2:bpar + 3],
                        nred[0:1, bpar + 5:bpar + 6], ALU.max), waits=[rprev], inc="vec")
                    rng = k.op("v", lambda e: e.tensor_scalar_mul(
                        nred[0:1, bpar + 7:bpar + 8], nred[0:1, bpar + 6:bpar + 7],
                        -1.0 / 16.0), waits=[rmx], inc="vec")
                    k.twait(k.takepr(7) + [rng])
                    tb = k.mmw(7, lambda e: e.matmul(
                        P[7][:, 0:1], sb["onesf"][0:1, 0:128],
                        nred[0:1, bpar + 7:bpar + 8], start=True, stop=True),
                        nowait=True)
                    tbias = k.prd("s", 7, lambda e: e.copy(
                        bias_sb[:, c % 2:c % 2 + 1], P[7][:, 0:1]))
                    info["bias"] = tbias
                return f

            def u_vt(tt):
                def f():
                    wt_v, slot_v = wtok[(iv, c)]
                    bnk = 6 + tt % 2
                    k.twait(k.takepr(bnk) + [wt_v, old_tok])
                    for ccx in range(CC):
                        k.mmw(bnk, lambda e, tt=tt, ccx=ccx, bnk=bnk:
                              e.matmul(P[bnk][:, 0:128],
                                       sb["xn"][:, ccx * S + tt * 128:ccx * S + tt * 128 + 128],
                                       wsl[slot_v][:, ccx * 128:ccx * 128 + 128],
                                       start=(ccx == 0), stop=(ccx == CC - 1)),
                              start=(ccx == 0), stop=(ccx == CC - 1), nowait=True)
                    dst = VT[:, tt * 130:tt * 130 + 130].rearrange(
                        "p (b c) -> p b c", c=65)[:, :, 0:64]
                    src = P[bnk][:, 0:128].rearrange("p (b c) -> p b c", c=64)
                    ev = k.prd("v", bnk, lambda e, dst=dst, src=src: e.tensor_copy(dst, src),
                               waits=[vz1 if tt == 0 else None, vz2 if tt == 0 else None])
                    info[("vt", tt)] = ev
                    if tt == 15:
                        wfree[slot_v] = ev
                        info["vt_done"] = ev
                        if c + 1 < CC:
                            loadw(iv, c + 1)
                return f

            for qc in range(NQA):
                units.append(u_proj(qc, 0))
                units.append(u_proj(qc, 1))
                units.append(u_norm(qc))
            units.append(u_bias())
            for tt in range(16):
                units.append(u_vt(tt))
            return units

        for f in bg_units(0):
            f()
        bgq = []

        dv_toks = [None, None]
        pending_div = [None]
        for c in range(CC):
            par = c % 2
            Qh = QhF[:, par * S:(par + 1) * S]
            Kh = KhF[:, par * S:(par + 1) * S]
            VT = VTF[:, par * 2080:(par + 1) * 2080]
            info = chunk_info[c]
            qk_done = info[("qk", NQA - 1, 1)]
            tbias = info["bias"]
            vt_done = info["vt_done"]
            bias_ap = bias_sb[:, c % 2:c % 2 + 1]
            if c + 1 < CC:
                if c + 2 < CC:
                    for mi in range(nm):
                        loadw(mi, c + 2)
                bgq = list(bg_units(c + 1))
            else:
                bgq = []

            def run_bg(n=1):
                for _ in range(n):
                    if bgq:
                        bgq.pop(0)()

            econs = {}
            emcons = {}
            for qc in range(NQ):
                q0 = qc * QC
                if a:
                    kts = list(range(16))
                    pv_lo, pv_hi = 0, 4 * qc + 4
                    str_lo = 4 * qc
                else:
                    kts = [t_ for t_ in range(4 * qc - 2, 4 * qc + 4) if t_ >= 0]
                    pv_lo, pv_hi = kts[0], kts[-1] + 1
                    str_lo = -10**9
                nkt = len(kts)
                state = {}
                pv_cnt = [0]
                d_cnt = [0]

                def emit_s_exp(j, qc=qc, q0=q0, kts=kts, str_lo=str_lo,
                               pv_lo=pv_lo, pv_hi=pv_hi, state=state,
                               Qh=Qh, Kh=Kh, qk_done=qk_done, tbias=tbias,
                               bias_ap=bias_ap):
                    kt = kts[j]
                    pb = j % 2
                    w = k.takepr(2 * pb) + k.takepr(2 * pb + 1)
                    if j == 0:
                        w += [qk_done, tbias]
                    k.twait(w)
                    for hh in range(2):
                        k.mmw(2 * pb + hh, lambda e, hh=hh, kt=kt, q0=q0, pb=pb, Qh=Qh, Kh=Kh:
                              e.matmul(P[2 * pb + hh][:, 0:QC],
                                       Kh[64 * hh:64 * hh + 64, kt * 128:kt * 128 + 128],
                                       Qh[64 * hh:64 * hh + 64, q0:q0 + QC],
                                       start=True, stop=True),
                              nowait=True, noinc=(hh == 0))
                    sl4 = (j % 2) * 2 * QC
                    eb2 = ebuf[:, sl4:sl4 + 2 * QC]
                    if QC == QCA:
                        src = PPf[pb][:, 0:1024]
                        dst = eb2
                    else:
                        src = PPf[pb][:].rearrange("p (b c) -> p b c", c=512)[:, :, 0:QC]
                        dst = eb2.rearrange("p (b c) -> p b c", c=QC)
                    texp = k.prd2("s", (2 * pb, 2 * pb + 1),
                                  lambda e, src=src, dst=dst, bias_ap=bias_ap:
                                  e.activation(dst, src, AF.Exp,
                                               bias=bias_ap[:, 0:1], scale=0.125),
                                  waits=[tbias, econs.get(j % 2)])
                    use_mask = ((a and str_lo <= kt < pv_hi) or (not a))
                    pv_use, tmask = eb2, texp
                    if use_mask and pv_lo <= kt < pv_hi:
                        mb2 = embuf[:, sl4:sl4 + 2 * QC]
                        r = (kt - 4 * qc) if a else (kt - 4 * qc + 2)
                        msl = mask[:, r * QC:r * QC + QC]
                        tmask = k.op("v", lambda e, mb2=mb2, eb2=eb2, msl=msl:
                                     e.tensor_tensor(
                                         mb2.rearrange("p (b c) -> p b c", c=QC),
                                         eb2.rearrange("p (b c) -> p b c", c=QC),
                                         msl.rearrange("p (b c) -> p b c", b=1).broadcast_to(
                                             (128, 2, QC)),
                                         ALU.mult),
                                     waits=[texp, cst_tok, emcons.get(j % 2)],
                                     inc="vec")
                        pv_use = mb2
                    state[j] = (texp, tmask, pv_use, eb2)

                def emit_dpv(j, qc=qc, kts=kts, pv_lo=pv_lo, pv_hi=pv_hi,
                             state=state, pv_cnt=pv_cnt, d_cnt=d_cnt,
                             VT=VT, vt_done=vt_done):
                    kt = kts[j]
                    nkt = len(kts)
                    npv = pv_hi - pv_lo
                    first_d = d_cnt[0] == 0
                    last_d = d_cnt[0] == nkt - 1
                    in_pv = pv_lo <= kt < pv_hi
                    first_pv = in_pv and pv_cnt[0] == 0
                    last_pv = in_pv and pv_cnt[0] == npv - 1
                    texp, tmask, pv_use, eb2 = state[j]
                    w = []
                    if first_d:
                        w += k.takepr(4) + [vt_done, zv_tok]
                    if first_pv:
                        w += k.takepr(5)
                    w.append(texp)
                    if in_pv:
                        w.append(tmask)
                    k.twait(w)
                    ebA = eb2[:, 0:QC]
                    ebB = eb2[:, QC:2 * QC]
                    pvA = pv_use[:, 0:QC]
                    pvB = pv_use[:, QC:2 * QC]
                    dsrcA = ebA if a else pvA
                    dsrcB = ebB if a else pvB
                    if not last_d:
                        k.mmw(4, lambda e, dsrcB=dsrcB, st=first_d:
                              e.matmul(P[4][0:66, 0:QC], zv[:, 66:132], dsrcB,
                                       start=st, stop=False),
                              start=first_d, stop=False, nowait=True, noinc=True)
                    if (not a) or (in_pv and pv_use is eb2):
                        fa = k.mmw(4, lambda e, kt=kt, src=(pvA if not a else ebA):
                                   e.matmul(P[4][0:65, 0:QC],
                                            VT[:, kt * 130:kt * 130 + 65],
                                            src, start=False, stop=False),
                                   start=False, stop=False, nowait=True, noinc="force")
                    elif in_pv:
                        k.mmw(4, lambda e, dsrcA=dsrcA:
                              e.matmul(P[4][0:65, 0:QC], zv[:, 0:65], dsrcA,
                                       start=False, stop=False),
                              start=False, stop=False, nowait=True, noinc=True)
                        fa = k.mmw(4, lambda e, kt=kt, pvA=pvA:
                                   e.matmul(P[4][0:64, 0:QC],
                                            VT[:, kt * 130:kt * 130 + 64],
                                            pvA, start=False, stop=False),
                                   start=False, stop=False, nowait=True, noinc="force")
                    else:
                        fa = k.mmw(4, lambda e, dsrcA=dsrcA:
                                   e.matmul(P[4][0:65, 0:QC], zv[:, 0:65], dsrcA,
                                            start=False, stop=False),
                                   start=False, stop=False, nowait=True, noinc="force")
                    if last_d:
                        fa = k.mmw(4, lambda e, dsrcB=dsrcB:
                                   e.matmul(P[4][0:66, 0:QC], zv[:, 66:132], dsrcB,
                                            start=False, stop=True),
                                   start=False, stop=True, nowait=True)
                    d_cnt[0] += 1
                    if in_pv:
                        fpb = k.mmw(5, lambda e, kt=kt, pvB=pvB, st=first_pv, sp=last_pv:
                                    e.matmul(P[5][64:128, 0:QC],
                                             VT[:, kt * 130 + 65:kt * 130 + 129],
                                             pvB, start=st, stop=sp,
                                             tile_position=(0, 64)),
                                    start=first_pv, stop=last_pv, nowait=True)
                        pv_cnt[0] += 1
                        last_tok = fpb
                    else:
                        last_tok = fa
                    if last_d:
                        k.pw[4] = last_tok
                    econs[j % 2] = last_tok
                    if pv_use is not eb2:
                        emcons[j % 2] = last_tok
                    del state[j]

                def make_divide(qc=qc, q0=q0, c=c):
                    def _recipa(e):
                        with nc.allow_low_precision(reason="bf16 inv-denominator, margin-checked"):
                            return e.reciprocal(dinv[64:66, 0:QC], P[4][64:66, 0:QC])
                    dra = k.prd("v", 4, _recipa)
                    drb = dra
                    ne1 = k.prd("v", 4, lambda e: e.tensor_copy(
                        nsb[0:64, 0:QC], P[4][0:64, 0:QC]), waits=[dv_toks[0]])
                    ne2 = k.prd("v", 5, lambda e: e.tensor_copy(
                        nsb[64:128, 0:QC], P[5][64:128, 0:QC]), waits=[dv_toks[1]])

                    def emit_divide():
                        k.twait(k.takepr(4) + k.takepr(5) + [dra, ne1, ne2, selAB_tok])
                        k.mmw(4, lambda e: e.matmul(
                            P[4][:, 0:QC], selAB[64:66, 0:128],
                            dinv[64:66, 0:QC],
                            start=True, stop=True), nowait=True, noinc=True)
                        bcb = k.mmw(5, lambda e: e.matmul(
                            P[5][:, 0:QC], selAB[64:66, 128:256],
                            dinv[64:66, 0:QC],
                            start=True, stop=True), nowait=True)
                        k.pw[4] = bcb
                        dv_toks[0] = k.prd("v", 4, lambda e: e.tensor_mul(
                            aout[0:64, c * S + q0:c * S + q0 + QC], nsb[0:64, 0:QC],
                            P[4][0:64, 0:QC]))
                        dv_toks[1] = k.prd("v", 5, lambda e: e.tensor_mul(
                            aout[64:128, c * S + q0:c * S + q0 + QC], nsb[64:128, 0:QC],
                            P[5][64:128, 0:QC]))
                    return emit_divide

                for j in range(nkt):
                    if j >= LAG:
                        emit_dpv(j - LAG)
                    run_bg(1)
                    emit_s_exp(j)
                    if j == 1 and pending_div[0] is not None:
                        pending_div[0]()
                        pending_div[0] = None
                for j in range(max(0, nkt - LAG), nkt):
                    emit_dpv(j)
                    run_bg(1)
                if pending_div[0] is not None:
                    pending_div[0]()
                    pending_div[0] = None
                pending_div[0] = make_divide()
            run_bg(len(bgq))
            chunk_done_mm[c] = ("mm", k.cnt["mm"])
        pending_div[0]()
        pending_div[0] = None
        last_aout = dv_toks[1]

        for fc in range(CC):
            slot = 20 + fc % 2
            t_wo = k.dma("y", wo_sl[fc % 2][:], scw[mats[nw]][fc], f"aw{slot}",
                         waits=[wfree.get(slot), pre["cast_tok"], last_aout])
            wfree[slot] = t_wo
            for qc in range(NQA):
                q0 = qc * QCA
                pp = 6 + (fc * NQA + qc) % 2
                k.twait(k.takepr(pp) + [t_wo, last_aout])
                for dc in range(CC):
                    k.mmw(pp, lambda e, dc=dc, q0=q0, fc=fc, pp=pp:
                          e.matmul(P[pp][:], wo_sl[fc % 2][:, dc * 128:dc * 128 + 128],
                                   aout[:, dc * S + q0:dc * S + q0 + QCA],
                                   start=(dc == 0), stop=(dc == CC - 1)),
                          start=(dc == 0), stop=(dc == CC - 1), nowait=True)
                tadd = k.prd("v", pp, lambda e, fc=fc, q0=q0, pp=pp: e.tensor_add(
                    sb["x"][:, fc * S + q0:fc * S + q0 + QCA],
                    sb["x"][:, fc * S + q0:fc * S + q0 + QCA], P[pp][:]))
            wfree[slot] = tadd
        return tadd

    def emit_ffn():
        FLV = int(os.environ.get("FFN_LV", "5"))
        esel_sb = ar("esel", (NE, NE * 128), BF16)
        cbias_sb = ar("cbias", (128, 256), F32)
        rw_sb = ar("rw", (128, 128), BF16)
        comb_fm = ar("comb_fm", (NE, S), BF16)
        hh = ar("hh", (128, 48 * QCA), BF16)
        hh_off = int(hh.handle.offset) if hasattr(hh, "handle") else None
        hh_base = A[0] - 48 * QCA * 2
        sco_fm = ar_at("sco_fm", (NE, S), F32, hh_base)
        scT = ar_at("scT", (128, 256), F32, hh_base + 8192)
        sbT = ar_at("sbT", (128, 256), F32, hh_base + 8192 + 1024)
        msk = ar_at("msk", (128, 256), F32, hh_base + 8192 + 2048)
        cnt = ar_at("cnt", (128, 256), F32, hh_base + 8192 + 3072)
        ctm = ar_at("ctm", (128, 256), F32, hh_base + 8192 + 4096)
        rb = hh_base + 8192 + 5120
        gs = ar_at("gs", (128, 64), F32, rb)
        gct = ar_at("gct", (128, 64), F32, rb + 256)
        gtm = ar_at("gtm", (128, 64), F32, rb + 512)
        gm = ar_at("gm", (128, 64), F32, rb + 768)
        s1 = ar_at("s1", (128, 64), F32, rb + 1024)
        s2 = ar_at("s2", (128, 64), F32, rb + 1280)
        s3 = ar_at("s3", (128, 64), F32, rb + 1536)
        wsum = ar_at("wsum", (128, 16), F32, rb + 1792)
        comb_tm = ar_at("comb_tm", (128, 256), F32, rb + 1856)
        silu_t = [ar(f"silu_t{i}", (128, QCA), BF16) for i in range(2)]
        t2b = [ar(f"t2b{i}", (128, QCA), BF16) for i in range(2)]
        sgsu_sl = [ar(f"sgsu{i}", (128, 2048), BF16) for i in range(2)]
        sd_sl = [ar(f"sd{i}", (128, 2048), BF16) for i in range(2)]
        dw_sl = [ar(f"dw{i}", (128, 4096), BF16) for i in range(2)]
        T[0] = top
        eg_sl = [ar(f"eg{i}", (128, 4096), BF16) for i in range(1)]
        eg_sl.append(ar_top("eg1", (128, 4096), BF16))
        ln_guard = ("vec", k.cnt["vec"])

        k.dma("y", esel_sb[:], ap["esel"], "dcst")
        tcs = k.dma("y", cbias_sb[:], ap["cbias"], "dcst")
        trw = k.dma("g", rw_sb[:], ap["rwT"], "gcst")
        cast_done = k.tok("fcst")
        last_xn = k.tok("vec")
        last_ln_act = k.tok("act")

        for qc in range(NQA):
            q0 = qc * QCA
            for ccx in range(CC):
                k.mmw(6, lambda e, ccx=ccx, q0=q0:
                      e.matmul(P[6][0:16, :], rw_sb[:, ccx * 16:ccx * 16 + 16],
                               sb["xn"][:, ccx * S + q0:ccx * S + q0 + QCA],
                               start=(ccx == 0), stop=(ccx == CC - 1)),
                      waits=[trw, last_xn, last_ln_act],
                      start=(ccx == 0), stop=(ccx == CC - 1))
            tsig = k.prd("s", 6, lambda e, q0=q0: e.activation(
                sco_fm[0:16, q0:q0 + QCA], P[6][0:16, :], AF.Sigmoid))
        for tt in range(16):
            k.mmw(7, lambda e, tt=tt: e.transpose(
                P[7][:, tt * 16:tt * 16 + 16], sco_fm[0:16, tt * 128:tt * 128 + 128],
                sb["identf_sb"][0:16, 0:16]),
                waits=[tsig, ident_tok[0]], start=(tt == 0), stop=(tt == 15))
        v0 = k.prd("v", 7, lambda e: e.tensor_copy(scT[:], P[7][:, 0:256]))
        vt = k.op("v", lambda e: e.tensor_add(sbT[:], scT[:], cbias_sb[:]),
                  waits=[v0, tcs], inc="vec")

        def re(apx):
            return apx.rearrange("p (t e) -> p t e", e=16)
        sb4 = sbT[:].rearrange("p (t g j) -> p t g j", g=4, j=4)
        g4 = gs[:].rearrange("p (t g) -> p t g", g=4)
        s1r = s1[:].rearrange("p (t g) -> p t g", g=4)
        s2r = s2[:].rearrange("p (t g) -> p t g", g=4)
        s3r = s3[:].rearrange("p (t g) -> p t g", g=4)
        gmr = gm[:].rearrange("p (t g) -> p t g", g=4)
        gcr = gct[:].rearrange("p (t g) -> p t g", g=4)
        gtr = gtm[:].rearrange("p (t g) -> p t g", g=4)

        def vop(fn):
            nonlocal vt
            vt = k.op("v", fn, waits=[vt], inc="vec")
            return vt
        vop(lambda e: e.tensor_tensor(s1r, sb4[:, :, :, 0], sb4[:, :, :, 1], ALU.max))
        vop(lambda e: e.tensor_tensor(s2r, sb4[:, :, :, 0], sb4[:, :, :, 1], ALU.min))
        vop(lambda e: e.tensor_tensor(s3r, sb4[:, :, :, 2], sb4[:, :, :, 3], ALU.max))
        vop(lambda e: e.tensor_tensor(gmr, sb4[:, :, :, 2], sb4[:, :, :, 3], ALU.min))
        vop(lambda e: e.tensor_tensor(gcr, s1r, s3r, ALU.max))
        vop(lambda e: e.tensor_tensor(s1r, s1r, s3r, ALU.min))
        vop(lambda e: e.tensor_tensor(s2r, s2r, gmr, ALU.max))
        vop(lambda e: e.tensor_tensor(s1r, s1r, s2r, ALU.max))
        vop(lambda e: e.tensor_tensor(g4, gcr, s1r, ALU.add))
        vop(lambda e: e.memset(gct[:], 0.0))
        for dd in range(1, 4):
            Aap = g4[:, :, 0:4 - dd]
            Bap = g4[:, :, dd:4]
            cA = gcr[:, :, 0:4 - dd]
            cB = gcr[:, :, dd:4]
            tmp = gtr[:, :, 0:4 - dd]
            vop(lambda e, A=Aap, B=Bap, t=tmp: e.tensor_tensor(t, A, B, ALU.is_ge))
            vop(lambda e, t=tmp, c2=cB: e.tensor_tensor(c2, c2, t, ALU.add))
            vop(lambda e, A=Aap, B=Bap, t=tmp: e.tensor_tensor(t, B, A, ALU.is_gt))
            vop(lambda e, t=tmp, c2=cA: e.tensor_tensor(c2, c2, t, ALU.add))
        vop(lambda e: e.tensor_single_scalar(gm[:], gct[:], 2.0, ALU.is_lt))
        mr4 = msk[:].rearrange("p (t g j) -> p t g j", g=4, j=4)
        for j in range(4):
            vop(lambda e, j=j: e.scalar_tensor_tensor(
                mr4[:, :, :, j], sb4[:, :, :, j], 4.0, gmr, ALU.add, ALU.mult))
        vop(lambda e: e.memset(cnt[:], 0.0))
        mre = re(msk[:])
        cre = re(cnt[:])
        tre = re(ctm[:])
        for dd in range(1, 16):
            Aap = mre[:, :, 0:16 - dd]
            Bap = mre[:, :, dd:16]
            cA = cre[:, :, 0:16 - dd]
            cB = cre[:, :, dd:16]
            tmp = tre[:, :, 0:16 - dd]
            vop(lambda e, A=Aap, B=Bap, t=tmp: e.tensor_tensor(t, A, B, ALU.is_ge))
            vop(lambda e, t=tmp, c2=cB: e.tensor_tensor(c2, c2, t, ALU.add))
            vop(lambda e, A=Aap, B=Bap, t=tmp: e.tensor_tensor(t, B, A, ALU.is_gt))
            vop(lambda e, t=tmp, c2=cA: e.tensor_tensor(c2, c2, t, ALU.add))
        vop(lambda e: e.tensor_single_scalar(msk[:], cnt[:], 4.0, ALU.is_lt))
        vop(lambda e: e.tensor_mul(ctm[:], scT[:], msk[:]))
        vop(lambda e: e.tensor_reduce(wsum[:], re(ctm[:]), AX.X, ALU.add))
        vop(lambda e: e.tensor_scalar_add(wsum[:], wsum[:], 1e-20))
        vop(lambda e: e.reciprocal(wsum[:], wsum[:]))
        winb = wsum[:].rearrange("p (t e) -> p t e", e=1).broadcast_to((128, 16, 16))
        vop(lambda e: e.tensor_tensor(re(comb_tm[:]), re(ctm[:]), winb, ALU.mult))
        comb_vt = vt
        comb_state = {}

        def emit_comb_back():
            cevac = None
            for b4 in range(4):
                for j4 in range(4):
                    tt = 4 * b4 + j4
                    k.mmw(7, lambda e, tt=tt, j4=j4: e.transpose(
                        P[7][0:16, j4 * 128:j4 * 128 + 128],
                        comb_tm[:, tt * 16:tt * 16 + 16], sb["identf_sb"][:]),
                        waits=[comb_vt], start=(j4 == 0), stop=(j4 == 3))
                cevac = k.prd("s", 7, lambda e, b4=b4: e.copy(
                    comb_fm[0:16, b4 * 512:b4 * 512 + 512], P[7][0:16, 0:512]))
            comb_state["done"] = cevac

        emit_comb_back()

        sl_free = {0: None, 1: None}
        ed_free = {0: None, 1: None}
        dwf = {0: None, 1: None}
        sdf = {0: None, 1: None}
        st_free = {0: None, 1: None}
        t2_free = {0: None, 1: None}
        hh_tok = {}
        down_done = None
        res_tok = None
        for qc in range(NQA):
            q0 = qc * QCA
            if FLV < 2:
                continue
            for hc in range(16):
                sl = hc % 2
                tl = k.dma("y", sgsu_sl[sl][:], sc["sgsu"][hc], f"fs{sl}",
                           waits=[cast_done, sl_free[sl]])
                pp = hc % 2
                for ccx in range(CC):
                    k.mmw(pp, lambda e, ccx=ccx, q0=q0, sl=sl, pp=pp:
                          e.matmul(P[pp][:], sgsu_sl[sl][:, ccx * 128:ccx * 128 + 128],
                                   sb["xn"][:, ccx * S + q0:ccx * S + q0 + QCA],
                                   start=(ccx == 0), stop=(ccx == CC - 1)),
                          waits=[tl, last_xn, last_ln_act],
                          start=(ccx == 0), stop=(ccx == CC - 1))
                for ccx in range(CC):
                    tu = k.mmw(2 + pp, lambda e, ccx=ccx, q0=q0, sl=sl, pp=pp:
                               e.matmul(P[2 + pp][:],
                                        sgsu_sl[sl][:, 1024 + ccx * 128:1024 + ccx * 128 + 128],
                                        sb["xn"][:, ccx * S + q0:ccx * S + q0 + QCA],
                                        start=(ccx == 0), stop=(ccx == CC - 1)),
                               waits=[tl],
                               start=(ccx == 0), stop=(ccx == CC - 1))
                ts = k.prd("s", pp, lambda e, pp=pp: e.activation(
                    silu_t[pp][:], P[pp][:], AF.Silu), waits=[st_free[pp]])
                th = k.prd("v", 2 + pp, lambda e, hc=hc, pp=pp: e.tensor_mul(
                    hh[:, hc * QCA:hc * QCA + QCA], silu_t[pp][:], P[2 + pp][:]),
                    waits=[ts, down_done])
                st_free[pp] = th
                hh_tok[hc] = th
                sl_free[sl] = tu
            comb_done = comb_state["done"]
            for ee in range(NE):
                if FLV < 3:
                    break
                sl = ee % 2
                tl = k.dma("y", eg_sl[sl][:], sc["expgu"][ee], f"fe{sl}",
                           waits=[cast_done, ed_free[sl], ln_guard if sl == 1 else None])
                tcb = k.mmw(4, lambda e, ee=ee, q0=q0: e.matmul(
                    P[4][:], esel_sb[:, ee * 128:ee * 128 + 128],
                    comb_fm[0:16, q0:q0 + QCA], start=True, stop=True),
                    waits=[comb_done])
                for h2 in range(2):
                    pp = h2
                    for ccx in range(CC):
                        k.mmw(pp, lambda e, ccx=ccx, q0=q0, sl=sl, h2=h2, pp=pp:
                              e.matmul(P[pp][:],
                                       eg_sl[sl][:, (h2 * 8 + ccx) * 128:(h2 * 8 + ccx) * 128 + 128],
                                       sb["xn"][:, ccx * S + q0:ccx * S + q0 + QCA],
                                       start=(ccx == 0), stop=(ccx == CC - 1)),
                              waits=[tl], start=(ccx == 0), stop=(ccx == CC - 1))
                    for ccx in range(CC):
                        tu = k.mmw(2 + pp, lambda e, ccx=ccx, q0=q0, sl=sl, h2=h2, pp=pp:
                                   e.matmul(P[2 + pp][:],
                                            eg_sl[sl][:, 2048 + (h2 * 8 + ccx) * 128:2048 + (h2 * 8 + ccx) * 128 + 128],
                                            sb["xn"][:, ccx * S + q0:ccx * S + q0 + QCA],
                                            start=(ccx == 0), stop=(ccx == CC - 1)),
                                   waits=[tl], start=(ccx == 0), stop=(ccx == CC - 1))
                    ts = k.prd("s", pp, lambda e, pp=pp: e.activation(
                        silu_t[pp][:], P[pp][:], AF.Silu), waits=[st_free[pp]])
                    tm = k.prd("v", 2 + pp, lambda e, pp=pp: e.tensor_mul(
                        t2b[pp][:], silu_t[pp][:], P[2 + pp][:]),
                        waits=[ts, t2_free[pp]])
                    st_free[pp] = tm
                    th = k.prd("v", 4, lambda e, ee=ee, h2=h2, pp=pp: e.tensor_mul(
                        hh[:, (16 + 2 * ee + h2) * QCA:(16 + 2 * ee + h2 + 1) * QCA],
                        t2b[pp][:], P[4][:]), waits=[tm, down_done])
                    t2_free[pp] = th
                    hh_tok[16 + 2 * ee + h2] = th
                ed_free[sl] = tu
            for ccx in range(CC):
                if FLV < 4:
                    break
                sl = ccx % 2
                tsd = k.dma("y", sd_sl[sl][:], sc["sd"][ccx], f"fd{sl}",
                            waits=[cast_done, sdf[sl]])
                tdw = k.dma("y", dw_sl[sl][:], sc["dwcc"][ccx], f"fw{sl}",
                            waits=[cast_done, dwf[sl]])
                pp = 5 + ccx % 2
                shstop = FLV < 5
                for hc in range(16):
                    tk = k.mmw(pp, lambda e, hc=hc, sl=sl, pp=pp,
                               sp=(shstop and hc == 15):
                               e.matmul(P[pp][:], sd_sl[sl][:, hc * 128:hc * 128 + 128],
                                        hh[:, hc * QCA:hc * QCA + QCA],
                                        start=(hc == 0), stop=sp),
                               waits=[tsd, hh_tok[hc]],
                               start=(hc == 0), stop=(shstop and hc == 15))
                for ee in range(NE):
                    if FLV < 5:
                        break
                    for h2 in range(2):
                        idx = 2 * ee + h2
                        tk = k.mmw(pp, lambda e, idx=idx, sl=sl, pp=pp, sp=(idx == 31):
                                   e.matmul(P[pp][:],
                                            dw_sl[sl][:, idx * 128:idx * 128 + 128],
                                            hh[:, (16 + idx) * QCA:(16 + idx + 1) * QCA],
                                            start=False, stop=sp),
                                   waits=[tdw, hh_tok[16 + idx]],
                                   start=False, stop=(idx == 31))
                def _res(e, ccx=ccx, q0=q0, pp=pp, qc=qc):
                    inst = e.tensor_add(
                        sb["x"][:, ccx * S + q0:ccx * S + q0 + QCA],
                        sb["x"][:, ccx * S + q0:ccx * S + q0 + QCA], P[pp][:])
                    inst.annotate(f"res qc{qc} ccx{ccx} pp{pp}")
                    return inst
                res_tok = k.prd("v", pp, _res)
                sdf[sl] = tk
                dwf[sl] = tk
                down_done = res_tok
        return res_tok

    def emit_store():
        for ccx in range(CC):
            k.dma("y", yout[ccx], sb["x"][:, ccx * S:(ccx + 1) * S], "dst")
        k.op("y", lambda e: None, waits=[("dst", 16 * CC)])

    emit_load()
    k.barrier()
    i = 0
    while i < len(stages):
        st = stages[i]
        ar_reset()
        if st.startswith("ln"):
            nxt = stages[i + 1] if i + 1 < len(stages) else None
            pre = None
            if nxt in ("attn_a", "attn_b"):
                pre = attn_alloc_preload("a" if nxt == "attn_a" else "b")
            emit_ln()
            if nxt in ("attn_a", "attn_b"):
                emit_attn(pre)
                i += 1
            elif nxt is not None and nxt.startswith("ffn"):
                emit_ffn()
                i += 1
        k.barrier()
        i += 1
    emit_store()

    for name in list(k.cnt):
        k.sems[name] = nc.alloc_semaphore(name)

    with nc.Block() as block:
        @block.tensor
        def _(e):
            for f in k.prog["t"]:
                f(e)

        @block.vector
        def _(e):
            for f in k.prog["v"]:
                f(e)

        @block.scalar
        def _(e):
            for f in k.prog["s"]:
                f(e)

        @block.gpsimd
        def _(e):
            for f in k.prog["g"]:
                f(e)

        @block.sync
        def _(e):
            for f in k.prog["y"]:
                f(e)
    return nc


def kernel(**inputs):
    return _run(inputs, list(ALL_STAGES))


def _run(inputs, stages, cores=None):
    hp = host_prep(inputs)
    x = np.asarray(inputs["L0_attn_a_x"])
    B = x.shape[0]
    if cores is None:
        cores = list(range(B))
    nc = build_nc(tuple(stages))
    in_maps = []
    for b in cores:
        m = dict(hp)
        m["xin"] = np.ascontiguousarray(x[b].T.reshape(CC, 128, S))
        in_maps.append(m)
    res = run_bass_kernel_spmd(nc, in_maps, core_ids=list(range(len(cores))))
    outs = []
    for r in res.results:
        y = np.asarray(r["y"]).reshape(H, S)
        outs.append(np.ascontiguousarray(y.T))
    return np.stack(outs, axis=0)


# version 11
# speedup vs baseline: 1.1602x; 1.0255x over previous
import os
import numpy as np
import ml_dtypes
import concourse.bass as bass
import concourse.mybir as mybir
from concourse.bass_utils import run_bass_kernel_spmd

F32 = mybir.dt.float32
BF16 = mybir.dt.bfloat16
AF = mybir.ActivationFunctionType
ALU = mybir.AluOpType
AX = mybir.AxisListType
BF = ml_dtypes.bfloat16

S = 2048
H = 1024
CC = 8
NH = 16
DH = 64
QCA = 512
NQA = 4
QCB = 512
NQB = 4
NE = 16
WINDOW = 256

ENGS = ("t", "v", "s", "g", "y")
ALL_STAGES = ("ln1", "attn_a", "ln2", "ffn1", "ln3", "attn_b", "ln4", "ffn2")


class Rec:

    def __init__(self, nc):
        self.nc = nc
        self.prog = {e: [] for e in ENGS}
        self.cnt = {}
        self.lastwait = {e: {} for e in ENGS}
        self.sems = {}
        self.owner = {}
        self.nbar = 0
        self.pw = [None] * 8
        self.pr = [[] for _ in range(8)]

    def sem(self, name, owner=None):
        self.cnt[name] = 0
        self.owner[name] = owner

    def tok(self, name):
        return (name, self.cnt[name])

    def op(self, eng, fn, waits=(), inc=None, n=1):
        lw = self.lastwait[eng]
        real = []
        for w in waits:
            if w is None:
                continue
            s_, v = w
            if v > 0 and lw.get(s_, 0) < v:
                real.append((s_, v))
                lw[s_] = v
        out_tok = None
        if inc is not None:
            self.cnt[inc] += n
            out_tok = (inc, self.cnt[inc])
        sems = self.sems

        def play(e, real=real, fn=fn, inc=inc, n=n):
            for s_, v in real:
                e.wait_ge(sems[s_], v)
            inst = fn(e)
            if inc is not None:
                if inst is None:
                    e.nop().then_inc(sems[inc], n)
                else:
                    inst.then_inc(sems[inc], n)
        self.prog[eng].append(play)
        return out_tok

    def dma(self, eng, out, in_, sem, waits=()):
        return self.op(eng, lambda e, o=out, i=in_: e.dma_start(out=o, in_=i),
                       waits=waits, inc=sem, n=16)

    def mmw(self, bank, fn, waits=(), start=True, stop=True, nowait=False, noinc=False):
        ws = list(waits)
        if start and not nowait:
            ws += self.pr[bank]
        do_inc = (stop and not noinc) or noinc == "force"
        tok = self.op("t", fn, ws, inc="mm" if do_inc else None)
        if start:
            self.pr[bank] = []
        if stop:
            self.pw[bank] = tok
        return tok

    def takepr(self, bank):
        r = self.pr[bank]
        self.pr[bank] = []
        return list(r)

    def twait(self, waits):
        self.op("t", lambda e: None, waits=waits)

    def prd(self, eng, bank, fn, waits=(), inc=None):
        inc = inc or {"v": "vec", "s": "act"}[eng]
        tok = self.op(eng, fn, list(waits) + [self.pw[bank]], inc=inc)
        self.pr[bank].append(tok)
        return tok

    def prd2(self, eng, banks, fn, waits=(), inc=None):
        inc = inc or {"v": "vec", "s": "act"}[eng]
        tok = self.op(eng, fn, list(waits) + [self.pw[b] for b in banks], inc=inc)
        for b in banks:
            self.pr[b].append(tok)
        return tok

    def barrier(self):
        self.nbar += 1
        target = 5 * self.nbar
        for eng in ENGS:
            own = [(n, self.cnt[n]) for n, o in self.owner.items()
                   if o == eng and self.cnt[n] > 0]
            self.op(eng, lambda e: None, waits=own, inc="ph")
        for eng in ENGS:
            self.op(eng, lambda e: None, waits=[("ph", target)])
        return ("ph", target)


def _blocked(wt):
    c, f = wt.shape
    return np.ascontiguousarray(
        wt.reshape(c // 128, 128, f // 128, 128).transpose(2, 1, 0, 3))


def host_prep(ins):
    d = {}
    pm = np.zeros(H, np.int64)
    pr = np.zeros(H, np.int64)
    for h in range(NH):
        for j in range(DH):
            if j < 32:
                pm[64 * h + j] = 64 * h + 2 * j
                pr[64 * h + j] = 64 * h + 2 * j + 1
            else:
                pm[64 * h + j] = 64 * h + 2 * (j - 32) + 1
                pr[64 * h + j] = 64 * h + 2 * (j - 32)
    wq, wk = np.asarray(ins["L0_attn_a_w_q"]), np.asarray(ins["L0_attn_a_w_k"])
    d["aqm"] = _blocked(np.ascontiguousarray(wq[pm].T))
    d["aqr"] = _blocked(np.ascontiguousarray(wq[pr].T))
    d["akm"] = _blocked(np.ascontiguousarray(wk[pm].T))
    d["akr"] = _blocked(np.ascontiguousarray(wk[pr].T))
    d["avT"] = _blocked(np.ascontiguousarray(np.asarray(ins["L0_attn_a_w_v"]).T))
    d["aoT"] = _blocked(np.ascontiguousarray(np.asarray(ins["L0_attn_a_w_o"]).T))
    d["bqT"] = _blocked(np.ascontiguousarray(np.asarray(ins["L0_attn_b_Wq"]).T))
    d["bkT"] = _blocked(np.ascontiguousarray(np.asarray(ins["L0_attn_b_Wk"]).T))
    d["bvT"] = _blocked(np.ascontiguousarray(np.asarray(ins["L0_attn_b_Wv"]).T))
    d["boT"] = _blocked(np.ascontiguousarray(np.asarray(ins["L0_attn_b_Wo"]).T))
    sgb = _blocked(np.ascontiguousarray(np.asarray(ins["L0_ffn_shared_gate_w"]).T))
    sub = _blocked(np.ascontiguousarray(np.asarray(ins["L0_ffn_shared_up_w"]).T))
    d["sgsu"] = np.ascontiguousarray(
        np.concatenate([sgb.reshape(16, 128, 1024), sub.reshape(16, 128, 1024)], axis=2))
    sdw = np.ascontiguousarray(np.asarray(ins["L0_ffn_shared_down_w"]).T)
    d["sd"] = np.ascontiguousarray(
        sdw.reshape(16, 128, 8, 128).transpose(2, 1, 0, 3).reshape(8, 128, 2048))
    gw = np.asarray(ins["L0_ffn_gate_w"])
    uw = np.asarray(ins["L0_ffn_up_w"])
    dw = np.asarray(ins["L0_ffn_down_w"])

    def gu_blk(w):
        return np.ascontiguousarray(
            w.reshape(NE, 8, 128, 2, 128).transpose(0, 2, 3, 1, 4).reshape(NE, 128, 2048))
    d["expgu"] = np.ascontiguousarray(
        np.concatenate([gu_blk(gw), gu_blk(uw)], axis=2))
    d["dwcc"] = np.ascontiguousarray(
        dw.reshape(NE, 2, 128, 8, 128).transpose(3, 2, 0, 1, 4).reshape(8, 128, 4096))
    d["rwT"] = np.ascontiguousarray(
        np.asarray(ins["L0_ffn_router_w"]).T.reshape(8, 128, 16).transpose(1, 0, 2).reshape(128, 128))
    d["cbias"] = np.ascontiguousarray(
        np.tile(np.asarray(ins["L0_ffn_correction_bias"]).astype(np.float32), (128, NE)))
    half = DH // 2
    inv = 10000.0 ** (-(np.arange(half, dtype=np.float64) * 2.0 / DH))
    t = np.arange(S, dtype=np.float64)
    ang = t[None, :] * inv[:, None]
    cos = np.cos(ang).astype(np.float32)
    sin = np.sin(ang).astype(np.float32)
    csA = np.empty((128, S), BF)
    csB = np.empty((128, S), BF)
    for b in range(4):
        csA[32 * b:32 * b + 32] = cos
        csB[32 * b:32 * b + 32] = (-sin) if (b % 2 == 0) else sin
    d["csA"], d["csB"] = csA, csB
    kk = np.arange(128)[:, None]
    qq = np.arange(QCA)[None, :]
    ma = np.zeros((128, 4 * QCA), BF)
    for r in range(4):
        ma[:, r * QCA:(r + 1) * QCA] = (qq >= kk + 128 * r).astype(BF)
    d["maska"] = ma
    qq2 = np.arange(QCB)[None, :]
    mb = np.zeros((128, 6 * QCB), BF)
    for i, r in enumerate((-2, -1, 0, 1, 2, 3)):
        off = qq2 - kk - 128 * r
        mb[:, i * QCB:(i + 1) * QCB] = ((off >= 0) & (off <= WINDOW)).astype(BF)
    d["maskb"] = mb
    d["identf"] = np.eye(128, dtype=np.float32)
    selab = np.zeros((128, 256), BF)
    selab[64, 0:128] = 1.0
    selab[65, 128:256] = 1.0
    d["selab"] = selab
    esel = np.zeros((NE, NE * 128), BF)
    for e in range(NE):
        esel[e, 128 * e:128 * e + 128] = 1.0
    d["esel"] = esel
    return d


def build_nc(stages):
    nc = bass.Bass("TRN2", target_bir_lowering=False)
    ap = {}

    def din(name, shape, dtype=F32):
        ap[name] = nc.dram_tensor(name, list(shape), dtype, kind="ExternalInput").ap()

    din("xin", (CC, 128, S))
    for n in ("aqm", "aqr", "akm", "akr", "avT", "aoT", "bqT", "bkT", "bvT", "boT"):
        din(n, (8, 128, 8, 128))
    din("sgsu", (16, 128, 2048))
    din("sd", (8, 128, 2048))
    din("expgu", (NE, 128, 4096))
    din("dwcc", (8, 128, 4096))
    din("rwT", (128, 128))
    din("cbias", (128, 256))
    din("csA", (128, S), BF16)
    din("csB", (128, S), BF16)
    din("maska", (128, 4 * QCA), BF16)
    din("maskb", (128, 6 * QCB), BF16)
    din("identf", (128, 128))
    din("esel", (NE, NE * 128), BF16)
    din("selab", (128, 256), BF16)
    yout = nc.dram_tensor("y", [CC, 128, S], F32, kind="ExternalOutput").ap()

    sc = {}
    scw = {}
    need_attn = any(st.startswith("attn") for st in stages)
    if need_attn:
        for n in ("aqm", "aqr", "akm", "akr", "avT", "aoT", "bqT", "bkT", "bvT", "boT"):
            scw[n] = nc.dram_tensor(f"{n}_bf", [8, 128, 1024], BF16, kind="Internal").ap()
    need_ffn = any(st.startswith("ffn") for st in stages)
    if need_ffn:
        sc["sgsu"] = nc.dram_tensor("sgsu_bf", [16, 128, 2048], BF16, kind="Internal").ap()
        sc["sd"] = nc.dram_tensor("sd_bf", [8, 128, 2048], BF16, kind="Internal").ap()
        sc["expgu"] = nc.dram_tensor("expgu_bf", [NE, 128, 4096], BF16, kind="Internal").ap()
        sc["dwcc"] = nc.dram_tensor("dwcc_bf", [8, 128, 4096], BF16, kind="Internal").ap()

    sb = {}

    def sbt(name, shape, dtype, off=None):
        if off is None:
            sb[name] = nc.alloc_sbuf_tensor(name, list(shape), dtype)
        else:
            sb[name] = nc.alloc_sbuf_tensor_at(name, list(shape), dtype, offset=off)
        return sb[name]

    sbt("x", (128, CC * S), F32)
    sbt("xn", (128, CC * S), BF16)
    sbt("ones_bf", (128, 128), BF16)
    sbt("eps_sb", (128, 1), F32)
    sbt("onesf", (128, 128), F32)
    sbt("identf_sb", (128, 128), F32)
    arena = nc.sbuf_base
    top = nc.sbuf_top
    A = [arena]
    ph_id = [0]

    def ar(name, shape, dtype, align=64):
        nbytes = int(np.prod(shape[1:])) * (2 if dtype == BF16 else 4)
        off = (A[0] + align - 1) // align * align
        assert off + nbytes <= top, f"SBUF arena overflow at {name}: {off+nbytes-top} over"
        A[0] = off + nbytes
        return sbt(f"{name}_{ph_id[0]}", shape, dtype, off=off)

    def ar_at(name, shape, dtype, off):
        return sbt(f"{name}_{ph_id[0]}", shape, dtype, off=off)

    T = [top]

    ar_offs = {}

    def ar_top(name, shape, dtype, align=64):
        nbytes = int(np.prod(shape[1:])) * (2 if dtype == BF16 else 4)
        off = (T[0] - nbytes) // align * align
        assert off >= A[0], (
            f"SBUF top-arena overflow at {name}: off={off} A={A[0]} short={A[0]-off}")
        T[0] = off
        ar_offs[name] = off
        return sbt(f"{name}_{ph_id[0]}", shape, dtype, off=off)

    def ar_reset():
        A[0] = arena
        T[0] = top
        ph_id[0] += 1

    PPf = [nc.alloc_psum_tensor("PPa", [128, 1024], F32),
           nc.alloc_psum_tensor("PPb", [128, 1024], F32)]
    P = [PPf[0][:, 0:512], PPf[0][:, 512:1024],
         PPf[1][:, 0:512], PPf[1][:, 512:1024]]
    P += [nc.alloc_psum_tensor(f"P{i}", [128, 512], F32)[:] for i in range(4, 8)]

    k = Rec(nc)
    k.sem("mm", "t")
    k.sem("act", "s")
    k.sem("vec", "v")
    k.sem("ph", None)
    for s_ in ("dxin", "dcst", "dst", "fs0", "fs1", "fd0", "fd1", "fe0", "fe1",
               "fw0", "fw1"):
        k.sem(s_, "y")
    for s_ in ("gcst", "fcst", "awcast"):
        k.sem(s_, "g")
    for i in range(12):
        k.sem(f"aw{i}", "y")
    for i in (20, 21):
        k.sem(f"aw{i}", "y")

    ident_tok = [None]

    def emit_load():
        for ccx in range(CC):
            k.dma("y", sb["x"][:, ccx * S:(ccx + 1) * S], ap["xin"][ccx], "dxin")
        ident_tok[0] = k.dma("y", sb["identf_sb"][:], ap["identf"], "dcst")
        k.op("v", lambda e: e.memset(sb["ones_bf"][:], 1.0), inc="vec")
        k.op("v", lambda e: e.memset(sb["onesf"][:], 1.0), inc="vec")
        k.op("v", lambda e: e.memset(sb["eps_sb"][:], 1e-5), inc="vec")
        if need_attn:
            for n in scw:
                k.dma("g", scw[n], ap[n].rearrange("a p c j -> a p (c j)"), "awcast")
        if need_ffn:
            k.dma("g", sc["sgsu"], ap["sgsu"], "fcst")
            k.dma("g", sc["sd"], ap["sd"], "fcst")
            for e2 in range(4):
                k.dma("g", sc["expgu"][4 * e2:4 * e2 + 4], ap["expgu"][4 * e2:4 * e2 + 4], "fcst")
            k.dma("g", sc["dwcc"], ap["dwcc"], "fcst")

    def emit_ln():
        lnsq = ar_top("lnsq", (128, 2 * QCA), BF16)
        m1p = ar_top("m1", (128, 2 * QCA), F32)
        m2p = ar_top("m2", (128, 2 * QCA), F32)
        m3p = ar_top("m3", (128, 2 * QCA), F32)
        m4 = ar_top("m4", (128, QCA), F32)
        xin_tok = k.tok("dxin")
        prev_last = {0: None, 1: None}
        sq_war = {0: None, 1: None}
        mm_war = {0: None, 1: None}
        for qc in range(NQA):
            q0 = qc * QCA
            par = qc % 2
            m1 = m1p[:, par * QCA:(par + 1) * QCA]
            m2 = m2p[:, par * QCA:(par + 1) * QCA]
            m3 = m3p[:, par * QCA:(par + 1) * QCA]
            pb0, pb1 = 2 * par, 2 * par + 1
            for ccx in range(CC):
                slot = ccx % 2
                war = mm_war.get(slot)
                xnc = sb["xn"][:, ccx * S + q0:ccx * S + q0 + QCA]
                tb = k.op("s", lambda e, ccx=ccx, q0=q0, xnc=xnc:
                          e.copy(xnc, sb["x"][:, ccx * S + q0:ccx * S + q0 + QCA]),
                          waits=[xin_tok], inc="act")
                tq = k.op("v", lambda e, slot=slot, xnc=xnc:
                          e.tensor_mul(lnsq[:, slot * QCA:(slot + 1) * QCA],
                                       xnc, xnc),
                          waits=[tb, war], inc="vec")
                sq_war[slot] = tq
                k.mmw(pb0, lambda e, xnc=xnc, ccx=ccx, pb0=pb0:
                      e.matmul(P[pb0][:], sb["ones_bf"][:], xnc,
                               start=(ccx == 0), stop=(ccx == CC - 1)),
                      waits=[tb], start=(ccx == 0), stop=(ccx == CC - 1))
                mm_war[slot] = k.mmw(pb1, lambda e, slot=slot, ccx=ccx, pb1=pb1:
                      e.matmul(P[pb1][:], sb["ones_bf"][:],
                               lnsq[:, slot * QCA:(slot + 1) * QCA],
                               start=(ccx == 0), stop=(ccx == CC - 1)),
                      waits=[tq], start=(ccx == 0), stop=(ccx == CC - 1),
                      noinc="force")
            pl = prev_last.get(par)
            t1 = k.prd("s", pb0, lambda e, m1=m1, pb0=pb0: e.mul(m1, P[pb0][:], 1.0 / H),
                       waits=[pl])
            t2 = k.prd("s", pb1, lambda e, m2=m2, pb1=pb1: e.mul(m2, P[pb1][:], 1.0 / H),
                       waits=[pl])
            t3 = k.op("v", lambda e, m3=m3, m1=m1: e.tensor_mul(m3, m1, m1),
                      waits=[t1, pl], inc="vec")
            t4 = k.op("v", lambda e, m2=m2, m3=m3: e.tensor_sub(m2, m2, m3),
                      waits=[t2, t3], inc="vec")
            t5 = k.op("s", lambda e, m3=m3, m2=m2: e.activation(
                m3, m2, AF.Ln, bias=sb["eps_sb"][:, 0:1]), waits=[t4], inc="act")
            t6 = k.op("s", lambda e, m2=m2, m3=m3: e.activation(
                m2, m3, AF.Exp, scale=-0.5), waits=[t5], inc="act")
            t7 = k.op("v", lambda e, m3=m3, m1=m1, m2=m2: e.tensor_mul(m3, m1, m2),
                      waits=[t6], inc="vec")
            lastc = pl
            m4_guard = k.tok("vec")
            for ccx in range(CC):
                t8 = k.op("v", lambda e, ccx=ccx, q0=q0, m2=m2:
                          e.tensor_mul(m4[:], sb["x"][:, ccx * S + q0:ccx * S + q0 + QCA], m2),
                          waits=[t6, lastc, m4_guard], inc="vec")
                lastc = k.op("v", lambda e, ccx=ccx, q0=q0, m3=m3:
                             e.tensor_sub(sb["xn"][:, ccx * S + q0:ccx * S + q0 + QCA],
                                          m4[:], m3),
                             waits=[t7, t8], inc="vec")
            prev_last[par] = lastc
            last = lastc
        return last

    def attn_alloc_preload(mode):
        a = mode == "a"
        pre = {"a": a}
        pre["csA"] = ar("csA", (128, S), BF16) if a else None
        pre["csB"] = ar("csB", (128, S), BF16) if a else None
        pre["mask"] = ar("maska", (128, 4 * QCA), BF16) if a else ar("maskb", (128, 6 * QCB), BF16)
        pre["Qh"] = ar("Qh", (128, 2 * S), BF16)
        pre["Kh"] = ar("Kh", (128, 2 * S), BF16)
        pre["VT"] = ar("VT", (128, 2 * 16 * 130), BF16)
        pre["zv"] = ar("zv", (128, 132), BF16)
        pre["selAB"] = ar("selAB", (128, 256), BF16)
        pre["aout"] = ar("aout", (128, CC * S), BF16)
        nsl = 6 if a else 4
        nqk = nsl - 2
        pre["nqk"] = nqk
        pre["wsl"] = [ar(f"w{i}", (128, 1024), BF16) for i in range(2 * nqk + 1)]
        pre["qsq"] = ar("qsq", (128, 2 * QCA), BF16)
        pre["ksq"] = ar("ksq", (128, 2 * QCA), BF16)
        pre["nacc"] = ar("nacc", (1, 32), F32)
        pre["nred"] = ar("nred", (1, 16), F32)
        pre["bias_sb"] = ar("bias_sb", (128, 2), F32)
        if a:
            k.dma("y", pre["csA"][:], ap["csA"], "dcst")
            k.dma("y", pre["csB"][:], ap["csB"], "dcst")
            k.dma("y", pre["mask"][:], ap["maska"], "dcst")
        else:
            k.dma("y", pre["mask"][:], ap["maskb"], "dcst")
        tok = k.dma("y", pre["selAB"][:], ap["selab"], "dcst")
        pre["cst_tok"] = tok
        pre["selAB_tok"] = tok
        mats = (["aqm", "aqr", "akm", "akr", "avT", "aoT"] if a
                else ["bqT", "bkT", "bvT", "boT"])
        pre["mats"] = mats
        nm = 4 if a else 2
        pre["nm"] = nm
        cast_tok = ("awcast", k.cnt["awcast"])
        pre["cast_tok"] = cast_tok
        wtok = {}
        wfree = {}
        pre["wtok"] = wtok
        pre["wfree"] = wfree

        def loadw(mi, c):
            slot = (c % 2) * nqk + mi if mi < nm else 2 * nqk
            t = k.dma("y", pre["wsl"][slot][:], scw[mats[mi]][c], f"aw{slot}",
                      waits=[wfree.get(slot), cast_tok])
            wtok[(mi, c)] = (t, slot)
        pre["loadw"] = loadw
        for mi in range(nm + 1):
            loadw(mi, 0)
        for mi in range(nm):
            loadw(mi, 1)
        return pre

    def emit_attn(pre):
        a = pre["a"]
        QC = QCA if a else QCB
        NQ = NQA if a else NQB
        LAG = 2
        T[0] = top
        ebuf = ar_top("ebuf", (128, 4 * QC), BF16)
        embuf = ar_top("embuf", (128, 4 * QC), BF16)
        nsb = ar_top("nsb", (128, QC), F32)
        dinv = ar_top("dinv", (66, QC), BF16)
        wo_sl = [ar_at("wo0", (128, 1024), BF16, ar_offs["dinv"]),
                 ar_at("wo1", (128, 1024), BF16, ar_offs["dinv"] + 2048)]
        pre["wo_sl"] = wo_sl
        pre["rp1"] = ar_top("rp1", (128, QCA), F32)
        pre["rp2"] = ar_top("rp2", (128, QCA), F32)
        selAB = pre["selAB"]
        selAB_tok = pre["selAB_tok"]
        csA, csB, mask = pre["csA"], pre["csB"], pre["mask"]
        QhF, KhF, VTF = pre["Qh"], pre["Kh"], pre["VT"]
        zv = pre["zv"]
        aout = pre["aout"]
        wsl, wo_sl = pre["wsl"], pre["wo_sl"]
        rp1, rp2 = pre["rp1"], pre["rp2"]
        qsq_a, ksq_a = pre["qsq"], pre["ksq"]
        nacc, nred = pre["nacc"], pre["nred"]
        bias_sb = pre["bias_sb"]
        cst_tok = pre["cst_tok"]
        mats, nm, nqk = pre["mats"], pre["nm"], pre["nqk"]
        nw = len(mats) - 1
        iv = nm
        wtok, wfree, loadw = pre["wtok"], pre["wfree"], pre["loadw"]
        last_xn = k.tok("vec")
        last_ln_act = k.tok("act")
        vz1 = k.op("v", lambda e: e.memset(
            VTF[:].rearrange("p (t c) -> p t c", c=130)[:, :, 64:65], 1.0), inc="vec")
        vz2 = k.op("v", lambda e: e.memset(
            VTF[:].rearrange("p (t c) -> p t c", c=130)[:, :, 129:130], 1.0), inc="vec")
        vz3 = k.op("v", lambda e: e.memset(zv[:, 0:64], 0.0), inc="vec")
        vz4 = k.op("v", lambda e: e.memset(zv[:, 64:65], 1.0), inc="vec")
        vz5 = k.op("v", lambda e: e.memset(zv[:, 65:131], 0.0), inc="vec")
        vz6 = k.op("v", lambda e: e.memset(zv[:, 131:132], 1.0), inc="vec")
        zv_tok = vz6

        chunk_info = {}
        prev_sq = {}
        prev_norm = {}
        rope_prev = [None]
        chunk_done_mm = {-1: None, -2: None}

        def bg_units(c):
            par = c % 2
            Qh = QhF[:, par * S:(par + 1) * S]
            Kh = KhF[:, par * S:(par + 1) * S]
            VT = VTF[:, par * 2080:(par + 1) * 2080]
            units = []
            info = {}
            chunk_info[c] = info
            old_tok = chunk_done_mm.get(c - 2)

            def u_proj(qc, pair):
                def f():
                    q0 = qc * QCA
                    mi0 = pair * (2 if a else 1)
                    wt0, sl0 = wtok[(mi0, c)]
                    w = k.takepr(6) + [wt0, last_xn, last_ln_act, old_tok]
                    if a:
                        w += k.takepr(7) + [wtok[(mi0 + 1, c)][0]]
                    k.twait(w)
                    wb0 = wsl[sl0]
                    for ccx in range(CC):
                        tk = k.mmw(6, lambda e, ccx=ccx, q0=q0, wb0=wb0:
                                   e.matmul(P[6][:], wb0[:, ccx * 128:ccx * 128 + 128],
                                            sb["xn"][:, ccx * S + q0:ccx * S + q0 + QCA],
                                            start=(ccx == 0), stop=(ccx == CC - 1)),
                                   start=(ccx == 0), stop=(ccx == CC - 1), nowait=True)
                    wfree[sl0] = tk
                    dstQ = (QhF if pair == 0 else KhF)[:, (c % 2) * S:(c % 2) * S + S]
                    if a:
                        wt1, sl1 = wtok[(mi0 + 1, c)]
                        wb1 = wsl[sl1]
                        for ccx in range(CC):
                            tk = k.mmw(7, lambda e, ccx=ccx, q0=q0, wb1=wb1:
                                       e.matmul(P[7][:], wb1[:, ccx * 128:ccx * 128 + 128],
                                                sb["xn"][:, ccx * S + q0:ccx * S + q0 + QCA],
                                                start=(ccx == 0), stop=(ccx == CC - 1)),
                                       start=(ccx == 0), stop=(ccx == CC - 1), nowait=True)
                        wfree[sl1] = tk
                        t1 = k.prd("v", 6, lambda e, q0=q0: e.tensor_mul(
                            rp1[:], P[6][:], csA[:, q0:q0 + QCA]),
                            waits=[cst_tok, rope_prev[0]])
                        t2 = k.prd("v", 7, lambda e, q0=q0: e.tensor_mul(
                            rp2[:], P[7][:], csB[:, q0:q0 + QCA]))
                        tq = k.op("v", lambda e, q0=q0, dstQ=dstQ: e.tensor_add(
                            dstQ[:, q0:q0 + QCA], rp1[:], rp2[:]), waits=[t1, t2], inc="vec")
                        rope_prev[0] = tq
                    else:
                        tq = k.prd("s", 6, lambda e, q0=q0, dstQ=dstQ: e.copy(
                            dstQ[:, q0:q0 + QCA], P[6][:]))
                    dst = (qsq_a if pair == 0 else ksq_a)[:, (qc % 2) * QCA:(qc % 2 + 1) * QCA]
                    tsq = k.op("s", lambda e, dst=dst, dstQ=dstQ, q0=q0:
                               e.square(dst, dstQ[:, q0:q0 + QCA]),
                               waits=[tq, prev_norm.get((pair, qc % 2))], inc="act")
                    prev_sq[(pair, qc % 2, c)] = tsq
                    info[("qk", qc, pair)] = tq
                return f

            def u_norm(qc):
                def f():
                    npar = (c % 2) * 16
                    tr = None
                    for pair in range(2):
                        src_a = (qsq_a if pair == 0 else ksq_a)[:, (qc % 2) * QCA:(qc % 2 + 1) * QCA]
                        bnk = 6 + pair
                        k.twait(k.takepr(bnk) + [prev_sq[(pair, qc % 2, c)]])
                        tn = None
                        for hh in range(2):
                            tn = k.mmw(bnk, lambda e, src_a=src_a, hh=hh, bnk=bnk:
                                       e.matmul(P[bnk][32 * hh:32 * hh + 1, :],
                                                sb["ones_bf"][64 * hh:64 * hh + 64, 0:1],
                                                src_a[64 * hh:64 * hh + 64, :],
                                                start=True, stop=True,
                                                tile_position=(64 * hh, 32 * hh)),
                                       nowait=True, noinc=(hh == 0))
                        k.pw[bnk] = tn
                        for hh in range(2):
                            col = npar + 8 * hh + 4 * pair + qc
                            tr = k.prd("v", bnk, lambda e, col=col, hh=hh, bnk=bnk:
                                       e.tensor_reduce(nacc[0:1, col:col + 1],
                                                       P[bnk][32 * hh:32 * hh + 1, :],
                                                       AX.X, ALU.max))
                            prev_norm[(pair, qc % 2)] = tr
                    info[("nrm", qc)] = tr
                return f

            def u_bias():
                def f():
                    npar = (c % 2) * 16
                    bpar = (c % 2) * 8
                    rprev = info[("nrm", NQA - 1)]
                    for hh in range(2):
                        rr1 = k.op("v", lambda e, hh=hh: e.tensor_reduce(
                            nred[0:1, bpar + 3 * hh:bpar + 3 * hh + 1],
                            nacc[0:1, npar + 8 * hh:npar + 8 * hh + 4], AX.X, ALU.max),
                            waits=[rprev], inc="vec")
                        rr2 = k.op("v", lambda e, hh=hh: e.tensor_reduce(
                            nred[0:1, bpar + 3 * hh + 1:bpar + 3 * hh + 2],
                            nacc[0:1, npar + 8 * hh + 4:npar + 8 * hh + 8], AX.X, ALU.max),
                            waits=[rr1], inc="vec")
                        rprev = k.op("v", lambda e, hh=hh: e.tensor_add(
                            nred[0:1, bpar + 3 * hh + 2:bpar + 3 * hh + 3],
                            nred[0:1, bpar + 3 * hh:bpar + 3 * hh + 1],
                            nred[0:1, bpar + 3 * hh + 1:bpar + 3 * hh + 2]),
                            waits=[rr2], inc="vec")
                    rmx = k.op("v", lambda e: e.tensor_tensor(
                        nred[0:1, bpar + 6:bpar + 7], nred[0:1, bpar + 2:bpar + 3],
                        nred[0:1, bpar + 5:bpar + 6], ALU.max), waits=[rprev], inc="vec")
                    rng = k.op("v", lambda e: e.tensor_scalar_mul(
                        nred[0:1, bpar + 7:bpar + 8], nred[0:1, bpar + 6:bpar + 7],
                        -1.0 / 16.0), waits=[rmx], inc="vec")
                    k.twait(k.takepr(7) + [rng])
                    tb = k.mmw(7, lambda e: e.matmul(
                        P[7][:, 0:1], sb["onesf"][0:1, 0:128],
                        nred[0:1, bpar + 7:bpar + 8], start=True, stop=True),
                        nowait=True)
                    tbias = k.prd("s", 7, lambda e: e.copy(
                        bias_sb[:, c % 2:c % 2 + 1], P[7][:, 0:1]))
                    info["bias"] = tbias
                return f

            def u_vt(tt):
                def f():
                    wt_v, slot_v = wtok[(iv, c)]
                    bnk = 6 + tt % 2
                    k.twait(k.takepr(bnk) + [wt_v, old_tok])
                    for ccx in range(CC):
                        k.mmw(bnk, lambda e, tt=tt, ccx=ccx, bnk=bnk:
                              e.matmul(P[bnk][:, 0:128],
                                       sb["xn"][:, ccx * S + tt * 128:ccx * S + tt * 128 + 128],
                                       wsl[slot_v][:, ccx * 128:ccx * 128 + 128],
                                       start=(ccx == 0), stop=(ccx == CC - 1)),
                              start=(ccx == 0), stop=(ccx == CC - 1), nowait=True)
                    dst = VT[:, tt * 130:tt * 130 + 130].rearrange(
                        "p (b c) -> p b c", c=65)[:, :, 0:64]
                    src = P[bnk][:, 0:128].rearrange("p (b c) -> p b c", c=64)
                    ev = k.prd("v", bnk, lambda e, dst=dst, src=src: e.tensor_copy(dst, src),
                               waits=[vz1 if tt == 0 else None, vz2 if tt == 0 else None])
                    info[("vt", tt)] = ev
                    if tt == 15:
                        wfree[slot_v] = ev
                        info["vt_done"] = ev
                        if c + 1 < CC:
                            loadw(iv, c + 1)
                return f

            for qc in range(NQA):
                units.append(u_proj(qc, 0))
                units.append(u_proj(qc, 1))
                units.append(u_norm(qc))
            units.append(u_bias())
            for tt in range(16):
                units.append(u_vt(tt))
            return units

        for f in bg_units(0):
            f()
        bgq = []

        dv_toks = [None, None]
        pending_div = [None]
        for c in range(CC):
            par = c % 2
            Qh = QhF[:, par * S:(par + 1) * S]
            Kh = KhF[:, par * S:(par + 1) * S]
            VT = VTF[:, par * 2080:(par + 1) * 2080]
            info = chunk_info[c]
            qk_done = info[("qk", NQA - 1, 1)]
            tbias = info["bias"]
            vt_done = info["vt_done"]
            bias_ap = bias_sb[:, c % 2:c % 2 + 1]
            if c + 1 < CC:
                if c + 2 < CC:
                    for mi in range(nm):
                        loadw(mi, c + 2)
                bgq = list(bg_units(c + 1))
            else:
                bgq = []

            def run_bg(n=1):
                for _ in range(n):
                    if bgq:
                        bgq.pop(0)()

            econs = {}
            emcons = {}
            for qc in range(NQ):
                q0 = qc * QC
                if a:
                    kts = list(range(16))
                    pv_lo, pv_hi = 0, 4 * qc + 4
                    str_lo = 4 * qc
                else:
                    kts = [t_ for t_ in range(4 * qc - 2, 4 * qc + 4) if t_ >= 0]
                    pv_lo, pv_hi = kts[0], kts[-1] + 1
                    str_lo = -10**9
                nkt = len(kts)
                state = {}
                pv_cnt = [0]
                d_cnt = [0]

                def emit_s_exp(j, qc=qc, q0=q0, kts=kts, str_lo=str_lo,
                               pv_lo=pv_lo, pv_hi=pv_hi, state=state,
                               Qh=Qh, Kh=Kh, qk_done=qk_done, tbias=tbias,
                               bias_ap=bias_ap):
                    kt = kts[j]
                    pb = j % 2
                    w = k.takepr(2 * pb) + k.takepr(2 * pb + 1)
                    if j == 0:
                        w += [qk_done, tbias]
                    k.twait(w)
                    for hh in range(2):
                        k.mmw(2 * pb + hh, lambda e, hh=hh, kt=kt, q0=q0, pb=pb, Qh=Qh, Kh=Kh:
                              e.matmul(P[2 * pb + hh][:, 0:QC],
                                       Kh[64 * hh:64 * hh + 64, kt * 128:kt * 128 + 128],
                                       Qh[64 * hh:64 * hh + 64, q0:q0 + QC],
                                       start=True, stop=True),
                              nowait=True, noinc=(hh == 0))
                    sl4 = (j % 2) * 2 * QC
                    eb2 = ebuf[:, sl4:sl4 + 2 * QC]
                    if QC == QCA:
                        src = PPf[pb][:, 0:1024]
                        dst = eb2
                    else:
                        src = PPf[pb][:].rearrange("p (b c) -> p b c", c=512)[:, :, 0:QC]
                        dst = eb2.rearrange("p (b c) -> p b c", c=QC)
                    texp = k.prd2("s", (2 * pb, 2 * pb + 1),
                                  lambda e, src=src, dst=dst, bias_ap=bias_ap:
                                  e.activation(dst, src, AF.Exp,
                                               bias=bias_ap[:, 0:1], scale=0.125),
                                  waits=[tbias, econs.get(j % 2)])
                    use_mask = ((a and str_lo <= kt < pv_hi) or (not a))
                    pv_use, tmask = eb2, texp
                    if use_mask and pv_lo <= kt < pv_hi:
                        mb2 = embuf[:, sl4:sl4 + 2 * QC]
                        r = (kt - 4 * qc) if a else (kt - 4 * qc + 2)
                        msl = mask[:, r * QC:r * QC + QC]
                        tmask = k.op("v", lambda e, mb2=mb2, eb2=eb2, msl=msl:
                                     e.tensor_tensor(
                                         mb2.rearrange("p (b c) -> p b c", c=QC),
                                         eb2.rearrange("p (b c) -> p b c", c=QC),
                                         msl.rearrange("p (b c) -> p b c", b=1).broadcast_to(
                                             (128, 2, QC)),
                                         ALU.mult),
                                     waits=[texp, cst_tok, emcons.get(j % 2)],
                                     inc="vec")
                        pv_use = mb2
                    state[j] = (texp, tmask, pv_use, eb2)

                def emit_dpv(j, qc=qc, kts=kts, pv_lo=pv_lo, pv_hi=pv_hi,
                             state=state, pv_cnt=pv_cnt, d_cnt=d_cnt,
                             VT=VT, vt_done=vt_done):
                    kt = kts[j]
                    nkt = len(kts)
                    npv = pv_hi - pv_lo
                    first_d = d_cnt[0] == 0
                    last_d = d_cnt[0] == nkt - 1
                    in_pv = pv_lo <= kt < pv_hi
                    first_pv = in_pv and pv_cnt[0] == 0
                    last_pv = in_pv and pv_cnt[0] == npv - 1
                    texp, tmask, pv_use, eb2 = state[j]
                    w = []
                    if first_d:
                        w += k.takepr(4) + [vt_done, zv_tok]
                    if first_pv:
                        w += k.takepr(5)
                    w.append(texp)
                    if in_pv:
                        w.append(tmask)
                    k.twait(w)
                    ebA = eb2[:, 0:QC]
                    ebB = eb2[:, QC:2 * QC]
                    pvA = pv_use[:, 0:QC]
                    pvB = pv_use[:, QC:2 * QC]
                    dsrcA = ebA if a else pvA
                    dsrcB = ebB if a else pvB
                    if not last_d:
                        k.mmw(4, lambda e, dsrcB=dsrcB, st=first_d:
                              e.matmul(P[4][0:66, 0:QC], zv[:, 66:132], dsrcB,
                                       start=st, stop=False),
                              start=first_d, stop=False, nowait=True, noinc=True)
                    if (not a) or (in_pv and pv_use is eb2):
                        fa = k.mmw(4, lambda e, kt=kt, src=(pvA if not a else ebA):
                                   e.matmul(P[4][0:65, 0:QC],
                                            VT[:, kt * 130:kt * 130 + 65],
                                            src, start=False, stop=False),
                                   start=False, stop=False, nowait=True, noinc="force")
                    elif in_pv:
                        k.mmw(4, lambda e, dsrcA=dsrcA:
                              e.matmul(P[4][0:65, 0:QC], zv[:, 0:65], dsrcA,
                                       start=False, stop=False),
                              start=False, stop=False, nowait=True, noinc=True)
                        fa = k.mmw(4, lambda e, kt=kt, pvA=pvA:
                                   e.matmul(P[4][0:64, 0:QC],
                                            VT[:, kt * 130:kt * 130 + 64],
                                            pvA, start=False, stop=False),
                                   start=False, stop=False, nowait=True, noinc="force")
                    else:
                        fa = k.mmw(4, lambda e, dsrcA=dsrcA:
                                   e.matmul(P[4][0:65, 0:QC], zv[:, 0:65], dsrcA,
                                            start=False, stop=False),
                                   start=False, stop=False, nowait=True, noinc="force")
                    if last_d:
                        fa = k.mmw(4, lambda e, dsrcB=dsrcB:
                                   e.matmul(P[4][0:66, 0:QC], zv[:, 66:132], dsrcB,
                                            start=False, stop=True),
                                   start=False, stop=True, nowait=True)
                    d_cnt[0] += 1
                    if in_pv:
                        fpb = k.mmw(5, lambda e, kt=kt, pvB=pvB, st=first_pv, sp=last_pv:
                                    e.matmul(P[5][64:128, 0:QC],
                                             VT[:, kt * 130 + 65:kt * 130 + 129],
                                             pvB, start=st, stop=sp,
                                             tile_position=(0, 64)),
                                    start=first_pv, stop=last_pv, nowait=True)
                        pv_cnt[0] += 1
                        last_tok = fpb
                    else:
                        last_tok = fa
                    if last_d:
                        k.pw[4] = last_tok
                    econs[j % 2] = last_tok
                    if pv_use is not eb2:
                        emcons[j % 2] = last_tok
                    del state[j]

                def make_divide(qc=qc, q0=q0, c=c):
                    def _recipa(e):
                        with nc.allow_low_precision(reason="bf16 inv-denominator, margin-checked"):
                            return e.reciprocal(dinv[64:66, 0:QC], P[4][64:66, 0:QC])
                    dra = k.prd("v", 4, _recipa)
                    drb = dra
                    ne1 = k.prd("v", 4, lambda e: e.tensor_copy(
                        nsb[0:64, 0:QC], P[4][0:64, 0:QC]), waits=[dv_toks[0]])
                    ne2 = k.prd("v", 5, lambda e: e.tensor_copy(
                        nsb[64:128, 0:QC], P[5][64:128, 0:QC]), waits=[dv_toks[1]])

                    def emit_divide():
                        k.twait(k.takepr(4) + k.takepr(5) + [dra, ne1, ne2, selAB_tok])
                        k.mmw(4, lambda e: e.matmul(
                            P[4][:, 0:QC], selAB[64:66, 0:128],
                            dinv[64:66, 0:QC],
                            start=True, stop=True), nowait=True, noinc=True)
                        bcb = k.mmw(5, lambda e: e.matmul(
                            P[5][:, 0:QC], selAB[64:66, 128:256],
                            dinv[64:66, 0:QC],
                            start=True, stop=True), nowait=True)
                        k.pw[4] = bcb
                        dv_toks[0] = k.prd("v", 4, lambda e: e.tensor_mul(
                            aout[0:64, c * S + q0:c * S + q0 + QC], nsb[0:64, 0:QC],
                            P[4][0:64, 0:QC]))
                        dv_toks[1] = k.prd("v", 5, lambda e: e.tensor_mul(
                            aout[64:128, c * S + q0:c * S + q0 + QC], nsb[64:128, 0:QC],
                            P[5][64:128, 0:QC]))
                    return emit_divide

                for j in range(nkt):
                    if j >= LAG:
                        emit_dpv(j - LAG)
                    run_bg(1)
                    emit_s_exp(j)
                    if j == 1 and pending_div[0] is not None:
                        pending_div[0]()
                        pending_div[0] = None
                for j in range(max(0, nkt - LAG), nkt):
                    emit_dpv(j)
                    run_bg(1)
                if pending_div[0] is not None:
                    pending_div[0]()
                    pending_div[0] = None
                pending_div[0] = make_divide()
            run_bg(len(bgq))
            chunk_done_mm[c] = ("mm", k.cnt["mm"])
        pending_div[0]()
        pending_div[0] = None
        last_aout = dv_toks[1]

        for fc in range(CC):
            slot = 20 + fc % 2
            t_wo = k.dma("y", wo_sl[fc % 2][:], scw[mats[nw]][fc], f"aw{slot}",
                         waits=[wfree.get(slot), pre["cast_tok"], last_aout])
            wfree[slot] = t_wo
            for qc in range(NQA):
                q0 = qc * QCA
                pp = 6 + (fc * NQA + qc) % 2
                k.twait(k.takepr(pp) + [t_wo, last_aout])
                for dc in range(CC):
                    k.mmw(pp, lambda e, dc=dc, q0=q0, fc=fc, pp=pp:
                          e.matmul(P[pp][:], wo_sl[fc % 2][:, dc * 128:dc * 128 + 128],
                                   aout[:, dc * S + q0:dc * S + q0 + QCA],
                                   start=(dc == 0), stop=(dc == CC - 1)),
                          start=(dc == 0), stop=(dc == CC - 1), nowait=True)
                tadd = k.prd("v", pp, lambda e, fc=fc, q0=q0, pp=pp: e.tensor_add(
                    sb["x"][:, fc * S + q0:fc * S + q0 + QCA],
                    sb["x"][:, fc * S + q0:fc * S + q0 + QCA], P[pp][:]))
            wfree[slot] = tadd
        return tadd

    def ffn_alloc_preload():
        pre = {}
        esel_sb = ar("esel", (NE, NE * 128), BF16)
        cbias_sb = ar("cbias", (128, 256), F32)
        rw_sb = ar("rw", (128, 128), BF16)
        comb_fm = ar("comb_fm", (NE, S), BF16)
        hh = ar("hh", (128, 48 * QCA), BF16)
        hh_off = int(hh.handle.offset) if hasattr(hh, "handle") else None
        hh_base = A[0] - 48 * QCA * 2
        sco_fm = ar_at("sco_fm", (NE, S), F32, hh_base)
        scT = ar_at("scT", (128, 256), F32, hh_base + 8192)
        sbT = ar_at("sbT", (128, 256), F32, hh_base + 8192 + 1024)
        msk = ar_at("msk", (128, 256), F32, hh_base + 8192 + 2048)
        cnt = ar_at("cnt", (128, 256), F32, hh_base + 8192 + 3072)
        ctm = ar_at("ctm", (128, 256), F32, hh_base + 8192 + 4096)
        rb = hh_base + 8192 + 5120
        gs = ar_at("gs", (128, 64), F32, rb)
        gct = ar_at("gct", (128, 64), F32, rb + 256)
        gtm = ar_at("gtm", (128, 64), F32, rb + 512)
        gm = ar_at("gm", (128, 64), F32, rb + 768)
        s1 = ar_at("s1", (128, 64), F32, rb + 1024)
        s2 = ar_at("s2", (128, 64), F32, rb + 1280)
        s3 = ar_at("s3", (128, 64), F32, rb + 1536)
        wsum = ar_at("wsum", (128, 16), F32, rb + 1792)
        comb_tm = ar_at("comb_tm", (128, 256), F32, rb + 1856)
        silu_t = [ar(f"silu_t{i}", (128, QCA), BF16) for i in range(2)]
        t2b = [ar(f"t2b{i}", (128, QCA), BF16) for i in range(2)]
        sgsu_sl = [ar(f"sgsu{i}", (128, 2048), BF16) for i in range(2)]
        sd_sl = [ar(f"sd{i}", (128, 2048), BF16) for i in range(2)]
        dw_sl = [ar(f"dw{i}", (128, 4096), BF16) for i in range(2)]
        eg_sl = []

        k.dma("y", esel_sb[:], ap["esel"], "dcst")
        tcs = k.dma("y", cbias_sb[:], ap["cbias"], "dcst")
        trw = k.dma("g", rw_sb[:], ap["rwT"], "gcst")
        cast_done = k.tok("fcst")
        pre["tl_sgsu"] = [k.dma("y", sgsu_sl[i][:], sc["sgsu"][i], f"fs{i}",
                                waits=[cast_done]) for i in range(2)]
        pre["tl_sd0"] = k.dma("y", sd_sl[0][:], sc["sd"][0], "fd0",
                              waits=[cast_done])
        pre["tl_dw0"] = k.dma("y", dw_sl[0][:], sc["dwcc"][0], "fw0",
                              waits=[cast_done])
        for nm_ in ("esel_sb", "cbias_sb", "rw_sb", "comb_fm", "hh", "sco_fm",
                    "scT", "sbT", "msk", "cnt", "ctm", "gs", "gct", "gtm", "gm",
                    "s1", "s2", "s3", "wsum", "comb_tm", "silu_t", "t2b",
                    "sgsu_sl", "sd_sl", "dw_sl", "eg_sl", "tcs", "trw",
                    "cast_done"):
            pre[nm_] = locals()[nm_]
        return pre

    def emit_ffn(pre):
        FLV = int(os.environ.get("FFN_LV", "5"))
        (esel_sb, cbias_sb, rw_sb, comb_fm, hh, sco_fm, scT, sbT, msk, cnt,
         ctm, gs, gct, gtm, gm, s1, s2, s3, wsum, comb_tm, silu_t, t2b,
         sgsu_sl, sd_sl, dw_sl, eg_sl, tcs, trw, cast_done) = (
            pre[nm_] for nm_ in (
                "esel_sb", "cbias_sb", "rw_sb", "comb_fm", "hh", "sco_fm",
                "scT", "sbT", "msk", "cnt", "ctm", "gs", "gct", "gtm", "gm",
                "s1", "s2", "s3", "wsum", "comb_tm", "silu_t", "t2b",
                "sgsu_sl", "sd_sl", "dw_sl", "eg_sl", "tcs", "trw", "cast_done"))
        T[0] = top
        eg_sl.append(ar_top("eg1", (128, 4096), BF16))
        eg_sl.append(ar_top("eg0", (128, 4096), BF16))
        eg_sl.reverse()
        ln_guard = ("vec", k.cnt["vec"])
        last_xn = k.tok("vec")
        last_ln_act = k.tok("act")

        for qc in range(NQA):
            q0 = qc * QCA
            for ccx in range(CC):
                k.mmw(6, lambda e, ccx=ccx, q0=q0:
                      e.matmul(P[6][0:16, :], rw_sb[:, ccx * 16:ccx * 16 + 16],
                               sb["xn"][:, ccx * S + q0:ccx * S + q0 + QCA],
                               start=(ccx == 0), stop=(ccx == CC - 1)),
                      waits=[trw, last_xn, last_ln_act],
                      start=(ccx == 0), stop=(ccx == CC - 1))
            tsig = k.prd("s", 6, lambda e, q0=q0: e.activation(
                sco_fm[0:16, q0:q0 + QCA], P[6][0:16, :], AF.Sigmoid))
        for tt in range(16):
            k.mmw(7, lambda e, tt=tt: e.transpose(
                P[7][:, tt * 16:tt * 16 + 16], sco_fm[0:16, tt * 128:tt * 128 + 128],
                sb["identf_sb"][0:16, 0:16]),
                waits=[tsig, ident_tok[0]], start=(tt == 0), stop=(tt == 15))
        v0 = k.prd("v", 7, lambda e: e.tensor_copy(scT[:], P[7][:, 0:256]))
        vt = k.op("v", lambda e: e.tensor_add(sbT[:], scT[:], cbias_sb[:]),
                  waits=[v0, tcs], inc="vec")

        def re(apx):
            return apx.rearrange("p (t e) -> p t e", e=16)
        sb4 = sbT[:].rearrange("p (t g j) -> p t g j", g=4, j=4)
        g4 = gs[:].rearrange("p (t g) -> p t g", g=4)
        s1r = s1[:].rearrange("p (t g) -> p t g", g=4)
        s2r = s2[:].rearrange("p (t g) -> p t g", g=4)
        s3r = s3[:].rearrange("p (t g) -> p t g", g=4)
        gmr = gm[:].rearrange("p (t g) -> p t g", g=4)
        gcr = gct[:].rearrange("p (t g) -> p t g", g=4)
        gtr = gtm[:].rearrange("p (t g) -> p t g", g=4)

        def vop(fn):
            nonlocal vt
            vt = k.op("v", fn, waits=[vt], inc="vec")
            return vt
        vop(lambda e: e.tensor_tensor(s1r, sb4[:, :, :, 0], sb4[:, :, :, 1], ALU.max))
        vop(lambda e: e.tensor_tensor(s2r, sb4[:, :, :, 0], sb4[:, :, :, 1], ALU.min))
        vop(lambda e: e.tensor_tensor(s3r, sb4[:, :, :, 2], sb4[:, :, :, 3], ALU.max))
        vop(lambda e: e.tensor_tensor(gmr, sb4[:, :, :, 2], sb4[:, :, :, 3], ALU.min))
        vop(lambda e: e.tensor_tensor(gcr, s1r, s3r, ALU.max))
        vop(lambda e: e.tensor_tensor(s1r, s1r, s3r, ALU.min))
        vop(lambda e: e.tensor_tensor(s2r, s2r, gmr, ALU.max))
        vop(lambda e: e.tensor_tensor(s1r, s1r, s2r, ALU.max))
        vop(lambda e: e.tensor_tensor(g4, gcr, s1r, ALU.add))
        vop(lambda e: e.memset(gct[:], 0.0))
        for dd in range(1, 4):
            Aap = g4[:, :, 0:4 - dd]
            Bap = g4[:, :, dd:4]
            cA = gcr[:, :, 0:4 - dd]
            cB = gcr[:, :, dd:4]
            tmp = gtr[:, :, 0:4 - dd]
            vop(lambda e, A=Aap, B=Bap, t=tmp: e.tensor_tensor(t, A, B, ALU.is_ge))
            vop(lambda e, t=tmp, c2=cB: e.tensor_tensor(c2, c2, t, ALU.add))
            vop(lambda e, A=Aap, B=Bap, t=tmp: e.tensor_tensor(t, B, A, ALU.is_gt))
            vop(lambda e, t=tmp, c2=cA: e.tensor_tensor(c2, c2, t, ALU.add))
        vop(lambda e: e.tensor_single_scalar(gm[:], gct[:], 2.0, ALU.is_lt))
        mr4 = msk[:].rearrange("p (t g j) -> p t g j", g=4, j=4)
        for j in range(4):
            vop(lambda e, j=j: e.scalar_tensor_tensor(
                mr4[:, :, :, j], sb4[:, :, :, j], 4.0, gmr, ALU.add, ALU.mult))
        vop(lambda e: e.memset(cnt[:], 0.0))
        mre = re(msk[:])
        cre = re(cnt[:])
        tre = re(ctm[:])
        for dd in range(1, 16):
            Aap = mre[:, :, 0:16 - dd]
            Bap = mre[:, :, dd:16]
            cA = cre[:, :, 0:16 - dd]
            cB = cre[:, :, dd:16]
            tmp = tre[:, :, 0:16 - dd]
            vop(lambda e, A=Aap, B=Bap, t=tmp: e.tensor_tensor(t, A, B, ALU.is_ge))
            vop(lambda e, t=tmp, c2=cB: e.tensor_tensor(c2, c2, t, ALU.add))
            vop(lambda e, A=Aap, B=Bap, t=tmp: e.tensor_tensor(t, B, A, ALU.is_gt))
            vop(lambda e, t=tmp, c2=cA: e.tensor_tensor(c2, c2, t, ALU.add))
        vop(lambda e: e.tensor_single_scalar(msk[:], cnt[:], 4.0, ALU.is_lt))
        vop(lambda e: e.tensor_mul(ctm[:], scT[:], msk[:]))
        vop(lambda e: e.tensor_reduce(wsum[:], re(ctm[:]), AX.X, ALU.add))
        vop(lambda e: e.tensor_scalar_add(wsum[:], wsum[:], 1e-20))
        vop(lambda e: e.reciprocal(wsum[:], wsum[:]))
        winb = wsum[:].rearrange("p (t e) -> p t e", e=1).broadcast_to((128, 16, 16))
        vop(lambda e: e.tensor_tensor(re(comb_tm[:]), re(ctm[:]), winb, ALU.mult))
        comb_vt = vt
        comb_state = {}

        def emit_comb_back():
            cevac = None
            for b4 in range(4):
                for j4 in range(4):
                    tt = 4 * b4 + j4
                    k.mmw(7, lambda e, tt=tt, j4=j4: e.transpose(
                        P[7][0:16, j4 * 128:j4 * 128 + 128],
                        comb_tm[:, tt * 16:tt * 16 + 16], sb["identf_sb"][:]),
                        waits=[comb_vt], start=(j4 == 0), stop=(j4 == 3))
                cevac = k.prd("s", 7, lambda e, b4=b4: e.copy(
                    comb_fm[0:16, b4 * 512:b4 * 512 + 512], P[7][0:16, 0:512]))
            comb_state["done"] = cevac

        emit_comb_back()

        sl_free = {0: None, 1: None}
        ed_free = {0: None, 1: None}
        dwf = {0: None, 1: None}
        sdf = {0: None, 1: None}
        st_free = {0: None, 1: None}
        t2_free = {0: None, 1: None}
        hh_tok = {}
        down_done = None
        res_tok = None
        for qc in range(NQA):
            q0 = qc * QCA
            if FLV < 2:
                continue
            for hc in range(16):
                sl = hc % 2
                if qc == 0 and hc < 2:
                    tl = pre["tl_sgsu"][hc]
                else:
                    tl = k.dma("y", sgsu_sl[sl][:], sc["sgsu"][hc], f"fs{sl}",
                               waits=[cast_done, sl_free[sl]])
                pp = hc % 2
                for ccx in range(CC):
                    k.mmw(pp, lambda e, ccx=ccx, q0=q0, sl=sl, pp=pp:
                          e.matmul(P[pp][:], sgsu_sl[sl][:, ccx * 128:ccx * 128 + 128],
                                   sb["xn"][:, ccx * S + q0:ccx * S + q0 + QCA],
                                   start=(ccx == 0), stop=(ccx == CC - 1)),
                          waits=[tl, last_xn, last_ln_act],
                          start=(ccx == 0), stop=(ccx == CC - 1))
                for ccx in range(CC):
                    tu = k.mmw(2 + pp, lambda e, ccx=ccx, q0=q0, sl=sl, pp=pp:
                               e.matmul(P[2 + pp][:],
                                        sgsu_sl[sl][:, 1024 + ccx * 128:1024 + ccx * 128 + 128],
                                        sb["xn"][:, ccx * S + q0:ccx * S + q0 + QCA],
                                        start=(ccx == 0), stop=(ccx == CC - 1)),
                               waits=[tl],
                               start=(ccx == 0), stop=(ccx == CC - 1))
                ts = k.prd("s", pp, lambda e, pp=pp: e.activation(
                    silu_t[pp][:], P[pp][:], AF.Silu), waits=[st_free[pp]])
                th = k.prd("v", 2 + pp, lambda e, hc=hc, pp=pp: e.tensor_mul(
                    hh[:, hc * QCA:hc * QCA + QCA], silu_t[pp][:], P[2 + pp][:]),
                    waits=[ts, down_done])
                st_free[pp] = th
                hh_tok[hc] = th
                sl_free[sl] = tu
            comb_done = comb_state["done"]
            for ee in range(NE):
                if FLV < 3:
                    break
                sl = ee % 2
                tl = k.dma("y", eg_sl[sl][:], sc["expgu"][ee], f"fe{sl}",
                           waits=[cast_done, ed_free[sl], ln_guard])
                tcb = k.mmw(4, lambda e, ee=ee, q0=q0: e.matmul(
                    P[4][:], esel_sb[:, ee * 128:ee * 128 + 128],
                    comb_fm[0:16, q0:q0 + QCA], start=True, stop=True),
                    waits=[comb_done])
                for h2 in range(2):
                    pp = h2
                    for ccx in range(CC):
                        k.mmw(pp, lambda e, ccx=ccx, q0=q0, sl=sl, h2=h2, pp=pp:
                              e.matmul(P[pp][:],
                                       eg_sl[sl][:, (h2 * 8 + ccx) * 128:(h2 * 8 + ccx) * 128 + 128],
                                       sb["xn"][:, ccx * S + q0:ccx * S + q0 + QCA],
                                       start=(ccx == 0), stop=(ccx == CC - 1)),
                              waits=[tl], start=(ccx == 0), stop=(ccx == CC - 1))
                    for ccx in range(CC):
                        tu = k.mmw(2 + pp, lambda e, ccx=ccx, q0=q0, sl=sl, h2=h2, pp=pp:
                                   e.matmul(P[2 + pp][:],
                                            eg_sl[sl][:, 2048 + (h2 * 8 + ccx) * 128:2048 + (h2 * 8 + ccx) * 128 + 128],
                                            sb["xn"][:, ccx * S + q0:ccx * S + q0 + QCA],
                                            start=(ccx == 0), stop=(ccx == CC - 1)),
                                   waits=[tl], start=(ccx == 0), stop=(ccx == CC - 1))
                    ts = k.prd("s", pp, lambda e, pp=pp: e.activation(
                        silu_t[pp][:], P[pp][:], AF.Silu), waits=[st_free[pp]])
                    tm = k.prd("v", 2 + pp, lambda e, pp=pp: e.tensor_mul(
                        t2b[pp][:], silu_t[pp][:], P[2 + pp][:]),
                        waits=[ts, t2_free[pp]])
                    st_free[pp] = tm
                    th = k.prd("v", 4, lambda e, ee=ee, h2=h2, pp=pp: e.tensor_mul(
                        hh[:, (16 + 2 * ee + h2) * QCA:(16 + 2 * ee + h2 + 1) * QCA],
                        t2b[pp][:], P[4][:]), waits=[tm, down_done])
                    t2_free[pp] = th
                    hh_tok[16 + 2 * ee + h2] = th
                ed_free[sl] = tu
            for ccx in range(CC):
                if FLV < 4:
                    break
                sl = ccx % 2
                if qc == 0 and ccx == 0:
                    tsd, tdw = pre["tl_sd0"], pre["tl_dw0"]
                else:
                    tsd = k.dma("y", sd_sl[sl][:], sc["sd"][ccx], f"fd{sl}",
                                waits=[cast_done, sdf[sl]])
                    tdw = k.dma("y", dw_sl[sl][:], sc["dwcc"][ccx], f"fw{sl}",
                                waits=[cast_done, dwf[sl]])
                pp = 5 + ccx % 2
                shstop = FLV < 5
                for hc in range(16):
                    tk = k.mmw(pp, lambda e, hc=hc, sl=sl, pp=pp,
                               sp=(shstop and hc == 15):
                               e.matmul(P[pp][:], sd_sl[sl][:, hc * 128:hc * 128 + 128],
                                        hh[:, hc * QCA:hc * QCA + QCA],
                                        start=(hc == 0), stop=sp),
                               waits=[tsd, hh_tok[hc]],
                               start=(hc == 0), stop=(shstop and hc == 15))
                for ee in range(NE):
                    if FLV < 5:
                        break
                    for h2 in range(2):
                        idx = 2 * ee + h2
                        tk = k.mmw(pp, lambda e, idx=idx, sl=sl, pp=pp, sp=(idx == 31):
                                   e.matmul(P[pp][:],
                                            dw_sl[sl][:, idx * 128:idx * 128 + 128],
                                            hh[:, (16 + idx) * QCA:(16 + idx + 1) * QCA],
                                            start=False, stop=sp),
                                   waits=[tdw, hh_tok[16 + idx]],
                                   start=False, stop=(idx == 31))
                def _res(e, ccx=ccx, q0=q0, pp=pp, qc=qc):
                    inst = e.tensor_add(
                        sb["x"][:, ccx * S + q0:ccx * S + q0 + QCA],
                        sb["x"][:, ccx * S + q0:ccx * S + q0 + QCA], P[pp][:])
                    inst.annotate(f"res qc{qc} ccx{ccx} pp{pp}")
                    return inst
                res_tok = k.prd("v", pp, _res)
                sdf[sl] = tk
                dwf[sl] = tk
                down_done = res_tok
        return res_tok

    def emit_store():
        for ccx in range(CC):
            k.dma("y", yout[ccx], sb["x"][:, ccx * S:(ccx + 1) * S], "dst")
        k.op("y", lambda e: None, waits=[("dst", 16 * CC)])

    emit_load()
    k.barrier()
    i = 0
    while i < len(stages):
        st = stages[i]
        ar_reset()
        if st.startswith("ln"):
            nxt = stages[i + 1] if i + 1 < len(stages) else None
            pre = None
            if nxt in ("attn_a", "attn_b"):
                pre = attn_alloc_preload("a" if nxt == "attn_a" else "b")
            elif nxt is not None and nxt.startswith("ffn"):
                pre = ffn_alloc_preload()
            emit_ln()
            if nxt in ("attn_a", "attn_b"):
                emit_attn(pre)
                i += 1
            elif nxt is not None and nxt.startswith("ffn"):
                emit_ffn(pre)
                i += 1
        k.barrier()
        i += 1
    emit_store()

    for name in list(k.cnt):
        k.sems[name] = nc.alloc_semaphore(name)

    with nc.Block() as block:
        @block.tensor
        def _(e):
            for f in k.prog["t"]:
                f(e)

        @block.vector
        def _(e):
            for f in k.prog["v"]:
                f(e)

        @block.scalar
        def _(e):
            for f in k.prog["s"]:
                f(e)

        @block.gpsimd
        def _(e):
            for f in k.prog["g"]:
                f(e)

        @block.sync
        def _(e):
            for f in k.prog["y"]:
                f(e)
    return nc


def kernel(**inputs):
    return _run(inputs, list(ALL_STAGES))


def _run(inputs, stages, cores=None):
    hp = host_prep(inputs)
    x = np.asarray(inputs["L0_attn_a_x"])
    B = x.shape[0]
    if cores is None:
        cores = list(range(B))
    nc = build_nc(tuple(stages))
    in_maps = []
    for b in cores:
        m = dict(hp)
        m["xin"] = np.ascontiguousarray(x[b].T.reshape(CC, 128, S))
        in_maps.append(m)
    res = run_bass_kernel_spmd(nc, in_maps, core_ids=list(range(len(cores))))
    outs = []
    for r in res.results:
        y = np.asarray(r["y"]).reshape(H, S)
        outs.append(np.ascontiguousarray(y.T))
    return np.stack(outs, axis=0)


# version 12
# speedup vs baseline: 1.1672x; 1.0060x over previous
import os
import numpy as np
import ml_dtypes
import concourse.bass as bass
import concourse.mybir as mybir
from concourse.bass_utils import run_bass_kernel_spmd

F32 = mybir.dt.float32
BF16 = mybir.dt.bfloat16
AF = mybir.ActivationFunctionType
ALU = mybir.AluOpType
AX = mybir.AxisListType
BF = ml_dtypes.bfloat16

S = 2048
H = 1024
CC = 8
NH = 16
DH = 64
QCA = 512
NQA = 4
QCB = 512
NQB = 4
NE = 16
WINDOW = 256

ENGS = ("t", "v", "s", "g", "y")
ALL_STAGES = ("ln1", "attn_a", "ln2", "ffn1", "ln3", "attn_b", "ln4", "ffn2")


class Rec:

    def __init__(self, nc):
        self.nc = nc
        self.prog = {e: [] for e in ENGS}
        self.cnt = {}
        self.lastwait = {e: {} for e in ENGS}
        self.sems = {}
        self.owner = {}
        self.nbar = 0
        self.pw = [None] * 8
        self.pr = [[] for _ in range(8)]

    def sem(self, name, owner=None):
        self.cnt[name] = 0
        self.owner[name] = owner

    def tok(self, name):
        return (name, self.cnt[name])

    def op(self, eng, fn, waits=(), inc=None, n=1):
        lw = self.lastwait[eng]
        real = []
        for w in waits:
            if w is None:
                continue
            s_, v = w
            if v > 0 and lw.get(s_, 0) < v:
                real.append((s_, v))
                lw[s_] = v
        out_tok = None
        if inc is not None:
            self.cnt[inc] += n
            out_tok = (inc, self.cnt[inc])
        sems = self.sems

        def play(e, real=real, fn=fn, inc=inc, n=n):
            for s_, v in real:
                e.wait_ge(sems[s_], v)
            inst = fn(e)
            if inc is not None:
                if inst is None:
                    e.nop().then_inc(sems[inc], n)
                else:
                    inst.then_inc(sems[inc], n)
        self.prog[eng].append(play)
        return out_tok

    def dma(self, eng, out, in_, sem, waits=()):
        return self.op(eng, lambda e, o=out, i=in_: e.dma_start(out=o, in_=i),
                       waits=waits, inc=sem, n=16)

    def mmw(self, bank, fn, waits=(), start=True, stop=True, nowait=False, noinc=False):
        ws = list(waits)
        if start and not nowait:
            ws += self.pr[bank]
        do_inc = (stop and not noinc) or noinc == "force"
        tok = self.op("t", fn, ws, inc="mm" if do_inc else None)
        if start:
            self.pr[bank] = []
        if stop:
            self.pw[bank] = tok
        return tok

    def takepr(self, bank):
        r = self.pr[bank]
        self.pr[bank] = []
        return list(r)

    def twait(self, waits):
        self.op("t", lambda e: None, waits=waits)

    def prd(self, eng, bank, fn, waits=(), inc=None):
        inc = inc or {"v": "vec", "s": "act"}[eng]
        tok = self.op(eng, fn, list(waits) + [self.pw[bank]], inc=inc)
        self.pr[bank].append(tok)
        return tok

    def prd2(self, eng, banks, fn, waits=(), inc=None):
        inc = inc or {"v": "vec", "s": "act"}[eng]
        tok = self.op(eng, fn, list(waits) + [self.pw[b] for b in banks], inc=inc)
        for b in banks:
            self.pr[b].append(tok)
        return tok

    def barrier(self, exclude=()):
        self.nbar += 1
        target = 5 * self.nbar
        for eng in ENGS:
            own = [(n, self.cnt[n]) for n, o in self.owner.items()
                   if o == eng and self.cnt[n] > 0 and n not in exclude]
            self.op(eng, lambda e: None, waits=own, inc="ph")
        for eng in ENGS:
            self.op(eng, lambda e: None, waits=[("ph", target)])
        return ("ph", target)


def _blocked(wt):
    c, f = wt.shape
    return np.ascontiguousarray(
        wt.reshape(c // 128, 128, f // 128, 128).transpose(2, 1, 0, 3))


def host_prep(ins):
    d = {}
    pm = np.zeros(H, np.int64)
    pr = np.zeros(H, np.int64)
    for h in range(NH):
        for j in range(DH):
            if j < 32:
                pm[64 * h + j] = 64 * h + 2 * j
                pr[64 * h + j] = 64 * h + 2 * j + 1
            else:
                pm[64 * h + j] = 64 * h + 2 * (j - 32) + 1
                pr[64 * h + j] = 64 * h + 2 * (j - 32)
    wq, wk = np.asarray(ins["L0_attn_a_w_q"]), np.asarray(ins["L0_attn_a_w_k"])
    d["aqm"] = _blocked(np.ascontiguousarray(wq[pm].T))
    d["aqr"] = _blocked(np.ascontiguousarray(wq[pr].T))
    d["akm"] = _blocked(np.ascontiguousarray(wk[pm].T))
    d["akr"] = _blocked(np.ascontiguousarray(wk[pr].T))
    d["avT"] = _blocked(np.ascontiguousarray(np.asarray(ins["L0_attn_a_w_v"]).T))
    d["aoT"] = _blocked(np.ascontiguousarray(np.asarray(ins["L0_attn_a_w_o"]).T))
    d["bqT"] = _blocked(np.ascontiguousarray(np.asarray(ins["L0_attn_b_Wq"]).T))
    d["bkT"] = _blocked(np.ascontiguousarray(np.asarray(ins["L0_attn_b_Wk"]).T))
    d["bvT"] = _blocked(np.ascontiguousarray(np.asarray(ins["L0_attn_b_Wv"]).T))
    d["boT"] = _blocked(np.ascontiguousarray(np.asarray(ins["L0_attn_b_Wo"]).T))
    sgb = _blocked(np.ascontiguousarray(np.asarray(ins["L0_ffn_shared_gate_w"]).T))
    sub = _blocked(np.ascontiguousarray(np.asarray(ins["L0_ffn_shared_up_w"]).T))
    d["sgsu"] = np.ascontiguousarray(
        np.concatenate([sgb.reshape(16, 128, 1024), sub.reshape(16, 128, 1024)], axis=2))
    sdw = np.ascontiguousarray(np.asarray(ins["L0_ffn_shared_down_w"]).T)
    d["sd"] = np.ascontiguousarray(
        sdw.reshape(16, 128, 8, 128).transpose(2, 1, 0, 3).reshape(8, 128, 2048))
    gw = np.asarray(ins["L0_ffn_gate_w"])
    uw = np.asarray(ins["L0_ffn_up_w"])
    dw = np.asarray(ins["L0_ffn_down_w"])

    def gu_blk(w):
        return np.ascontiguousarray(
            w.reshape(NE, 8, 128, 2, 128).transpose(0, 2, 3, 1, 4).reshape(NE, 128, 2048))
    d["expgu"] = np.ascontiguousarray(
        np.concatenate([gu_blk(gw), gu_blk(uw)], axis=2))
    d["dwcc"] = np.ascontiguousarray(
        dw.reshape(NE, 2, 128, 8, 128).transpose(3, 2, 0, 1, 4).reshape(8, 128, 4096))
    d["rwT"] = np.ascontiguousarray(
        np.asarray(ins["L0_ffn_router_w"]).T.reshape(8, 128, 16).transpose(1, 0, 2).reshape(128, 128))
    d["cbias"] = np.ascontiguousarray(
        np.tile(np.asarray(ins["L0_ffn_correction_bias"]).astype(np.float32), (128, NE)))
    half = DH // 2
    inv = 10000.0 ** (-(np.arange(half, dtype=np.float64) * 2.0 / DH))
    t = np.arange(S, dtype=np.float64)
    ang = t[None, :] * inv[:, None]
    cos = np.cos(ang).astype(np.float32)
    sin = np.sin(ang).astype(np.float32)
    csA = np.empty((128, S), BF)
    csB = np.empty((128, S), BF)
    for b in range(4):
        csA[32 * b:32 * b + 32] = cos
        csB[32 * b:32 * b + 32] = (-sin) if (b % 2 == 0) else sin
    d["csA"], d["csB"] = csA, csB
    kk = np.arange(128)[:, None]
    qq = np.arange(QCA)[None, :]
    ma = np.zeros((128, 4 * QCA), BF)
    for r in range(4):
        ma[:, r * QCA:(r + 1) * QCA] = (qq >= kk + 128 * r).astype(BF)
    d["maska"] = ma
    qq2 = np.arange(QCB)[None, :]
    mb = np.zeros((128, 6 * QCB), BF)
    for i, r in enumerate((-2, -1, 0, 1, 2, 3)):
        off = qq2 - kk - 128 * r
        mb[:, i * QCB:(i + 1) * QCB] = ((off >= 0) & (off <= WINDOW)).astype(BF)
    d["maskb"] = mb
    d["identf"] = np.eye(128, dtype=np.float32)
    selab = np.zeros((128, 256), BF)
    selab[64, 0:128] = 1.0
    selab[65, 128:256] = 1.0
    d["selab"] = selab
    esel = np.zeros((NE, NE * 128), BF)
    for e in range(NE):
        esel[e, 128 * e:128 * e + 128] = 1.0
    d["esel"] = esel
    return d


def build_nc(stages):
    nc = bass.Bass("TRN2", target_bir_lowering=False)
    ap = {}

    def din(name, shape, dtype=F32):
        ap[name] = nc.dram_tensor(name, list(shape), dtype, kind="ExternalInput").ap()

    din("xin", (CC, 128, S))
    for n in ("aqm", "aqr", "akm", "akr", "avT", "aoT", "bqT", "bkT", "bvT", "boT"):
        din(n, (8, 128, 8, 128))
    din("sgsu", (16, 128, 2048))
    din("sd", (8, 128, 2048))
    din("expgu", (NE, 128, 4096))
    din("dwcc", (8, 128, 4096))
    din("rwT", (128, 128))
    din("cbias", (128, 256))
    din("csA", (128, S), BF16)
    din("csB", (128, S), BF16)
    din("maska", (128, 4 * QCA), BF16)
    din("maskb", (128, 6 * QCB), BF16)
    din("identf", (128, 128))
    din("esel", (NE, NE * 128), BF16)
    din("selab", (128, 256), BF16)
    yout = nc.dram_tensor("y", [CC, 128, S], F32, kind="ExternalOutput").ap()

    sc = {}
    scw = {}
    need_attn = any(st.startswith("attn") for st in stages)
    if need_attn:
        for n in ("aqm", "aqr", "akm", "akr", "avT", "aoT", "bqT", "bkT", "bvT", "boT"):
            scw[n] = nc.dram_tensor(f"{n}_bf", [8, 128, 1024], BF16, kind="Internal").ap()
    need_ffn = any(st.startswith("ffn") for st in stages)
    if need_ffn:
        sc["sgsu"] = nc.dram_tensor("sgsu_bf", [16, 128, 2048], BF16, kind="Internal").ap()
        sc["sd"] = nc.dram_tensor("sd_bf", [8, 128, 2048], BF16, kind="Internal").ap()
        sc["expgu"] = nc.dram_tensor("expgu_bf", [NE, 128, 4096], BF16, kind="Internal").ap()
        sc["dwcc"] = nc.dram_tensor("dwcc_bf", [8, 128, 4096], BF16, kind="Internal").ap()

    sb = {}

    def sbt(name, shape, dtype, off=None):
        if off is None:
            sb[name] = nc.alloc_sbuf_tensor(name, list(shape), dtype)
        else:
            sb[name] = nc.alloc_sbuf_tensor_at(name, list(shape), dtype, offset=off)
        return sb[name]

    sbt("x", (128, CC * S), F32)
    sbt("xn", (128, CC * S), BF16)
    sbt("ones_bf", (128, 128), BF16)
    sbt("eps_sb", (128, 1), F32)
    sbt("onesf", (128, 128), F32)
    sbt("identf_sb", (128, 128), F32)
    arena = nc.sbuf_base
    top = nc.sbuf_top
    A = [arena]
    ph_id = [0]

    def ar(name, shape, dtype, align=64):
        nbytes = int(np.prod(shape[1:])) * (2 if dtype == BF16 else 4)
        off = (A[0] + align - 1) // align * align
        assert off + nbytes <= top, f"SBUF arena overflow at {name}: {off+nbytes-top} over"
        A[0] = off + nbytes
        return sbt(f"{name}_{ph_id[0]}", shape, dtype, off=off)

    def ar_at(name, shape, dtype, off):
        return sbt(f"{name}_{ph_id[0]}", shape, dtype, off=off)

    T = [top]

    ar_offs = {}

    def ar_top(name, shape, dtype, align=64):
        nbytes = int(np.prod(shape[1:])) * (2 if dtype == BF16 else 4)
        off = (T[0] - nbytes) // align * align
        assert off >= A[0], (
            f"SBUF top-arena overflow at {name}: off={off} A={A[0]} short={A[0]-off}")
        T[0] = off
        ar_offs[name] = off
        return sbt(f"{name}_{ph_id[0]}", shape, dtype, off=off)

    def ar_reset():
        A[0] = arena
        T[0] = top
        ph_id[0] += 1

    PPf = [nc.alloc_psum_tensor("PPa", [128, 1024], F32),
           nc.alloc_psum_tensor("PPb", [128, 1024], F32)]
    P = [PPf[0][:, 0:512], PPf[0][:, 512:1024],
         PPf[1][:, 0:512], PPf[1][:, 512:1024]]
    P += [nc.alloc_psum_tensor(f"P{i}", [128, 512], F32)[:] for i in range(4, 8)]

    k = Rec(nc)
    k.sem("mm", "t")
    k.sem("act", "s")
    k.sem("vec", "v")
    k.sem("ph", None)
    for s_ in ("dxin", "dcst", "dst", "fs0", "fs1", "fd0", "fd1", "fe0", "fe1",
               "fw0", "fw1"):
        k.sem(s_, "y")
    for s_ in ("gcst", "fcst", "awq", "awo", "awb"):
        k.sem(s_, "g")
    for i in range(12):
        k.sem(f"aw{i}", "y")
    for i in (20, 21):
        k.sem(f"aw{i}", "y")

    ident_tok = [None]
    cast_mat = {}

    def emit_load():
        for ccx in range(CC):
            k.dma("y", sb["x"][:, ccx * S:(ccx + 1) * S], ap["xin"][ccx], "dxin")
        ident_tok[0] = k.dma("y", sb["identf_sb"][:], ap["identf"], "dcst")
        k.op("v", lambda e: e.memset(sb["ones_bf"][:], 1.0), inc="vec")
        k.op("v", lambda e: e.memset(sb["onesf"][:], 1.0), inc="vec")
        k.op("v", lambda e: e.memset(sb["eps_sb"][:], 1e-5), inc="vec")
        if need_attn:
            fam = {"aqm": "awq", "aqr": "awq", "akm": "awq", "akr": "awq",
                   "avT": "awq", "aoT": "awo", "bqT": "awb", "bkT": "awb",
                   "bvT": "awb", "boT": "awb"}
            for n in scw:
                cast_mat[n] = k.dma("g", scw[n],
                                    ap[n].rearrange("a p c j -> a p (c j)"), fam[n])
            for n in scw:
                cast_mat[n] = k.tok(fam[n])
        if need_ffn:
            k.dma("g", sc["sgsu"], ap["sgsu"], "fcst")
            k.dma("g", sc["sd"], ap["sd"], "fcst")
            for e2 in range(4):
                k.dma("g", sc["expgu"][4 * e2:4 * e2 + 4], ap["expgu"][4 * e2:4 * e2 + 4], "fcst")
            k.dma("g", sc["dwcc"], ap["dwcc"], "fcst")

    def emit_ln():
        lnsq = ar_top("lnsq", (128, 2 * QCA), BF16)
        m1p = ar_top("m1", (128, 2 * QCA), F32)
        m2p = ar_top("m2", (128, 2 * QCA), F32)
        m3p = ar_top("m3", (128, 2 * QCA), F32)
        m4 = ar_top("m4", (128, QCA), F32)
        xin_tok = k.tok("dxin")
        prev_last = {0: None, 1: None}
        sq_war = {0: None, 1: None}
        mm_war = {0: None, 1: None}
        for qc in range(NQA):
            q0 = qc * QCA
            par = qc % 2
            m1 = m1p[:, par * QCA:(par + 1) * QCA]
            m2 = m2p[:, par * QCA:(par + 1) * QCA]
            m3 = m3p[:, par * QCA:(par + 1) * QCA]
            pb0, pb1 = 2 * par, 2 * par + 1
            for ccx in range(CC):
                slot = ccx % 2
                war = mm_war.get(slot)
                xnc = sb["xn"][:, ccx * S + q0:ccx * S + q0 + QCA]
                tb = k.op("s", lambda e, ccx=ccx, q0=q0, xnc=xnc:
                          e.copy(xnc, sb["x"][:, ccx * S + q0:ccx * S + q0 + QCA]),
                          waits=[xin_tok], inc="act")
                tq = k.op("v", lambda e, slot=slot, xnc=xnc:
                          e.tensor_mul(lnsq[:, slot * QCA:(slot + 1) * QCA],
                                       xnc, xnc),
                          waits=[tb, war], inc="vec")
                sq_war[slot] = tq
                k.mmw(pb0, lambda e, xnc=xnc, ccx=ccx, pb0=pb0:
                      e.matmul(P[pb0][:], sb["ones_bf"][:], xnc,
                               start=(ccx == 0), stop=(ccx == CC - 1)),
                      waits=[tb], start=(ccx == 0), stop=(ccx == CC - 1))
                mm_war[slot] = k.mmw(pb1, lambda e, slot=slot, ccx=ccx, pb1=pb1:
                      e.matmul(P[pb1][:], sb["ones_bf"][:],
                               lnsq[:, slot * QCA:(slot + 1) * QCA],
                               start=(ccx == 0), stop=(ccx == CC - 1)),
                      waits=[tq], start=(ccx == 0), stop=(ccx == CC - 1),
                      noinc="force")
            pl = prev_last.get(par)
            t1 = k.prd("s", pb0, lambda e, m1=m1, pb0=pb0: e.mul(m1, P[pb0][:], 1.0 / H),
                       waits=[pl])
            t2 = k.prd("s", pb1, lambda e, m2=m2, pb1=pb1: e.mul(m2, P[pb1][:], 1.0 / H),
                       waits=[pl])
            t3 = k.op("v", lambda e, m3=m3, m1=m1: e.tensor_mul(m3, m1, m1),
                      waits=[t1, pl], inc="vec")
            t4 = k.op("v", lambda e, m2=m2, m3=m3: e.tensor_sub(m2, m2, m3),
                      waits=[t2, t3], inc="vec")
            t5 = k.op("s", lambda e, m3=m3, m2=m2: e.activation(
                m3, m2, AF.Ln, bias=sb["eps_sb"][:, 0:1]), waits=[t4], inc="act")
            t6 = k.op("s", lambda e, m2=m2, m3=m3: e.activation(
                m2, m3, AF.Exp, scale=-0.5), waits=[t5], inc="act")
            t7 = k.op("v", lambda e, m3=m3, m1=m1, m2=m2: e.tensor_mul(m3, m1, m2),
                      waits=[t6], inc="vec")
            lastc = pl
            m4_guard = k.tok("vec")
            for ccx in range(CC):
                t8 = k.op("v", lambda e, ccx=ccx, q0=q0, m2=m2:
                          e.tensor_mul(m4[:], sb["x"][:, ccx * S + q0:ccx * S + q0 + QCA], m2),
                          waits=[t6, lastc, m4_guard], inc="vec")
                lastc = k.op("v", lambda e, ccx=ccx, q0=q0, m3=m3:
                             e.tensor_sub(sb["xn"][:, ccx * S + q0:ccx * S + q0 + QCA],
                                          m4[:], m3),
                             waits=[t7, t8], inc="vec")
            prev_last[par] = lastc
            last = lastc
        return last

    def attn_alloc_preload(mode):
        a = mode == "a"
        pre = {"a": a}
        pre["csA"] = ar("csA", (128, S), BF16) if a else None
        pre["csB"] = ar("csB", (128, S), BF16) if a else None
        pre["mask"] = ar("maska", (128, 4 * QCA), BF16) if a else ar("maskb", (128, 6 * QCB), BF16)
        pre["Qh"] = ar("Qh", (128, 2 * S), BF16)
        pre["Kh"] = ar("Kh", (128, 2 * S), BF16)
        pre["VT"] = ar("VT", (128, 2 * 16 * 130), BF16)
        pre["zv"] = ar("zv", (128, 132), BF16)
        pre["selAB"] = ar("selAB", (128, 256), BF16)
        pre["aout"] = ar("aout", (128, CC * S), BF16)
        nsl = 6 if a else 4
        nqk = nsl - 2
        pre["nqk"] = nqk
        pre["wsl"] = [ar(f"w{i}", (128, 1024), BF16) for i in range(2 * nqk + 1)]
        pre["qsq"] = ar("qsq", (128, 2 * QCA), BF16)
        pre["ksq"] = ar("ksq", (128, 2 * QCA), BF16)
        pre["nacc"] = ar("nacc", (1, 32), F32)
        pre["nred"] = ar("nred", (1, 16), F32)
        pre["bias_sb"] = ar("bias_sb", (128, 2), F32)
        if a:
            k.dma("y", pre["csA"][:], ap["csA"], "dcst")
            k.dma("y", pre["csB"][:], ap["csB"], "dcst")
            k.dma("y", pre["mask"][:], ap["maska"], "dcst")
        else:
            k.dma("y", pre["mask"][:], ap["maskb"], "dcst")
        tok = k.dma("y", pre["selAB"][:], ap["selab"], "dcst")
        pre["cst_tok"] = tok
        pre["selAB_tok"] = tok
        mats = (["aqm", "aqr", "akm", "akr", "avT", "aoT"] if a
                else ["bqT", "bkT", "bvT", "boT"])
        pre["mats"] = mats
        nm = 4 if a else 2
        pre["nm"] = nm
        pre["cast_tok"] = cast_mat.get(mats[-1])
        wtok = {}
        wfree = {}
        pre["wtok"] = wtok
        pre["wfree"] = wfree

        def loadw(mi, c):
            slot = (c % 2) * nqk + mi if mi < nm else 2 * nqk
            t = k.dma("y", pre["wsl"][slot][:], scw[mats[mi]][c], f"aw{slot}",
                      waits=[wfree.get(slot), cast_mat[mats[mi]]])
            wtok[(mi, c)] = (t, slot)
        pre["loadw"] = loadw
        for mi in range(nm + 1):
            loadw(mi, 0)
        for mi in range(nm):
            loadw(mi, 1)
        return pre

    def emit_attn(pre):
        a = pre["a"]
        QC = QCA if a else QCB
        NQ = NQA if a else NQB
        LAG = 2
        T[0] = top
        ebuf = ar_top("ebuf", (128, 4 * QC), BF16)
        embuf = ar_top("embuf", (128, 4 * QC), BF16)
        nsb = ar_top("nsb", (128, QC), F32)
        dinv = ar_top("dinv", (66, QC), BF16)
        wo_sl = [ar_at("wo0", (128, 1024), BF16, ar_offs["dinv"]),
                 ar_at("wo1", (128, 1024), BF16, ar_offs["dinv"] + 2048)]
        pre["wo_sl"] = wo_sl
        pre["rp1"] = ar_top("rp1", (128, QCA), F32)
        pre["rp2"] = ar_top("rp2", (128, QCA), F32)
        selAB = pre["selAB"]
        selAB_tok = pre["selAB_tok"]
        csA, csB, mask = pre["csA"], pre["csB"], pre["mask"]
        QhF, KhF, VTF = pre["Qh"], pre["Kh"], pre["VT"]
        zv = pre["zv"]
        aout = pre["aout"]
        wsl, wo_sl = pre["wsl"], pre["wo_sl"]
        rp1, rp2 = pre["rp1"], pre["rp2"]
        qsq_a, ksq_a = pre["qsq"], pre["ksq"]
        nacc, nred = pre["nacc"], pre["nred"]
        bias_sb = pre["bias_sb"]
        cst_tok = pre["cst_tok"]
        mats, nm, nqk = pre["mats"], pre["nm"], pre["nqk"]
        nw = len(mats) - 1
        iv = nm
        wtok, wfree, loadw = pre["wtok"], pre["wfree"], pre["loadw"]
        last_xn = k.tok("vec")
        last_ln_act = k.tok("act")
        vz1 = k.op("v", lambda e: e.memset(
            VTF[:].rearrange("p (t c) -> p t c", c=130)[:, :, 64:65], 1.0), inc="vec")
        vz2 = k.op("v", lambda e: e.memset(
            VTF[:].rearrange("p (t c) -> p t c", c=130)[:, :, 129:130], 1.0), inc="vec")
        vz3 = k.op("v", lambda e: e.memset(zv[:, 0:64], 0.0), inc="vec")
        vz4 = k.op("v", lambda e: e.memset(zv[:, 64:65], 1.0), inc="vec")
        vz5 = k.op("v", lambda e: e.memset(zv[:, 65:131], 0.0), inc="vec")
        vz6 = k.op("v", lambda e: e.memset(zv[:, 131:132], 1.0), inc="vec")
        zv_tok = vz6

        chunk_info = {}
        prev_sq = {}
        prev_norm = {}
        rope_prev = [None]
        chunk_done_mm = {-1: None, -2: None}

        def bg_units(c):
            par = c % 2
            Qh = QhF[:, par * S:(par + 1) * S]
            Kh = KhF[:, par * S:(par + 1) * S]
            VT = VTF[:, par * 2080:(par + 1) * 2080]
            units = []
            info = {}
            chunk_info[c] = info
            old_tok = chunk_done_mm.get(c - 2)

            def u_proj(qc, pair):
                def f():
                    q0 = qc * QCA
                    mi0 = pair * (2 if a else 1)
                    wt0, sl0 = wtok[(mi0, c)]
                    w = k.takepr(6) + [wt0, last_xn, last_ln_act, old_tok]
                    if a:
                        w += k.takepr(7) + [wtok[(mi0 + 1, c)][0]]
                    k.twait(w)
                    wb0 = wsl[sl0]
                    for ccx in range(CC):
                        tk = k.mmw(6, lambda e, ccx=ccx, q0=q0, wb0=wb0:
                                   e.matmul(P[6][:], wb0[:, ccx * 128:ccx * 128 + 128],
                                            sb["xn"][:, ccx * S + q0:ccx * S + q0 + QCA],
                                            start=(ccx == 0), stop=(ccx == CC - 1)),
                                   start=(ccx == 0), stop=(ccx == CC - 1), nowait=True)
                    wfree[sl0] = tk
                    dstQ = (QhF if pair == 0 else KhF)[:, (c % 2) * S:(c % 2) * S + S]
                    if a:
                        wt1, sl1 = wtok[(mi0 + 1, c)]
                        wb1 = wsl[sl1]
                        for ccx in range(CC):
                            tk = k.mmw(7, lambda e, ccx=ccx, q0=q0, wb1=wb1:
                                       e.matmul(P[7][:], wb1[:, ccx * 128:ccx * 128 + 128],
                                                sb["xn"][:, ccx * S + q0:ccx * S + q0 + QCA],
                                                start=(ccx == 0), stop=(ccx == CC - 1)),
                                       start=(ccx == 0), stop=(ccx == CC - 1), nowait=True)
                        wfree[sl1] = tk
                        t1 = k.prd("v", 6, lambda e, q0=q0: e.tensor_mul(
                            rp1[:], P[6][:], csA[:, q0:q0 + QCA]),
                            waits=[cst_tok, rope_prev[0]])
                        t2 = k.prd("v", 7, lambda e, q0=q0: e.tensor_mul(
                            rp2[:], P[7][:], csB[:, q0:q0 + QCA]))
                        tq = k.op("v", lambda e, q0=q0, dstQ=dstQ: e.tensor_add(
                            dstQ[:, q0:q0 + QCA], rp1[:], rp2[:]), waits=[t1, t2], inc="vec")
                        rope_prev[0] = tq
                    else:
                        tq = k.prd("s", 6, lambda e, q0=q0, dstQ=dstQ: e.copy(
                            dstQ[:, q0:q0 + QCA], P[6][:]))
                    dst = (qsq_a if pair == 0 else ksq_a)[:, (qc % 2) * QCA:(qc % 2 + 1) * QCA]
                    tsq = k.op("s", lambda e, dst=dst, dstQ=dstQ, q0=q0:
                               e.square(dst, dstQ[:, q0:q0 + QCA]),
                               waits=[tq, prev_norm.get((pair, qc % 2))], inc="act")
                    prev_sq[(pair, qc % 2, c)] = tsq
                    info[("qk", qc, pair)] = tq
                return f

            def u_norm(qc):
                def f():
                    npar = (c % 2) * 16
                    tr = None
                    for pair in range(2):
                        src_a = (qsq_a if pair == 0 else ksq_a)[:, (qc % 2) * QCA:(qc % 2 + 1) * QCA]
                        bnk = 6 + pair
                        k.twait(k.takepr(bnk) + [prev_sq[(pair, qc % 2, c)]])
                        tn = None
                        for hh in range(2):
                            tn = k.mmw(bnk, lambda e, src_a=src_a, hh=hh, bnk=bnk:
                                       e.matmul(P[bnk][32 * hh:32 * hh + 1, :],
                                                sb["ones_bf"][64 * hh:64 * hh + 64, 0:1],
                                                src_a[64 * hh:64 * hh + 64, :],
                                                start=True, stop=True,
                                                tile_position=(64 * hh, 32 * hh)),
                                       nowait=True, noinc=(hh == 0))
                        k.pw[bnk] = tn
                        for hh in range(2):
                            col = npar + 8 * hh + 4 * pair + qc
                            tr = k.prd("v", bnk, lambda e, col=col, hh=hh, bnk=bnk:
                                       e.tensor_reduce(nacc[0:1, col:col + 1],
                                                       P[bnk][32 * hh:32 * hh + 1, :],
                                                       AX.X, ALU.max))
                            prev_norm[(pair, qc % 2)] = tr
                    info[("nrm", qc)] = tr
                return f

            def u_bias():
                def f():
                    npar = (c % 2) * 16
                    bpar = (c % 2) * 8
                    rprev = info[("nrm", NQA - 1)]
                    for hh in range(2):
                        rr1 = k.op("v", lambda e, hh=hh: e.tensor_reduce(
                            nred[0:1, bpar + 3 * hh:bpar + 3 * hh + 1],
                            nacc[0:1, npar + 8 * hh:npar + 8 * hh + 4], AX.X, ALU.max),
                            waits=[rprev], inc="vec")
                        rr2 = k.op("v", lambda e, hh=hh: e.tensor_reduce(
                            nred[0:1, bpar + 3 * hh + 1:bpar + 3 * hh + 2],
                            nacc[0:1, npar + 8 * hh + 4:npar + 8 * hh + 8], AX.X, ALU.max),
                            waits=[rr1], inc="vec")
                        rprev = k.op("v", lambda e, hh=hh: e.tensor_add(
                            nred[0:1, bpar + 3 * hh + 2:bpar + 3 * hh + 3],
                            nred[0:1, bpar + 3 * hh:bpar + 3 * hh + 1],
                            nred[0:1, bpar + 3 * hh + 1:bpar + 3 * hh + 2]),
                            waits=[rr2], inc="vec")
                    rmx = k.op("v", lambda e: e.tensor_tensor(
                        nred[0:1, bpar + 6:bpar + 7], nred[0:1, bpar + 2:bpar + 3],
                        nred[0:1, bpar + 5:bpar + 6], ALU.max), waits=[rprev], inc="vec")
                    rng = k.op("v", lambda e: e.tensor_scalar_mul(
                        nred[0:1, bpar + 7:bpar + 8], nred[0:1, bpar + 6:bpar + 7],
                        -1.0 / 16.0), waits=[rmx], inc="vec")
                    k.twait(k.takepr(7) + [rng])
                    tb = k.mmw(7, lambda e: e.matmul(
                        P[7][:, 0:1], sb["onesf"][0:1, 0:128],
                        nred[0:1, bpar + 7:bpar + 8], start=True, stop=True),
                        nowait=True)
                    tbias = k.prd("s", 7, lambda e: e.copy(
                        bias_sb[:, c % 2:c % 2 + 1], P[7][:, 0:1]))
                    info["bias"] = tbias
                return f

            def u_vt(tt):
                def f():
                    wt_v, slot_v = wtok[(iv, c)]
                    bnk = 6 + tt % 2
                    k.twait(k.takepr(bnk) + [wt_v, old_tok])
                    for ccx in range(CC):
                        k.mmw(bnk, lambda e, tt=tt, ccx=ccx, bnk=bnk:
                              e.matmul(P[bnk][:, 0:128],
                                       sb["xn"][:, ccx * S + tt * 128:ccx * S + tt * 128 + 128],
                                       wsl[slot_v][:, ccx * 128:ccx * 128 + 128],
                                       start=(ccx == 0), stop=(ccx == CC - 1)),
                              start=(ccx == 0), stop=(ccx == CC - 1), nowait=True)
                    dst = VT[:, tt * 130:tt * 130 + 130].rearrange(
                        "p (b c) -> p b c", c=65)[:, :, 0:64]
                    src = P[bnk][:, 0:128].rearrange("p (b c) -> p b c", c=64)
                    ev = k.prd("v", bnk, lambda e, dst=dst, src=src: e.tensor_copy(dst, src),
                               waits=[vz1 if tt == 0 else None, vz2 if tt == 0 else None])
                    info[("vt", tt)] = ev
                    if tt == 15:
                        wfree[slot_v] = ev
                        info["vt_done"] = ev
                        if c + 1 < CC:
                            loadw(iv, c + 1)
                return f

            for qc in range(NQA):
                units.append(u_proj(qc, 0))
                units.append(u_proj(qc, 1))
                units.append(u_norm(qc))
            units.append(u_bias())
            for tt in range(16):
                units.append(u_vt(tt))
            return units

        for f in bg_units(0):
            f()
        bgq = []

        dv_toks = [None, None]
        pending_div = [None]
        for c in range(CC):
            par = c % 2
            Qh = QhF[:, par * S:(par + 1) * S]
            Kh = KhF[:, par * S:(par + 1) * S]
            VT = VTF[:, par * 2080:(par + 1) * 2080]
            info = chunk_info[c]
            qk_done = info[("qk", NQA - 1, 1)]
            tbias = info["bias"]
            vt_done = info["vt_done"]
            bias_ap = bias_sb[:, c % 2:c % 2 + 1]
            if c + 1 < CC:
                if c + 2 < CC:
                    for mi in range(nm):
                        loadw(mi, c + 2)
                bgq = list(bg_units(c + 1))
            else:
                bgq = []

            def run_bg(n=1):
                for _ in range(n):
                    if bgq:
                        bgq.pop(0)()

            econs = {}
            emcons = {}
            for qc in range(NQ):
                q0 = qc * QC
                if a:
                    kts = list(range(16))
                    pv_lo, pv_hi = 0, 4 * qc + 4
                    str_lo = 4 * qc
                else:
                    kts = [t_ for t_ in range(4 * qc - 2, 4 * qc + 4) if t_ >= 0]
                    pv_lo, pv_hi = kts[0], kts[-1] + 1
                    str_lo = -10**9
                nkt = len(kts)
                state = {}
                pv_cnt = [0]
                d_cnt = [0]

                def emit_s_exp(j, qc=qc, q0=q0, kts=kts, str_lo=str_lo,
                               pv_lo=pv_lo, pv_hi=pv_hi, state=state,
                               Qh=Qh, Kh=Kh, qk_done=qk_done, tbias=tbias,
                               bias_ap=bias_ap):
                    kt = kts[j]
                    pb = j % 2
                    w = k.takepr(2 * pb) + k.takepr(2 * pb + 1)
                    if j == 0:
                        w += [qk_done, tbias]
                    k.twait(w)
                    for hh in range(2):
                        k.mmw(2 * pb + hh, lambda e, hh=hh, kt=kt, q0=q0, pb=pb, Qh=Qh, Kh=Kh:
                              e.matmul(P[2 * pb + hh][:, 0:QC],
                                       Kh[64 * hh:64 * hh + 64, kt * 128:kt * 128 + 128],
                                       Qh[64 * hh:64 * hh + 64, q0:q0 + QC],
                                       start=True, stop=True),
                              nowait=True, noinc=(hh == 0))
                    sl4 = (j % 2) * 2 * QC
                    eb2 = ebuf[:, sl4:sl4 + 2 * QC]
                    if QC == QCA:
                        src = PPf[pb][:, 0:1024]
                        dst = eb2
                    else:
                        src = PPf[pb][:].rearrange("p (b c) -> p b c", c=512)[:, :, 0:QC]
                        dst = eb2.rearrange("p (b c) -> p b c", c=QC)
                    texp = k.prd2("s", (2 * pb, 2 * pb + 1),
                                  lambda e, src=src, dst=dst, bias_ap=bias_ap:
                                  e.activation(dst, src, AF.Exp,
                                               bias=bias_ap[:, 0:1], scale=0.125),
                                  waits=[tbias, econs.get(j % 2)])
                    use_mask = ((a and str_lo <= kt < pv_hi) or (not a))
                    pv_use, tmask = eb2, texp
                    if use_mask and pv_lo <= kt < pv_hi:
                        mb2 = embuf[:, sl4:sl4 + 2 * QC]
                        r = (kt - 4 * qc) if a else (kt - 4 * qc + 2)
                        msl = mask[:, r * QC:r * QC + QC]
                        tmask = k.op("v", lambda e, mb2=mb2, eb2=eb2, msl=msl:
                                     e.tensor_tensor(
                                         mb2.rearrange("p (b c) -> p b c", c=QC),
                                         eb2.rearrange("p (b c) -> p b c", c=QC),
                                         msl.rearrange("p (b c) -> p b c", b=1).broadcast_to(
                                             (128, 2, QC)),
                                         ALU.mult),
                                     waits=[texp, cst_tok, emcons.get(j % 2)],
                                     inc="vec")
                        pv_use = mb2
                    state[j] = (texp, tmask, pv_use, eb2)

                def emit_dpv(j, qc=qc, kts=kts, pv_lo=pv_lo, pv_hi=pv_hi,
                             state=state, pv_cnt=pv_cnt, d_cnt=d_cnt,
                             VT=VT, vt_done=vt_done):
                    kt = kts[j]
                    nkt = len(kts)
                    npv = pv_hi - pv_lo
                    first_d = d_cnt[0] == 0
                    last_d = d_cnt[0] == nkt - 1
                    in_pv = pv_lo <= kt < pv_hi
                    first_pv = in_pv and pv_cnt[0] == 0
                    last_pv = in_pv and pv_cnt[0] == npv - 1
                    texp, tmask, pv_use, eb2 = state[j]
                    w = []
                    if first_d:
                        w += k.takepr(4) + [vt_done, zv_tok]
                    if first_pv:
                        w += k.takepr(5)
                    w.append(texp)
                    if in_pv:
                        w.append(tmask)
                    k.twait(w)
                    ebA = eb2[:, 0:QC]
                    ebB = eb2[:, QC:2 * QC]
                    pvA = pv_use[:, 0:QC]
                    pvB = pv_use[:, QC:2 * QC]
                    dsrcA = ebA if a else pvA
                    dsrcB = ebB if a else pvB
                    if not last_d:
                        k.mmw(4, lambda e, dsrcB=dsrcB, st=first_d:
                              e.matmul(P[4][0:66, 0:QC], zv[:, 66:132], dsrcB,
                                       start=st, stop=False),
                              start=first_d, stop=False, nowait=True, noinc=True)
                    if (not a) or (in_pv and pv_use is eb2):
                        fa = k.mmw(4, lambda e, kt=kt, src=(pvA if not a else ebA):
                                   e.matmul(P[4][0:65, 0:QC],
                                            VT[:, kt * 130:kt * 130 + 65],
                                            src, start=False, stop=False),
                                   start=False, stop=False, nowait=True, noinc="force")
                    elif in_pv:
                        k.mmw(4, lambda e, dsrcA=dsrcA:
                              e.matmul(P[4][0:65, 0:QC], zv[:, 0:65], dsrcA,
                                       start=False, stop=False),
                              start=False, stop=False, nowait=True, noinc=True)
                        fa = k.mmw(4, lambda e, kt=kt, pvA=pvA:
                                   e.matmul(P[4][0:64, 0:QC],
                                            VT[:, kt * 130:kt * 130 + 64],
                                            pvA, start=False, stop=False),
                                   start=False, stop=False, nowait=True, noinc="force")
                    else:
                        fa = k.mmw(4, lambda e, dsrcA=dsrcA:
                                   e.matmul(P[4][0:65, 0:QC], zv[:, 0:65], dsrcA,
                                            start=False, stop=False),
                                   start=False, stop=False, nowait=True, noinc="force")
                    if last_d:
                        fa = k.mmw(4, lambda e, dsrcB=dsrcB:
                                   e.matmul(P[4][0:66, 0:QC], zv[:, 66:132], dsrcB,
                                            start=False, stop=True),
                                   start=False, stop=True, nowait=True)
                    d_cnt[0] += 1
                    if in_pv:
                        fpb = k.mmw(5, lambda e, kt=kt, pvB=pvB, st=first_pv, sp=last_pv:
                                    e.matmul(P[5][64:128, 0:QC],
                                             VT[:, kt * 130 + 65:kt * 130 + 129],
                                             pvB, start=st, stop=sp,
                                             tile_position=(0, 64)),
                                    start=first_pv, stop=last_pv, nowait=True)
                        pv_cnt[0] += 1
                        last_tok = fpb
                    else:
                        last_tok = fa
                    if last_d:
                        k.pw[4] = last_tok
                    econs[j % 2] = last_tok
                    if pv_use is not eb2:
                        emcons[j % 2] = last_tok
                    del state[j]

                def make_divide(qc=qc, q0=q0, c=c):
                    def _recipa(e):
                        with nc.allow_low_precision(reason="bf16 inv-denominator, margin-checked"):
                            return e.reciprocal(dinv[64:66, 0:QC], P[4][64:66, 0:QC])
                    dra = k.prd("v", 4, _recipa)
                    drb = dra
                    ne1 = k.prd("v", 4, lambda e: e.tensor_copy(
                        nsb[0:64, 0:QC], P[4][0:64, 0:QC]), waits=[dv_toks[0]])
                    ne2 = k.prd("v", 5, lambda e: e.tensor_copy(
                        nsb[64:128, 0:QC], P[5][64:128, 0:QC]), waits=[dv_toks[1]])

                    def emit_divide():
                        k.twait(k.takepr(4) + k.takepr(5) + [dra, ne1, ne2, selAB_tok])
                        k.mmw(4, lambda e: e.matmul(
                            P[4][:, 0:QC], selAB[64:66, 0:128],
                            dinv[64:66, 0:QC],
                            start=True, stop=True), nowait=True, noinc=True)
                        bcb = k.mmw(5, lambda e: e.matmul(
                            P[5][:, 0:QC], selAB[64:66, 128:256],
                            dinv[64:66, 0:QC],
                            start=True, stop=True), nowait=True)
                        k.pw[4] = bcb
                        dv_toks[0] = k.prd("v", 4, lambda e: e.tensor_mul(
                            aout[0:64, c * S + q0:c * S + q0 + QC], nsb[0:64, 0:QC],
                            P[4][0:64, 0:QC]))
                        dv_toks[1] = k.prd("v", 5, lambda e: e.tensor_mul(
                            aout[64:128, c * S + q0:c * S + q0 + QC], nsb[64:128, 0:QC],
                            P[5][64:128, 0:QC]))
                    return emit_divide

                for j in range(nkt):
                    if j >= LAG:
                        emit_dpv(j - LAG)
                    run_bg(1)
                    emit_s_exp(j)
                    if j == 1 and pending_div[0] is not None:
                        pending_div[0]()
                        pending_div[0] = None
                for j in range(max(0, nkt - LAG), nkt):
                    emit_dpv(j)
                    run_bg(1)
                if pending_div[0] is not None:
                    pending_div[0]()
                    pending_div[0] = None
                pending_div[0] = make_divide()
            run_bg(len(bgq))
            chunk_done_mm[c] = ("mm", k.cnt["mm"])
        pending_div[0]()
        pending_div[0] = None
        last_aout = dv_toks[1]

        for fc in range(CC):
            slot = 20 + fc % 2
            t_wo = k.dma("y", wo_sl[fc % 2][:], scw[mats[nw]][fc], f"aw{slot}",
                         waits=[wfree.get(slot), pre["cast_tok"], last_aout])
            wfree[slot] = t_wo
            for qc in range(NQA):
                q0 = qc * QCA
                pp = 6 + (fc * NQA + qc) % 2
                k.twait(k.takepr(pp) + [t_wo, last_aout])
                for dc in range(CC):
                    k.mmw(pp, lambda e, dc=dc, q0=q0, fc=fc, pp=pp:
                          e.matmul(P[pp][:], wo_sl[fc % 2][:, dc * 128:dc * 128 + 128],
                                   aout[:, dc * S + q0:dc * S + q0 + QCA],
                                   start=(dc == 0), stop=(dc == CC - 1)),
                          start=(dc == 0), stop=(dc == CC - 1), nowait=True)
                tadd = k.prd("v", pp, lambda e, fc=fc, q0=q0, pp=pp: e.tensor_add(
                    sb["x"][:, fc * S + q0:fc * S + q0 + QCA],
                    sb["x"][:, fc * S + q0:fc * S + q0 + QCA], P[pp][:]))
            wfree[slot] = tadd
        return tadd

    def ffn_alloc_preload():
        pre = {}
        esel_sb = ar("esel", (NE, NE * 128), BF16)
        cbias_sb = ar("cbias", (128, 256), F32)
        rw_sb = ar("rw", (128, 128), BF16)
        comb_fm = ar("comb_fm", (NE, S), BF16)
        hh = ar("hh", (128, 48 * QCA), BF16)
        hh_off = int(hh.handle.offset) if hasattr(hh, "handle") else None
        hh_base = A[0] - 48 * QCA * 2
        sco_fm = ar_at("sco_fm", (NE, S), F32, hh_base)
        scT = ar_at("scT", (128, 256), F32, hh_base + 8192)
        sbT = ar_at("sbT", (128, 256), F32, hh_base + 8192 + 1024)
        msk = ar_at("msk", (128, 256), F32, hh_base + 8192 + 2048)
        cnt = ar_at("cnt", (128, 256), F32, hh_base + 8192 + 3072)
        ctm = ar_at("ctm", (128, 256), F32, hh_base + 8192 + 4096)
        rb = hh_base + 8192 + 5120
        gs = ar_at("gs", (128, 64), F32, rb)
        gct = ar_at("gct", (128, 64), F32, rb + 256)
        gtm = ar_at("gtm", (128, 64), F32, rb + 512)
        gm = ar_at("gm", (128, 64), F32, rb + 768)
        s1 = ar_at("s1", (128, 64), F32, rb + 1024)
        s2 = ar_at("s2", (128, 64), F32, rb + 1280)
        s3 = ar_at("s3", (128, 64), F32, rb + 1536)
        wsum = ar_at("wsum", (128, 16), F32, rb + 1792)
        comb_tm = ar_at("comb_tm", (128, 256), F32, rb + 1856)
        silu_t = [ar(f"silu_t{i}", (128, QCA), BF16) for i in range(2)]
        t2b = [ar(f"t2b{i}", (128, QCA), BF16) for i in range(2)]
        sgsu_sl = [ar(f"sgsu{i}", (128, 2048), BF16) for i in range(2)]
        sd_sl = [ar(f"sd{i}", (128, 2048), BF16) for i in range(2)]
        dw_sl = [ar(f"dw{i}", (128, 4096), BF16) for i in range(2)]
        eg_sl = []

        k.dma("y", esel_sb[:], ap["esel"], "dcst")
        tcs = k.dma("y", cbias_sb[:], ap["cbias"], "dcst")
        trw = k.dma("g", rw_sb[:], ap["rwT"], "gcst")
        cast_done = k.tok("fcst")
        pre["tl_sgsu"] = [k.dma("y", sgsu_sl[i][:], sc["sgsu"][i], f"fs{i}",
                                waits=[cast_done]) for i in range(2)]
        pre["tl_sd0"] = k.dma("y", sd_sl[0][:], sc["sd"][0], "fd0",
                              waits=[cast_done])
        pre["tl_dw0"] = k.dma("y", dw_sl[0][:], sc["dwcc"][0], "fw0",
                              waits=[cast_done])
        for nm_ in ("esel_sb", "cbias_sb", "rw_sb", "comb_fm", "hh", "sco_fm",
                    "scT", "sbT", "msk", "cnt", "ctm", "gs", "gct", "gtm", "gm",
                    "s1", "s2", "s3", "wsum", "comb_tm", "silu_t", "t2b",
                    "sgsu_sl", "sd_sl", "dw_sl", "eg_sl", "tcs", "trw",
                    "cast_done"):
            pre[nm_] = locals()[nm_]
        return pre

    def emit_ffn(pre):
        FLV = int(os.environ.get("FFN_LV", "5"))
        (esel_sb, cbias_sb, rw_sb, comb_fm, hh, sco_fm, scT, sbT, msk, cnt,
         ctm, gs, gct, gtm, gm, s1, s2, s3, wsum, comb_tm, silu_t, t2b,
         sgsu_sl, sd_sl, dw_sl, eg_sl, tcs, trw, cast_done) = (
            pre[nm_] for nm_ in (
                "esel_sb", "cbias_sb", "rw_sb", "comb_fm", "hh", "sco_fm",
                "scT", "sbT", "msk", "cnt", "ctm", "gs", "gct", "gtm", "gm",
                "s1", "s2", "s3", "wsum", "comb_tm", "silu_t", "t2b",
                "sgsu_sl", "sd_sl", "dw_sl", "eg_sl", "tcs", "trw", "cast_done"))
        T[0] = top
        eg_sl.append(ar_top("eg1", (128, 4096), BF16))
        eg_sl.append(ar_top("eg0", (128, 4096), BF16))
        eg_sl.reverse()
        ln_guard = ("vec", k.cnt["vec"])
        last_xn = k.tok("vec")
        last_ln_act = k.tok("act")

        for qc in range(NQA):
            q0 = qc * QCA
            for ccx in range(CC):
                k.mmw(6, lambda e, ccx=ccx, q0=q0:
                      e.matmul(P[6][0:16, :], rw_sb[:, ccx * 16:ccx * 16 + 16],
                               sb["xn"][:, ccx * S + q0:ccx * S + q0 + QCA],
                               start=(ccx == 0), stop=(ccx == CC - 1)),
                      waits=[trw, last_xn, last_ln_act],
                      start=(ccx == 0), stop=(ccx == CC - 1))
            tsig = k.prd("s", 6, lambda e, q0=q0: e.activation(
                sco_fm[0:16, q0:q0 + QCA], P[6][0:16, :], AF.Sigmoid))
        for tt in range(16):
            k.mmw(7, lambda e, tt=tt: e.transpose(
                P[7][:, tt * 16:tt * 16 + 16], sco_fm[0:16, tt * 128:tt * 128 + 128],
                sb["identf_sb"][0:16, 0:16]),
                waits=[tsig, ident_tok[0]], start=(tt == 0), stop=(tt == 15))
        v0 = k.prd("v", 7, lambda e: e.tensor_copy(scT[:], P[7][:, 0:256]))
        vt = k.op("v", lambda e: e.tensor_add(sbT[:], scT[:], cbias_sb[:]),
                  waits=[v0, tcs], inc="vec")

        def re(apx):
            return apx.rearrange("p (t e) -> p t e", e=16)
        sb4 = sbT[:].rearrange("p (t g j) -> p t g j", g=4, j=4)
        g4 = gs[:].rearrange("p (t g) -> p t g", g=4)
        s1r = s1[:].rearrange("p (t g) -> p t g", g=4)
        s2r = s2[:].rearrange("p (t g) -> p t g", g=4)
        s3r = s3[:].rearrange("p (t g) -> p t g", g=4)
        gmr = gm[:].rearrange("p (t g) -> p t g", g=4)
        gcr = gct[:].rearrange("p (t g) -> p t g", g=4)
        gtr = gtm[:].rearrange("p (t g) -> p t g", g=4)

        def vop(fn):
            nonlocal vt
            vt = k.op("v", fn, waits=[vt], inc="vec")
            return vt
        vop(lambda e: e.tensor_tensor(s1r, sb4[:, :, :, 0], sb4[:, :, :, 1], ALU.max))
        vop(lambda e: e.tensor_tensor(s2r, sb4[:, :, :, 0], sb4[:, :, :, 1], ALU.min))
        vop(lambda e: e.tensor_tensor(s3r, sb4[:, :, :, 2], sb4[:, :, :, 3], ALU.max))
        vop(lambda e: e.tensor_tensor(gmr, sb4[:, :, :, 2], sb4[:, :, :, 3], ALU.min))
        vop(lambda e: e.tensor_tensor(gcr, s1r, s3r, ALU.max))
        vop(lambda e: e.tensor_tensor(s1r, s1r, s3r, ALU.min))
        vop(lambda e: e.tensor_tensor(s2r, s2r, gmr, ALU.max))
        vop(lambda e: e.tensor_tensor(s1r, s1r, s2r, ALU.max))
        vop(lambda e: e.tensor_tensor(g4, gcr, s1r, ALU.add))
        vop(lambda e: e.memset(gct[:], 0.0))
        for dd in range(1, 4):
            Aap = g4[:, :, 0:4 - dd]
            Bap = g4[:, :, dd:4]
            cA = gcr[:, :, 0:4 - dd]
            cB = gcr[:, :, dd:4]
            tmp = gtr[:, :, 0:4 - dd]
            vop(lambda e, A=Aap, B=Bap, t=tmp: e.tensor_tensor(t, A, B, ALU.is_ge))
            vop(lambda e, t=tmp, c2=cB: e.tensor_tensor(c2, c2, t, ALU.add))
            vop(lambda e, A=Aap, B=Bap, t=tmp: e.tensor_tensor(t, B, A, ALU.is_gt))
            vop(lambda e, t=tmp, c2=cA: e.tensor_tensor(c2, c2, t, ALU.add))
        vop(lambda e: e.tensor_single_scalar(gm[:], gct[:], 2.0, ALU.is_lt))
        mr4 = msk[:].rearrange("p (t g j) -> p t g j", g=4, j=4)
        for j in range(4):
            vop(lambda e, j=j: e.scalar_tensor_tensor(
                mr4[:, :, :, j], sb4[:, :, :, j], 4.0, gmr, ALU.add, ALU.mult))
        vop(lambda e: e.memset(cnt[:], 0.0))
        mre = re(msk[:])
        cre = re(cnt[:])
        tre = re(ctm[:])
        for dd in range(1, 16):
            Aap = mre[:, :, 0:16 - dd]
            Bap = mre[:, :, dd:16]
            cA = cre[:, :, 0:16 - dd]
            cB = cre[:, :, dd:16]
            tmp = tre[:, :, 0:16 - dd]
            vop(lambda e, A=Aap, B=Bap, t=tmp: e.tensor_tensor(t, A, B, ALU.is_ge))
            vop(lambda e, t=tmp, c2=cB: e.tensor_tensor(c2, c2, t, ALU.add))
            vop(lambda e, A=Aap, B=Bap, t=tmp: e.tensor_tensor(t, B, A, ALU.is_gt))
            vop(lambda e, t=tmp, c2=cA: e.tensor_tensor(c2, c2, t, ALU.add))
        vop(lambda e: e.tensor_single_scalar(msk[:], cnt[:], 4.0, ALU.is_lt))
        vop(lambda e: e.tensor_mul(ctm[:], scT[:], msk[:]))
        vop(lambda e: e.tensor_reduce(wsum[:], re(ctm[:]), AX.X, ALU.add))
        vop(lambda e: e.tensor_scalar_add(wsum[:], wsum[:], 1e-20))
        vop(lambda e: e.reciprocal(wsum[:], wsum[:]))
        winb = wsum[:].rearrange("p (t e) -> p t e", e=1).broadcast_to((128, 16, 16))
        vop(lambda e: e.tensor_tensor(re(comb_tm[:]), re(ctm[:]), winb, ALU.mult))
        comb_vt = vt
        comb_state = {}

        def emit_comb_back():
            cevac = None
            for b4 in range(4):
                for j4 in range(4):
                    tt = 4 * b4 + j4
                    k.mmw(7, lambda e, tt=tt, j4=j4: e.transpose(
                        P[7][0:16, j4 * 128:j4 * 128 + 128],
                        comb_tm[:, tt * 16:tt * 16 + 16], sb["identf_sb"][:]),
                        waits=[comb_vt], start=(j4 == 0), stop=(j4 == 3))
                cevac = k.prd("s", 7, lambda e, b4=b4: e.copy(
                    comb_fm[0:16, b4 * 512:b4 * 512 + 512], P[7][0:16, 0:512]))
            comb_state["done"] = cevac

        emit_comb_back()

        sl_free = {0: None, 1: None}
        ed_free = {0: None, 1: None}
        dwf = {0: None, 1: None}
        sdf = {0: None, 1: None}
        st_free = {0: None, 1: None}
        t2_free = {0: None, 1: None}
        hh_tok = {}
        down_done = None
        res_tok = None
        for qc in range(NQA):
            q0 = qc * QCA
            if FLV < 2:
                continue
            for hc in range(16):
                sl = hc % 2
                if qc == 0 and hc < 2:
                    tl = pre["tl_sgsu"][hc]
                else:
                    tl = k.dma("y", sgsu_sl[sl][:], sc["sgsu"][hc], f"fs{sl}",
                               waits=[cast_done, sl_free[sl]])
                pp = hc % 2
                for ccx in range(CC):
                    k.mmw(pp, lambda e, ccx=ccx, q0=q0, sl=sl, pp=pp:
                          e.matmul(P[pp][:], sgsu_sl[sl][:, ccx * 128:ccx * 128 + 128],
                                   sb["xn"][:, ccx * S + q0:ccx * S + q0 + QCA],
                                   start=(ccx == 0), stop=(ccx == CC - 1)),
                          waits=[tl, last_xn, last_ln_act],
                          start=(ccx == 0), stop=(ccx == CC - 1))
                for ccx in range(CC):
                    tu = k.mmw(2 + pp, lambda e, ccx=ccx, q0=q0, sl=sl, pp=pp:
                               e.matmul(P[2 + pp][:],
                                        sgsu_sl[sl][:, 1024 + ccx * 128:1024 + ccx * 128 + 128],
                                        sb["xn"][:, ccx * S + q0:ccx * S + q0 + QCA],
                                        start=(ccx == 0), stop=(ccx == CC - 1)),
                               waits=[tl],
                               start=(ccx == 0), stop=(ccx == CC - 1))
                ts = k.prd("s", pp, lambda e, pp=pp: e.activation(
                    silu_t[pp][:], P[pp][:], AF.Silu), waits=[st_free[pp]])
                th = k.prd("v", 2 + pp, lambda e, hc=hc, pp=pp: e.tensor_mul(
                    hh[:, hc * QCA:hc * QCA + QCA], silu_t[pp][:], P[2 + pp][:]),
                    waits=[ts, down_done])
                st_free[pp] = th
                hh_tok[hc] = th
                sl_free[sl] = tu
            comb_done = comb_state["done"]
            for ee in range(NE):
                if FLV < 3:
                    break
                sl = ee % 2
                tl = k.dma("y", eg_sl[sl][:], sc["expgu"][ee], f"fe{sl}",
                           waits=[cast_done, ed_free[sl], ln_guard])
                tcb = k.mmw(4, lambda e, ee=ee, q0=q0: e.matmul(
                    P[4][:], esel_sb[:, ee * 128:ee * 128 + 128],
                    comb_fm[0:16, q0:q0 + QCA], start=True, stop=True),
                    waits=[comb_done])
                for h2 in range(2):
                    pp = h2
                    for ccx in range(CC):
                        k.mmw(pp, lambda e, ccx=ccx, q0=q0, sl=sl, h2=h2, pp=pp:
                              e.matmul(P[pp][:],
                                       eg_sl[sl][:, (h2 * 8 + ccx) * 128:(h2 * 8 + ccx) * 128 + 128],
                                       sb["xn"][:, ccx * S + q0:ccx * S + q0 + QCA],
                                       start=(ccx == 0), stop=(ccx == CC - 1)),
                              waits=[tl], start=(ccx == 0), stop=(ccx == CC - 1))
                    for ccx in range(CC):
                        tu = k.mmw(2 + pp, lambda e, ccx=ccx, q0=q0, sl=sl, h2=h2, pp=pp:
                                   e.matmul(P[2 + pp][:],
                                            eg_sl[sl][:, 2048 + (h2 * 8 + ccx) * 128:2048 + (h2 * 8 + ccx) * 128 + 128],
                                            sb["xn"][:, ccx * S + q0:ccx * S + q0 + QCA],
                                            start=(ccx == 0), stop=(ccx == CC - 1)),
                                   waits=[tl], start=(ccx == 0), stop=(ccx == CC - 1))
                    ts = k.prd("s", pp, lambda e, pp=pp: e.activation(
                        silu_t[pp][:], P[pp][:], AF.Silu), waits=[st_free[pp]])
                    tm = k.prd("v", 2 + pp, lambda e, pp=pp: e.tensor_mul(
                        t2b[pp][:], silu_t[pp][:], P[2 + pp][:]),
                        waits=[ts, t2_free[pp]])
                    st_free[pp] = tm
                    th = k.prd("v", 4, lambda e, ee=ee, h2=h2, pp=pp: e.tensor_mul(
                        hh[:, (16 + 2 * ee + h2) * QCA:(16 + 2 * ee + h2 + 1) * QCA],
                        t2b[pp][:], P[4][:]), waits=[tm, down_done])
                    t2_free[pp] = th
                    hh_tok[16 + 2 * ee + h2] = th
                ed_free[sl] = tu
            for ccx in range(CC):
                if FLV < 4:
                    break
                sl = ccx % 2
                if qc == 0 and ccx == 0:
                    tsd, tdw = pre["tl_sd0"], pre["tl_dw0"]
                else:
                    tsd = k.dma("y", sd_sl[sl][:], sc["sd"][ccx], f"fd{sl}",
                                waits=[cast_done, sdf[sl]])
                    tdw = k.dma("y", dw_sl[sl][:], sc["dwcc"][ccx], f"fw{sl}",
                                waits=[cast_done, dwf[sl]])
                pp = 5 + ccx % 2
                shstop = FLV < 5
                for hc in range(16):
                    tk = k.mmw(pp, lambda e, hc=hc, sl=sl, pp=pp,
                               sp=(shstop and hc == 15):
                               e.matmul(P[pp][:], sd_sl[sl][:, hc * 128:hc * 128 + 128],
                                        hh[:, hc * QCA:hc * QCA + QCA],
                                        start=(hc == 0), stop=sp),
                               waits=[tsd, hh_tok[hc]],
                               start=(hc == 0), stop=(shstop and hc == 15))
                for ee in range(NE):
                    if FLV < 5:
                        break
                    for h2 in range(2):
                        idx = 2 * ee + h2
                        tk = k.mmw(pp, lambda e, idx=idx, sl=sl, pp=pp, sp=(idx == 31):
                                   e.matmul(P[pp][:],
                                            dw_sl[sl][:, idx * 128:idx * 128 + 128],
                                            hh[:, (16 + idx) * QCA:(16 + idx + 1) * QCA],
                                            start=False, stop=sp),
                                   waits=[tdw, hh_tok[16 + idx]],
                                   start=False, stop=(idx == 31))
                def _res(e, ccx=ccx, q0=q0, pp=pp, qc=qc):
                    inst = e.tensor_add(
                        sb["x"][:, ccx * S + q0:ccx * S + q0 + QCA],
                        sb["x"][:, ccx * S + q0:ccx * S + q0 + QCA], P[pp][:])
                    inst.annotate(f"res qc{qc} ccx{ccx} pp{pp}")
                    return inst
                res_tok = k.prd("v", pp, _res)
                sdf[sl] = tk
                dwf[sl] = tk
                down_done = res_tok
        return res_tok

    def emit_store():
        for ccx in range(CC):
            k.dma("y", yout[ccx], sb["x"][:, ccx * S:(ccx + 1) * S], "dst")
        k.op("y", lambda e: None, waits=[("dst", 16 * CC)])

    emit_load()
    k.barrier(exclude=("awq", "awo", "awb", "fcst"))
    i = 0
    while i < len(stages):
        st = stages[i]
        ar_reset()
        if st.startswith("ln"):
            nxt = stages[i + 1] if i + 1 < len(stages) else None
            pre = None
            if nxt in ("attn_a", "attn_b"):
                pre = attn_alloc_preload("a" if nxt == "attn_a" else "b")
            elif nxt is not None and nxt.startswith("ffn"):
                pre = ffn_alloc_preload()
            emit_ln()
            if nxt in ("attn_a", "attn_b"):
                emit_attn(pre)
                i += 1
            elif nxt is not None and nxt.startswith("ffn"):
                emit_ffn(pre)
                i += 1
        k.barrier()
        i += 1
    emit_store()

    for name in list(k.cnt):
        k.sems[name] = nc.alloc_semaphore(name)

    with nc.Block() as block:
        @block.tensor
        def _(e):
            for f in k.prog["t"]:
                f(e)

        @block.vector
        def _(e):
            for f in k.prog["v"]:
                f(e)

        @block.scalar
        def _(e):
            for f in k.prog["s"]:
                f(e)

        @block.gpsimd
        def _(e):
            for f in k.prog["g"]:
                f(e)

        @block.sync
        def _(e):
            for f in k.prog["y"]:
                f(e)
    return nc


def kernel(**inputs):
    return _run(inputs, list(ALL_STAGES))


def _run(inputs, stages, cores=None):
    hp = host_prep(inputs)
    x = np.asarray(inputs["L0_attn_a_x"])
    B = x.shape[0]
    if cores is None:
        cores = list(range(B))
    nc = build_nc(tuple(stages))
    in_maps = []
    for b in cores:
        m = dict(hp)
        m["xin"] = np.ascontiguousarray(x[b].T.reshape(CC, 128, S))
        in_maps.append(m)
    res = run_bass_kernel_spmd(nc, in_maps, core_ids=list(range(len(cores))))
    outs = []
    for r in res.results:
        y = np.asarray(r["y"]).reshape(H, S)
        outs.append(np.ascontiguousarray(y.T))
    return np.stack(outs, axis=0)
